# version 07
# baseline (speedup 1.0000x reference)
import numpy as np
import ml_dtypes
import concourse.bass as bass
import concourse.mybir as mybir
from concourse.bass_utils import run_bass_kernel_spmd

F32 = mybir.dt.float32
BF16 = mybir.dt.bfloat16
FP8 = mybir.dt.float8e4
U8 = mybir.dt.uint8
AF = mybir.ActivationFunctionType
ALU = mybir.AluOpType
AXX = mybir.AxisListType.X

DMODEL = 1024
DIN = 2048
DH = 1024
NST = 16
T = 8192
NSC = 4
TSC = T // NSC
NCH = 4
C = TSC // NCH
NTILE = C // 128
KD = DMODEL // 128
MD = DH // 128
NBQ = DMODEL // 32

ENGS = ("sync", "vector", "scalar", "tensor", "gpsimd")
DEF_SEM = {"vector": "VE", "scalar": "AC", "tensor": "TE"}


class Sched:
    def __init__(self):
        self.q = {e: [] for e in ENGS}
        self.cnt = {}
        self.last = {}

    def op(self, eng, fn, inc="default", k=1, sw=True):
        if inc == "default":
            inc = DEF_SEM.get(eng)
        if sw and eng in ("vector", "scalar"):
            prev = self.last.get(eng)
            if prev is not None:
                self.q[eng].append(["wait", prev[0], prev[1]])
        self.q[eng].append(["op", fn, inc, k])
        if inc is not None:
            self.cnt[inc] = self.cnt.get(inc, 0) + k
            self.last[eng] = (inc, self.cnt[inc])

    def dma(self, eng, fn, inc):
        self.op(eng, fn, inc, 16)

    def nop(self, eng, cycles=512):
        pass

    def wait(self, eng, sem, val=None):
        if val is None:
            val = self.cnt.get(sem, 0)
        self.q[eng].append(["wait", sem, val])

    def n(self, sem):
        return self.cnt.get(sem, 0)

    def replay(self, eng_name, eng, sems):
        for item in self.q[eng_name]:
            if item[0] == "op":
                _, fn, inc, k = item
                inst = fn(eng)
                if inc is not None:
                    inst.then_inc(sems[inc], k)
            else:
                _, sem, val = item
                if val > 0:
                    eng.wait_ge(sems[sem], val)


def build_program():
    nc = bass.Bass("TRN2", target_bir_lowering=False)

    x_in = nc.dram_tensor("x_b", [T, DMODEL], F32, kind="ExternalInput").ap()
    win_in = nc.dram_tensor("win_sl", [DMODEL, 2 * DH], F32, kind="ExternalInput").ap()
    convw_in = nc.dram_tensor("convw_sl", [4, DH], F32, kind="ExternalInput").ap()
    convb_in = nc.dram_tensor("convb_sl", [DH, 1], F32, kind="ExternalInput").ap()
    wx_in = nc.dram_tensor("wx_sl", [DH, 2 * NST], F32, kind="ExternalInput").ap()
    wdt_in = nc.dram_tensor("wdt_sl", [NST, DH], F32, kind="ExternalInput").ap()
    bdt_in = nc.dram_tensor("bdt_sl", [DH, 1], F32, kind="ExternalInput").ap()
    alog_in = nc.dram_tensor("alog_sl", [DH, NST], F32, kind="ExternalInput").ap()
    dvec_in = nc.dram_tensor("dvec_sl", [DH, 1], F32, kind="ExternalInput").ap()
    wout_in = nc.dram_tensor("wout_sl", [DH, DMODEL], F32, kind="ExternalInput").ap()
    fw_in = nc.dram_tensor("ffnw", [DMODEL, DMODEL], F32, kind="ExternalInput").ap()
    fb_in = nc.dram_tensor("ffnb", [DMODEL, 1], F32, kind="ExternalInput").ap()
    ident_in = nc.dram_tensor("ident", [128, 128], BF16, kind="ExternalInput").ap()
    out_d = nc.dram_tensor("out_r", [NSC * 1024, DMODEL], F32,
                           kind="ExternalOutput").ap()

    pay_in = [nc.dram_tensor(f"pay_in{k}", [2, 1024, DMODEL], F32, kind="Internal")
              for k in range(NSC)]
    pay_out = [nc.dram_tensor(f"pay_out{k}", [1024, DMODEL], F32, kind="Internal")
               for k in range(NSC)]
    ar_in = [nc.dram_tensor(f"ar_in{k}", [2 * NST, TSC], F32, kind="Internal")
             for k in range(NSC)]
    ar_out = [nc.dram_tensor(f"ar_out{k}", [2 * NST, TSC], F32, kind="Internal")
              for k in range(NSC)]
    xconv_d = nc.dram_tensor("xconv_d", [DH, T], BF16, kind="Internal")
    zsil_d = nc.dram_tensor("zsil_d", [DH, T], BF16, kind="Internal")
    y3_d = nc.dram_tensor("y3_d", [DH, T], BF16, kind="Internal")
    Bd = nc.dram_tensor("Bd", [NST, TSC], BF16, kind="Internal")

    s = Sched()

    from contextlib import ExitStack
    with ExitStack() as st:
        win_s = st.enter_context(nc.sbuf_tensor("win_s", [128, KD * 2 * DH], BF16))
        wout_s = st.enter_context(nc.sbuf_tensor("wout_s", [128, MD * DMODEL], BF16))
        wd_s = st.enter_context(nc.sbuf_tensor("wd_s", [128, KD * DMODEL], BF16))
        wx_s = st.enter_context(nc.sbuf_tensor("wx_s", [128, MD * 2 * NST], BF16))
        wdt_s = st.enter_context(nc.sbuf_tensor("wdt_s", [NST, DH], BF16))
        ident = st.enter_context(nc.sbuf_tensor("ident_s", [128, 128], BF16))
        convw_c = st.enter_context(nc.sbuf_tensor("convw_c", [128, 4 * MD], F32))
        convb_c = st.enter_context(nc.sbuf_tensor("convb_c", [128, MD], F32))
        bdt_c = st.enter_context(nc.sbuf_tensor("bdt_c", [128, MD], F32))
        dvec_c = st.enter_context(nc.sbuf_tensor("dvec_c", [128, MD], F32))
        acol_c = st.enter_context(nc.sbuf_tensor("acol_c", [128, MD * NST], F32))
        fbias_c = st.enter_context(nc.sbuf_tensor("fbias_c", [128, KD], F32))
        qsc = st.enter_context(nc.sbuf_tensor("qsc", [128, NBQ], F32))
        qrcp = st.enter_context(nc.sbuf_tensor("qrcp", [128, NBQ], F32))
        qu = st.enter_context(nc.sbuf_tensor("qu", [128, DMODEL], F32))
        qau = st.enter_context(nc.sbuf_tensor("qau", [128, DMODEL], F32))
        qsg = st.enter_context(nc.sbuf_tensor("qsg", [128, DMODEL], F32))
        qb8 = st.enter_context(nc.sbuf_tensor("qb8", [128, DMODEL], FP8))
        qc8 = st.enter_context(nc.sbuf_tensor("qc8", [128, DMODEL], FP8))
        qm = st.enter_context(nc.sbuf_tensor("qm", [128, DMODEL], U8))
        xa = st.enter_context(nc.sbuf_tensor("xa", [128, DMODEL], F32))
        x1t = st.enter_context(nc.sbuf_tensor("x1t", [128, DMODEL], F32))
        xdt = st.enter_context(nc.sbuf_tensor("xdt", [128, DMODEL], BF16))
        lns1 = st.enter_context(nc.sbuf_tensor("lns1", [128, NTILE], F32))
        lns2 = st.enter_context(nc.sbuf_tensor("lns2", [128, NTILE], F32))
        lnm = st.enter_context(nc.sbuf_tensor("lnm", [128, NTILE], F32))
        lnv = st.enter_context(nc.sbuf_tensor("lnv", [128, NTILE], F32))
        lnr = st.enter_context(nc.sbuf_tensor("lnr", [128, NTILE], F32))
        xdT = st.enter_context(nc.sbuf_tensor("xdT", [128, KD * C], BF16))
        fout = st.enter_context(nc.sbuf_tensor("fout", [128, KD * C], BF16))
        xn2T = st.enter_context(nc.sbuf_tensor("xn2T", [128, KD * C], BF16))
        xproj = st.enter_context(nc.sbuf_tensor("xproj", [128, MD * (4 + C)], BF16))
        zs2 = st.enter_context(nc.sbuf_tensor("zs2", [128, MD * C], BF16))
        dtin16 = st.enter_context(nc.sbuf_tensor("dtin16", [NST, TSC], BF16))
        cacc = st.enter_context(nc.sbuf_tensor("cacc", [128, 2 * C], F32))
        xcb = st.enter_context(nc.sbuf_tensor("xcb", [128, 2 * C], BF16))
        xcg = st.enter_context(nc.sbuf_tensor("xcg", [128, 2 * TSC], BF16))
        zsg = st.enter_context(nc.sbuf_tensor("zsg", [128, 2 * TSC], BF16))
        dts = st.enter_context(nc.sbuf_tensor("dts", [128, 2 * TSC], BF16))
        wgb = st.enter_context(nc.sbuf_tensor("wgb", [128, TSC], BF16))
        dA2 = st.enter_context(nc.sbuf_tensor("dA2", [128, 2 * TSC], F32))
        ub2 = st.enter_context(nc.sbuf_tensor("ub2", [128, 2 * TSC], BF16))
        Bb2 = st.enter_context(nc.sbuf_tensor("Bb2", [128, 2 * TSC], BF16))
        ygb = st.enter_context(nc.sbuf_tensor("ygb", [128, TSC], BF16))
        state = st.enter_context(nc.sbuf_tensor("state", [128, MD * NST], F32))
        ps_t = st.enter_context(nc.psum_tensor("ps_t", [128, 1024], BF16))
        ps_t2 = ps_t
        ps_f0 = st.enter_context(nc.psum_tensor("ps_f0", [128, 512], F32))
        ps_f1 = st.enter_context(nc.psum_tensor("ps_f1", [128, 512], F32))
        ps_dt = st.enter_context(nc.psum_tensor("ps_dt", [128, 512], F32))
        ps_x = ps_dt
        ps_y = st.enter_context(nc.psum_tensor("ps_y", [128, TSC], F32))
        sem_LD = st.enter_context(nc.semaphore("LD"))
        sem_GD = st.enter_context(nc.semaphore("GD"))
        sem_VE = st.enter_context(nc.semaphore("VE"))
        sem_AC = st.enter_context(nc.semaphore("AC"))
        sem_TE = st.enter_context(nc.semaphore("TE"))
        sem_CC = st.enter_context(nc.semaphore("CC"))
        sem_DA = st.enter_context(nc.semaphore("DA"))
        sem_SCN = st.enter_context(nc.semaphore("SCN"))
        sem_ODMA = st.enter_context(nc.semaphore("ODMA"))
        sem_PL = st.enter_context(nc.semaphore("PL"))
        block = st.enter_context(nc.Block())
        xh = qsg
        payt = qau
        sems = {"LD": sem_LD, "GD": sem_GD, "VE": sem_VE, "AC": sem_AC,
                "TE": sem_TE, "CC": sem_CC, "DA": sem_DA, "SCN": sem_SCN,
                "ODMA": sem_ODMA, "PL": sem_PL}
        ps_f = [ps_f0, ps_f1]
        ps_r = [ps_f0, ps_f1, ps_y[:, 0:512], ps_y[:, 512:1024]]

        def barrier():
            for e in ENGS:
                for sname in ("LD", "GD", "VE", "AC", "TE", "CC", "DA",
                              "SCN", "PL"):
                    s.wait(e, sname)

        def emit_quant(src, dst, wscale=False):
            v3 = src[:].rearrange("p (s n) -> p s n", n=32)
            s.op("vector", lambda v: v.tensor_reduce(
                out=qsc[:], in_=v3, axis=AXX, op=ALU.max,
                apply_absolute_value=True))
            sc2 = (1.0 / 7.5) * (0.5 if wscale else 1.0)
            s.op("vector", lambda v: v.tensor_scalar(
                out=qsc[:], in0=qsc[:], scalar1=sc2, op0=ALU.mult,
                scalar2=(0.5e-30 if wscale else 1e-30), op1=ALU.max))
            s.op("vector", lambda v: v.reciprocal(out=qrcp[:], in_=qsc[:]))
            if wscale:
                s.op("vector", lambda v: v.tensor_single_scalar(
                    out=qrcp[:], in_=qrcp[:], scalar=0.5, op=ALU.mult))
            s.op("vector", lambda v: v.tensor_tensor(
                out=qu[:].rearrange("p (s n) -> p s n", n=32), in0=v3,
                in1=qrcp[:].broadcast_to([128, NBQ, 32]), op=ALU.mult))
            s.wait("scalar", "VE")
            s.wait("scalar", "LD")
            s.op("scalar", lambda e: e.activation(out=qau[:], in_=qu[:],
                                                  func=AF.Abs, bias=0.0),
                 sw=False)
            s.op("scalar", lambda e: e.activation(out=qsg[:], in_=qu[:],
                                                  func=AF.Sign, bias=0.0),
                 sw=False)
            s.op("scalar", lambda e: e.activation(out=qb8[:], in_=qau[:],
                                                  func=AF.Copy, bias=0.0),
                 sw=False)
            s.op("scalar", lambda e: e.activation(out=qc8[:], in_=qau[:],
                                                  func=AF.Copy, bias=1.0),
                 sw=False)
            s.op("scalar", lambda e: e.activation(out=qu[:], in_=qc8[:],
                                                  func=AF.Copy, bias=-1.0),
                 sw=False)
            s.wait("vector", "AC")
            s.op("vector", lambda v: v.tensor_single_scalar(
                out=qm[:], in_=qau[:], scalar=1.0, op=ALU.is_lt), sw=False)
            s.op("vector", lambda v: v.tensor_copy(out=qau[:], in_=qb8[:]),
                 sw=False)
            s.op("vector", lambda v: v.copy_predicated(out=qau[:], mask=qm[:],
                                                       data=qu[:]), sw=False)
            s.op("vector", lambda v: v.tensor_mul(out=qsg[:], in0=qau[:],
                                                  in1=qsg[:]), sw=False)
            s.op("vector", lambda v: v.tensor_tensor(
                out=dst[:].rearrange("p (s n) -> p s n", n=32),
                in0=qsg[:].rearrange("p (s n) -> p s n", n=32),
                in1=qsc[:].broadcast_to([128, NBQ, 32]), op=ALU.mult),
                sw=False)

        def emit_ln_stats(src3, sq3):
            s.op("vector", lambda v: v.tensor_reduce(
                out=lns1[:, 0:NTILE], in_=src3, axis=AXX, op=ALU.add))
            s.op("scalar", lambda e: e.activation(
                out=sq3, in_=src3, func=AF.Square, bias=0.0))
            s.wait("vector", "AC")
            s.op("vector", lambda v: v.tensor_reduce(
                out=lns2[:, 0:NTILE], in_=sq3, axis=AXX, op=ALU.add),
                sw=False)
            s.op("vector", lambda v: v.tensor_single_scalar(
                out=lnm[:, 0:NTILE], in_=lns1[:, 0:NTILE],
                scalar=1.0 / DMODEL, op=ALU.mult))
            s.op("vector", lambda v: v.tensor_single_scalar(
                out=lnv[:, 0:NTILE], in_=lns2[:, 0:NTILE],
                scalar=1.0 / DMODEL, op=ALU.mult))
            s.op("vector", lambda v: v.tensor_mul(
                out=lns1[:, 0:NTILE], in0=lnm[:, 0:NTILE],
                in1=lnm[:, 0:NTILE]))
            s.op("vector", lambda v: v.tensor_sub(
                out=lnv[:, 0:NTILE], in0=lnv[:, 0:NTILE],
                in1=lns1[:, 0:NTILE]))
            s.op("vector", lambda v: v.tensor_single_scalar(
                out=lnv[:, 0:NTILE], in_=lnv[:, 0:NTILE], scalar=1e-5,
                op=ALU.add))
            s.wait("scalar", "VE")
            s.op("scalar", lambda e: e.activation(
                out=lnv[:, 0:NTILE], in_=lnv[:, 0:NTILE], func=AF.Ln,
                bias=0.0))
            s.wait("vector", "AC")
            s.op("vector", lambda v: v.tensor_single_scalar(
                out=lnv[:, 0:NTILE], in_=lnv[:, 0:NTILE], scalar=-0.5,
                op=ALU.mult))
            s.wait("scalar", "VE")
            s.op("scalar", lambda e: e.activation(
                out=lnr[:, 0:NTILE], in_=lnv[:, 0:NTILE], func=AF.Exp,
                bias=0.0))

        xck = dA2
        xck3 = dA2[:].rearrange("p (t d) -> p t d", d=DMODEL)
        sqA3 = Bb2[:].rearrange("p (t d) -> p t d", d=DMODEL)
        sqB3 = xcg[:].rearrange("p (t d) -> p t d", d=DMODEL)
        x1ck = ub2
        x1ck3 = ub2[:].rearrange("p (t d) -> p t d", d=DMODEL)

        def xdT_r(par, kt):
            if par == 0:
                return xdT[:, kt * C:(kt + 1) * C]
            if kt < 4:
                return wgb[:, kt * C:(kt + 1) * C]
            return ygb[:, (kt - 4) * C:(kt - 3) * C]

        def xdT_w(par, ti):
            if par == 0:
                s.op("scalar", lambda e, ti=ti: e.activation(
                    out=xdT[:].rearrange("p (k c) -> p k c", k=KD)
                    [:, :, ti * 128:(ti + 1) * 128],
                    in_=ps_t[:].rearrange("p (k c) -> p k c", k=KD),
                    func=AF.Copy, bias=0.0))
            else:
                s.op("scalar", lambda e, ti=ti: e.activation(
                    out=wgb[:].rearrange("p (k c) -> p k c", k=4)
                    [:, :, ti * 128:(ti + 1) * 128],
                    in_=ps_t[:, 0:512].rearrange("p (k c) -> p k c", k=4),
                    func=AF.Copy, bias=0.0))
                s.op("scalar", lambda e, ti=ti: e.activation(
                    out=ygb[:].rearrange("p (k c) -> p k c", k=4)
                    [:, :, ti * 128:(ti + 1) * 128],
                    in_=ps_t[:, 512:1024].rearrange("p (k c) -> p k c", k=4),
                    func=AF.Copy, bias=0.0), sw=False)

        def emit_ffn_s1(src_dram3, idx, par):
            s.wait("sync", "VE")
            s.wait("sync", "AC")
            s.dma("sync", lambda e: e.dma_start(
                out=xck3, in_=src_dram3[:, idx:idx + NTILE, :]), "LD")
            s.wait("vector", "LD")
            s.wait("scalar", "LD")
            s.wait("vector", "GD")
            emit_ln_stats(xck3, sqA3)
            for ti in range(NTILE):
                s.wait("vector", "AC")
                s.op("vector", lambda v, ti=ti: v.tensor_scalar(
                    out=xh[:], in0=xck3[:, ti, :], scalar1=lnm[:, ti:ti + 1],
                    op0=ALU.subtract, scalar2=lnr[:, ti:ti + 1],
                    op1=ALU.mult))
                emit_quant(xh, xdt)
                s.wait("tensor", "VE")
                s.wait("tensor", "AC")
                for kt in range(KD):
                    s.op("tensor", lambda e, kt=kt: e.transpose(
                        out=ps_t[:, kt * 128:(kt + 1) * 128],
                        in_=xdt[:, kt * 128:(kt + 1) * 128], identity=ident[:]))
                s.wait("scalar", "TE")
                if par == 1:
                    s.wait("scalar", "VE")
                    s.wait("scalar", "LD")
                xdT_w(par, ti)

        def emit_ffn_s2(par):
            s.wait("tensor", "AC")
            for mt in range(KD):
                pb = ps_r[mt % 4]
                if mt >= 4:
                    s.wait("tensor", "AC")
                for kt in range(KD):
                    s.op("tensor", lambda e, mt=mt, kt=kt, pb=pb: e.matmul(
                        out=pb[:], lhsT=wd_s[:, kt * DMODEL + mt * 128:
                                             kt * DMODEL + mt * 128 + 128],
                        rhs=xdT_r(par, kt),
                        start=(kt == 0), stop=(kt == KD - 1)))
                s.wait("scalar", "TE")
                s.op("scalar", lambda e, mt=mt, pb=pb: e.activation(
                    out=fout[:, mt * C:(mt + 1) * C], in_=pb[:],
                    func=AF.Identity, bias=fbias_c[:, mt:mt + 1]), sw=False)

        def emit_ffn_s3(epilogue):
            for ti in range(NTILE):
                s.wait("tensor", "AC")
                s.wait("tensor", "VE")
                for mt in range(KD):
                    s.op("tensor", lambda e, mt=mt, ti=ti: e.transpose(
                        out=ps_t2[:, mt * 128:(mt + 1) * 128],
                        in_=fout[:, mt * C + ti * 128:mt * C + ti * 128 + 128],
                        identity=ident[:]))
                s.wait("vector", "TE")
                epilogue(ti)

        x3src = x_in.rearrange("(t p) d -> p t d", p=128)

        def emit_A_s1(sc, ch):
            emit_ffn_s1(x3src, (sc * TSC + ch * C) // 128, ch % 2)

        def emit_A_s3(sc, ch):
            g0 = sc * TSC + ch * C

            def epilogue(ti):
                s.wait("sync", "VE")
                s.dma("sync", lambda e, ti=ti: e.dma_start(
                    out=xa[:],
                    in_=x_in[g0 + ti * 128:g0 + (ti + 1) * 128, :]), "LD")
                if ti == 0:
                    s.wait("vector", "AC")
                s.wait("vector", "LD")
                s.op("vector", lambda v: v.tensor_add(
                    out=x1t[:], in0=ps_t2[:], in1=xa[:]))
                s.wait("vector", "LD")
                s.op("vector", lambda v: v.tensor_single_scalar(
                    out=qu[:], in_=x1t[:], scalar=0.5, op=ALU.mult))
                tloc = ch * C + ti * 128
                slot, j0 = divmod(tloc, 1024)
                s.wait("sync", "VE")
                s.dma("sync", lambda e, slot=slot, j0=j0: e.dma_start(
                    out=pay_in[sc][slot, j0:j0 + 128, :], in_=qu[:]), "LD")
                s.op("vector", lambda v, ti=ti: v.tensor_copy(
                    out=x1ck[:, ti * DMODEL:(ti + 1) * DMODEL], in_=x1t[:]),
                    sw=False)

            emit_ffn_s3(epilogue)

        def emit_A_ln2(sc, ch):
            s.wait("scalar", "VE")
            emit_ln_stats(x1ck3, sqB3)
            for ti in range(NTILE):
                s.wait("vector", "AC")
                s.op("vector", lambda v, ti=ti: v.tensor_scalar(
                    out=xdt[:], in0=x1ck3[:, ti, :],
                    scalar1=lnm[:, ti:ti + 1], op0=ALU.subtract,
                    scalar2=lnr[:, ti:ti + 1], op1=ALU.mult))
                s.wait("tensor", "VE")
                s.wait("tensor", "AC")
                for kt in range(KD):
                    s.op("tensor", lambda e, kt=kt: e.transpose(
                        out=ps_t[:, kt * 128:(kt + 1) * 128],
                        in_=xdt[:, kt * 128:(kt + 1) * 128], identity=ident[:]))
                s.wait("scalar", "TE")
                s.op("scalar", lambda e, ti=ti: e.activation(
                    out=xn2T[:].rearrange("p (k c) -> p k c", k=KD)
                    [:, :, ti * 128:(ti + 1) * 128],
                    in_=ps_t[:].rearrange("p (k c) -> p k c", k=KD),
                    func=AF.Copy, bias=0.0))

        def emit_phaseB1_chunk(sc, ch):
            s.wait("tensor", "AC")
            s.wait("tensor", "VE")
            s.wait("scalar", "VE")
            s.wait("scalar", "GD")
            s.wait("scalar", "LD")
            for mt in range(2 * MD):
                pb = ps_r[mt % 4]
                if mt >= 4:
                    s.wait("tensor", "AC")
                for kt in range(KD):
                    s.op("tensor", lambda e, mt=mt, kt=kt, pb=pb: e.matmul(
                        out=pb[:], lhsT=win_s[:, kt * 2 * DH + mt * 128:
                                              kt * 2 * DH + mt * 128 + 128],
                        rhs=xn2T[:, kt * C:(kt + 1) * C],
                        start=(kt == 0), stop=(kt == KD - 1)))
                s.wait("scalar", "TE")
                if mt < MD:
                    g = mt
                    s.op("scalar", lambda e, g=g, pb=pb: e.activation(
                        out=xproj[:, g * (4 + C) + 4:g * (4 + C) + 4 + C],
                        in_=pb[:], func=AF.Copy, bias=0.0))
                else:
                    g = mt - MD
                    s.op("scalar", lambda e, g=g, pb=pb: e.activation(
                        out=zs2[:, g * C:(g + 1) * C],
                        in_=pb[:], func=AF.Silu, bias=0.0))
            s.wait("gpsimd", "AC")
            for g in range(MD):
                s.dma("gpsimd", lambda e, g=g: e.dma_start(
                    out=zsil_d[g * 128:(g + 1) * 128,
                               sc * TSC + ch * C:sc * TSC + ch * C + C],
                    in_=zs2[:, g * C:(g + 1) * C]), "GD")
            s.wait("vector", "AC")
            for g in range(MD):
                par = g % 2
                xp = xproj[:, g * (4 + C):(g + 1) * (4 + C)]
                ca = cacc[:, par * C:(par + 1) * C]
                if g >= 2:
                    s.wait("vector", "AC")
                s.op("vector", lambda v, xp=xp, ca=ca, g=g: v.tensor_single_scalar(
                    out=ca, in_=xp[:, 1:1 + C],
                    scalar=convw_c[:, 0 * MD + g:0 * MD + g + 1], op=ALU.mult))
                for tap in range(1, 4):
                    s.nop("vector")
                    s.op("vector", lambda v, xp=xp, ca=ca, g=g, tap=tap:
                         v.scalar_tensor_tensor(
                             out=ca, in0=xp[:, 1 + tap:1 + tap + C],
                             scalar=convw_c[:, tap * MD + g:tap * MD + g + 1],
                             in1=ca, op0=ALU.mult, op1=ALU.add))
                s.op("vector", lambda v, xp=xp: v.tensor_copy(
                    out=xp[:, 1:4], in_=xp[:, 1 + C:4 + C]))
                s.wait("scalar", "VE")
                s.wait("scalar", "TE")
                s.wait("scalar", "GD")
                s.op("scalar", lambda e, g=g, ca=ca, par=par: e.activation(
                    out=xcb[:, par * C:(par + 1) * C], in_=ca,
                    func=AF.Silu, bias=convb_c[:, g:g + 1]))
                s.wait("tensor", "AC")
                s.op("tensor", lambda e, g=g, par=par: e.matmul(
                    out=ps_x[0:32, :], lhsT=wx_s[:, g * 2 * NST:(g + 1) * 2 * NST],
                    rhs=xcb[:, par * C:(par + 1) * C],
                    start=(g == 0), stop=(g == MD - 1)))
                s.wait("gpsimd", "AC")
                s.dma("gpsimd", lambda e, g=g, par=par: e.dma_start(
                    out=xconv_d[g * 128:(g + 1) * 128,
                                sc * TSC + ch * C:sc * TSC + ch * C + C],
                    in_=xcb[:, par * C:(par + 1) * C]), "GD")
            s.wait("vector", "TE")
            s.op("vector", lambda v: v.tensor_copy(
                out=cacc[0:32, 0:C], in_=ps_x[0:32, :]))
            s.wait("sync", "VE")
            s.dma("sync", lambda e: e.dma_start(
                out=ar_in[sc][:, ch * C:(ch + 1) * C],
                in_=cacc[0:32, 0:C]), "LD")

        def emit_AR(sc):
            s.wait("gpsimd", "LD")
            s.op("gpsimd", lambda g: g.collective_compute(
                "AllReduce", ALU.add,
                replica_groups=[[0, 1], [2, 3], [4, 5], [6, 7]],
                ins=[ar_in[sc].ap().opt()],
                outs=[ar_out[sc].ap().opt()]), "CC")

        def emit_phaseB2_scroll(sc):
            t0 = sc * TSC
            s.wait("gpsimd", "CC")
            s.dma("gpsimd", lambda g: g.dma_start(
                out=dtin16[:, :], in_=ar_out[sc][0:NST, :]), "GD")
            s.dma("gpsimd", lambda g: g.dma_start(
                out=zs2[0:NST, 0:TSC], in_=ar_out[sc][NST:2 * NST, :]), "GD")
            s.wait("sync", "GD")
            s.dma("sync", lambda e: e.dma_start(out=Bd[:, :],
                                                in_=zs2[0:NST, 0:TSC]), "LD")
            s.wait("sync", "VE")
            s.wait("sync", "AC")
            s.dma("sync", lambda e: e.dma_start(
                out=xcg[:, 0:TSC], in_=xconv_d[0:128, t0:t0 + TSC]), "LD")
            s.dma("sync", lambda e: e.dma_start(
                out=zsg[:, 0:TSC], in_=zsil_d[0:128, t0:t0 + TSC]), "LD")
            ld_xc0 = s.n("LD")

            NL = MD * NST
            scn_a = [0] * NL
            te_a = [0] * NL
            pl_a = [0] * NL
            da_a = [0] * NL
            ld_a = [0] * NL
            ac_tail = [0] * NL
            ve_wmul = [0] * MD
            ve_evac = [0] * MD
            ac_evac = [0] * MD
            ld_y3 = [0] * MD
            ld_xc = [0] * MD
            ld_xc[0] = ld_xc0

            def emit_bcast(L):
                if L >= NL:
                    return
                bi = L % 2
                if L >= 2:
                    s.wait("sync", "PL", pl_a[L - 2])
                n = L % NST
                s.dma("sync", lambda e, n=n, bi=bi: e.dma_start(
                    out=Bb2[:, bi * TSC:(bi + 1) * TSC],
                    in_=Bd.ap()[None, n, :].broadcast_to([128, TSC])), "LD")
                ld_a[L] = s.n("LD")

            def emit_evac(g):
                pg = g % 2
                L15 = g * NST + NST - 1
                s.wait("scalar", "TE", te_a[L15])
                if g >= 1:
                    s.wait("scalar", "LD", ld_y3[g - 1])
                for qq in range(4):
                    s.op("scalar", lambda e, qq=qq: e.activation(
                        out=ygb[:, qq * 512:(qq + 1) * 512],
                        in_=ps_y[:, qq * 512:(qq + 1) * 512],
                        func=AF.Copy, bias=0.0), sw=(qq == 0))
                ac_evac[g] = s.n("AC")
                s.wait("vector", "AC", ac_evac[g])
                s.op("vector", lambda v, g=g, pg=pg: v.scalar_tensor_tensor(
                    out=ygb[:], in0=xcg[:, pg * TSC:(pg + 1) * TSC],
                    scalar=dvec_c[:, g:g + 1], in1=ygb[:],
                    op0=ALU.mult, op1=ALU.add), sw=False)
                s.op("vector", lambda v, pg=pg: v.tensor_mul(
                    out=ygb[:], in0=ygb[:], in1=zsg[:, pg * TSC:(pg + 1) * TSC]))
                ve_evac[g] = s.n("VE")
                s.wait("sync", "VE", ve_evac[g])
                s.dma("sync", lambda e, g=g: e.dma_start(
                    out=y3_d[g * 128:(g + 1) * 128, t0:t0 + TSC], in_=ygb[:]),
                    "LD")
                ld_y3[g] = s.n("LD")

            for L in range(NL):
                g, n = divmod(L, NST)
                bi = L % 2
                pg = g % 2
                idx = L
                if n == 0:
                    if g == 0:
                        emit_bcast(0)
                        emit_bcast(1)
                    else:
                        emit_evac(g - 1)
                    s.wait("tensor", "GD")
                    for q in range(4):
                        pb = ps_f[q % 2]
                        par = q % 2
                        qsl = slice(par * C, (par + 1) * C)
                        dsl = slice(pg * TSC + q * C, pg * TSC + (q + 1) * C)
                        s.wait("tensor", "AC")
                        if q >= 2:
                            s.wait("tensor", "VE")
                        s.op("tensor", lambda e, g=g, q=q, pb=pb: e.matmul(
                            out=pb[:], lhsT=wdt_s[:, g * 128:(g + 1) * 128],
                            rhs=dtin16[:, q * C:(q + 1) * C],
                            start=True, stop=True))
                        s.wait("scalar", "TE")
                        if q >= 2:
                            s.wait("scalar", "VE")
                        s.op("scalar", lambda e, g=g, pb=pb, qsl=qsl:
                             e.activation(out=qu[:, qsl], in_=pb[:],
                                          func=AF.Abs,
                                          bias=bdt_c[:, g:g + 1]), sw=False)
                        s.op("scalar", lambda e, qsl=qsl: e.activation(
                            out=qu[:, qsl], in_=qu[:, qsl],
                            func=AF.Exp, bias=0.0, scale=-1.0))
                        s.op("scalar", lambda e, qsl=qsl: e.activation(
                            out=qu[:, qsl], in_=qu[:, qsl],
                            func=AF.Ln, bias=1.0))
                        s.op("scalar", lambda e, g=g, pb=pb, dsl=dsl:
                             e.activation(out=dts[:, dsl], in_=pb[:],
                                          func=AF.Relu,
                                          bias=bdt_c[:, g:g + 1]), sw=False)
                        s.wait("vector", "AC")
                        s.op("vector", lambda v, dsl=dsl, qsl=qsl:
                             v.tensor_tensor(out=dts[:, dsl],
                                             in0=dts[:, dsl],
                                             in1=qu[:, qsl], op=ALU.add),
                             sw=False)
                    if g == 0:
                        s.wait("vector", "TE")
                    s.wait("vector", "LD", ld_xc[g])
                    s.op("vector", lambda v, pg=pg: v.tensor_mul(
                        out=wgb[:], in0=dts[:, pg * TSC:(pg + 1) * TSC],
                        in1=xcg[:, pg * TSC:(pg + 1) * TSC]), sw=True)
                    ve_wmul[g] = s.n("VE")
                    ve_dts = s.n("VE")
                    s.wait("scalar", "VE", ve_dts)
                if L >= 2:
                    s.wait("scalar", "SCN", scn_a[L - 2])
                s.op("scalar", lambda e, pg=pg, bi=bi, idx=idx: e.activation(
                    out=dA2[:, bi * TSC:(bi + 1) * TSC],
                    in_=dts[:, pg * TSC:(pg + 1) * TSC],
                    func=AF.Exp, bias=0.0,
                    scale=acol_c[:, idx:idx + 1]), inc="DA", sw=False)
                da_a[L] = s.n("DA")
                if sc < NSC - 1 and L >= 1:
                    s.wait("scalar", "SCN", scn_a[L - 1])
                    s.op("scalar", lambda e, Lp=L - 1: e.activation(
                        out=state[:, Lp:Lp + 1],
                        in_=ub2[:, ((Lp % 2) + 1) * TSC - 1:
                                ((Lp % 2) + 1) * TSC],
                        func=AF.Copy, bias=0.0), sw=False)
                ac_tail[L - 1] = s.n("AC")
                s.wait("vector", "LD", ld_a[L])
                if L >= 2:
                    s.wait("vector", "TE", te_a[L - 2])
                    s.wait("vector", "AC", ac_tail[L - 2])
                s.op("vector", lambda v, bi=bi: v.tensor_mul(
                    out=ub2[:, bi * TSC:(bi + 1) * TSC], in0=wgb[:],
                    in1=Bb2[:, bi * TSC:(bi + 1) * TSC]), inc="PL", sw=False)
                pl_a[L] = s.n("PL")
                emit_bcast(L + 2)
                if n == 2 and g + 1 < MD:
                    npg = (g + 1) % 2
                    if g >= 1:
                        s.wait("sync", "VE", ve_evac[g - 1])
                    s.dma("sync", lambda e, g=g, npg=npg: e.dma_start(
                        out=xcg[:, npg * TSC:(npg + 1) * TSC],
                        in_=xconv_d[(g + 1) * 128:(g + 2) * 128,
                                    t0:t0 + TSC]), "LD")
                    s.dma("sync", lambda e, g=g, npg=npg: e.dma_start(
                        out=zsg[:, npg * TSC:(npg + 1) * TSC],
                        in_=zsil_d[(g + 1) * 128:(g + 2) * 128,
                                   t0:t0 + TSC]), "LD")
                    ld_xc[g + 1] = s.n("LD")
                s.wait("vector", "DA", da_a[L])
                s.op("vector", lambda v, bi=bi, idx=idx: v.tensor_tensor_scan(
                    out=ub2[:, bi * TSC:(bi + 1) * TSC],
                    data0=dA2[:, bi * TSC:(bi + 1) * TSC],
                    data1=ub2[:, bi * TSC:(bi + 1) * TSC],
                    initial=state[:, idx:idx + 1],
                    op0=ALU.mult, op1=ALU.add), inc="SCN", sw=False)
                scn_a[L] = s.n("SCN")
                s.wait("tensor", "SCN", scn_a[L])
                if n == 0 and g >= 1:
                    s.wait("tensor", "AC", ac_evac[g - 1])
                for qq in range(4):
                    s.op("tensor", lambda e, bi=bi, qq=qq, n=n: e.matmul(
                        out=ps_y[:, qq * 512:(qq + 1) * 512],
                        lhsT=ident[:],
                        rhs=ub2[:, bi * TSC + qq * 512:
                                bi * TSC + (qq + 1) * 512],
                        start=(n == 0), stop=(n == NST - 1)))
                te_a[L] = s.n("TE")
            if sc < NSC - 1:
                s.wait("scalar", "SCN", scn_a[NL - 1])
                s.op("scalar", lambda e: e.activation(
                    out=state[:, NL - 1:NL],
                    in_=ub2[:, (((NL - 1) % 2) + 1) * TSC - 1:
                            (((NL - 1) % 2) + 1) * TSC],
                    func=AF.Copy, bias=0.0), sw=False)
            emit_evac(MD - 1)

            for ti in range(NTILE):
                s.wait("sync", "TE")
                s.wait("sync", "LD")
                s.dma("sync", lambda e, ti=ti: e.dma_start(
                    out=xdT[:].rearrange("p (g c) -> p g c", g=MD),
                    in_=y3_d.ap().rearrange("(g p) t -> p g t", p=128)
                    [:, :, t0 + ti * C:t0 + (ti + 1) * C]), "LD")
                s.wait("tensor", "LD")
                s.wait("tensor", "AC")
                for mt in range(KD):
                    pb = ps_r[mt % 4]
                    if mt >= 4:
                        s.wait("tensor", "AC")
                    for kt in range(MD):
                        s.op("tensor", lambda e, mt=mt, kt=kt, pb=pb: e.matmul(
                            out=pb[:], lhsT=wout_s[:, kt * DMODEL + mt * 128:
                                                   kt * DMODEL + mt * 128 + 128],
                            rhs=xdT[:, kt * C:(kt + 1) * C],
                            start=(kt == 0), stop=(kt == MD - 1)))
                    s.wait("scalar", "TE")
                    s.op("scalar", lambda e, mt=mt, pb=pb: e.activation(
                        out=fout[:, mt * C:(mt + 1) * C], in_=pb[:],
                        func=AF.Copy, bias=0.0), sw=False)
                for tj in range(NTILE):
                    stg = [qsg, qau][tj % 2]
                    s.wait("tensor", "AC")
                    for mt in range(KD):
                        s.op("tensor", lambda e, mt=mt, tj=tj: e.transpose(
                            out=ps_t2[:, mt * 128:(mt + 1) * 128],
                            in_=fout[:, mt * C + tj * 128:
                                     mt * C + tj * 128 + 128],
                            identity=ident[:]))
                    s.wait("scalar", "TE")
                    s.wait("scalar", "GD")
                    s.op("scalar", lambda e, stg=stg: e.activation(
                        out=stg[:], in_=ps_t2[:], func=AF.Copy, bias=0.0),
                        sw=False)
                    tloc = ti * C + tj * 128
                    slot, j0 = divmod(tloc, 1024)
                    s.wait("gpsimd", "AC")
                    s.dma("gpsimd", lambda g, slot=slot, j0=j0, stg=stg:
                          g.dma_start(
                              out=pay_in[sc][slot, j0:j0 + 128, :],
                              in_=stg[:], accum_op=ALU.add), "GD")

        def emit_RS(sc):
            s.wait("gpsimd", "GD")
            s.wait("gpsimd", "LD")
            s.op("gpsimd", lambda g: g.collective_compute(
                "ReduceScatter", ALU.add,
                replica_groups=[[0, 1], [2, 3], [4, 5], [6, 7]],
                ins=[pay_in[sc].ap().opt()],
                outs=[pay_out[sc].ap().opt()]), "CC")

        def emit_C_s1(sc, ch):
            p3 = pay_out[sc].ap().rearrange("(t p) d -> p t d", p=128)
            s.wait("sync", "CC", 2 * sc + 2)
            emit_ffn_s1(p3, ch * NTILE, (sc * 2 + ch) % 2)

        def emit_C_s3(sc, ch):
            def epilogue(ti):
                s.wait("sync", "VE")
                s.dma("sync", lambda e, ti=ti: e.dma_start(
                    out=xa[:],
                    in_=pay_out[sc][ch * C + ti * 128:ch * C + (ti + 1) * 128,
                                    :]), "LD")
                s.wait("vector", "ODMA")
                s.wait("vector", "LD")
                s.op("vector", lambda v: v.tensor_add(
                    out=x1t[:], in0=ps_t2[:], in1=xa[:]), sw=False)
                j0 = sc * 1024 + ch * C + ti * 128
                s.wait("sync", "VE")
                s.dma("sync", lambda e, j0=j0: e.dma_start(
                    out=out_d[j0:j0 + 128, :], in_=x1t[:]), "ODMA")

            emit_ffn_s3(epilogue)

        def emit_prep():
            def ld(out_ap, in_ap):
                s.dma("gpsimd", lambda g, o=out_ap, i=in_ap:
                      g.dma_start(out=o, in_=i), "GD")
            for kt in range(KD):
                ld(win_s[:, kt * 2 * DH:(kt + 1) * 2 * DH],
                   win_in[kt * 128:(kt + 1) * 128, :])
            for kt in range(MD):
                ld(wout_s[:, kt * DMODEL:(kt + 1) * DMODEL],
                   wout_in[kt * 128:(kt + 1) * 128, :])
                ld(wx_s[:, kt * 2 * NST:(kt + 1) * 2 * NST],
                   wx_in[kt * 128:(kt + 1) * 128, :])
            ld(wdt_s[:, :], wdt_in)
            ld(ident[:, :], ident_in)
            for kt in range(MD):
                for tap in range(4):
                    ld(convw_c[:, tap * MD + kt:tap * MD + kt + 1],
                       convw_in[tap, kt * 128:(kt + 1) * 128][:, None])
                ld(convb_c[:, kt:kt + 1], convb_in[kt * 128:(kt + 1) * 128, :])
                ld(bdt_c[:, kt:kt + 1], bdt_in[kt * 128:(kt + 1) * 128, :])
                ld(dvec_c[:, kt:kt + 1], dvec_in[kt * 128:(kt + 1) * 128, :])
                ld(acol_c[:, kt * NST:(kt + 1) * NST],
                   alog_in[kt * 128:(kt + 1) * 128, :])
            for mt in range(KD):
                ld(fbias_c[:, mt:mt + 1], fb_in[mt * 128:(mt + 1) * 128, :])
            s.op("vector", lambda v: v.memset(state[:], 0.0))
            s.op("vector", lambda v: v.memset(xproj[:], 0.0))
            s.wait("scalar", "GD")
            s.op("scalar", lambda e: e.activation(out=acol_c[:], in_=acol_c[:],
                                                  func=AF.Exp, bias=0.0))
            s.wait("vector", "AC")
            s.op("vector", lambda v: v.tensor_single_scalar(
                out=acol_c[:], in_=acol_c[:], scalar=-1.0, op=ALU.mult))
            s.op("vector", lambda v: v.tensor_single_scalar(
                out=fbias_c[:], in_=fbias_c[:], scalar=0.5, op=ALU.mult))
            for mt in range(KD):
                s.dma("sync", lambda e, mt=mt: e.dma_start(
                    out=xa[:], in_=fw_in[mt * 128:(mt + 1) * 128, :]), "LD")
                s.wait("vector", "LD")
                s.wait("scalar", "LD")
                emit_quant(xa, xdt, wscale=True)
                s.wait("tensor", "VE")
                for kt in range(KD):
                    s.op("tensor", lambda e, kt=kt: e.transpose(
                        out=ps_t[:, kt * 128:(kt + 1) * 128],
                        in_=xdt[:, kt * 128:(kt + 1) * 128], identity=ident[:]))
                s.wait("scalar", "TE")
                for kt in range(KD):
                    s.op("scalar", lambda e, kt=kt, mt=mt: e.activation(
                        out=wd_s[:, kt * DMODEL + mt * 128:
                                 kt * DMODEL + mt * 128 + 128],
                        in_=ps_t[:, kt * 128:(kt + 1) * 128],
                        func=AF.Copy, bias=0.0))
                barrier()

        emit_prep()
        for sc in range(NSC):
            for ch in range(NCH):
                emit_A_s1(sc, ch)
                if ch > 0:
                    emit_A_s3(sc, ch - 1)
                    emit_A_ln2(sc, ch - 1)
                emit_ffn_s2(ch % 2)
                if ch > 0:
                    emit_phaseB1_chunk(sc, ch - 1)
            emit_A_s3(sc, NCH - 1)
            emit_A_ln2(sc, NCH - 1)
            emit_phaseB1_chunk(sc, NCH - 1)
            emit_AR(sc)
            emit_phaseB2_scroll(sc)
            emit_RS(sc)
        prevc = None
        for sc in range(NSC):
            for ch in range(2):
                emit_C_s1(sc, ch)
                if prevc is not None:
                    emit_C_s3(*prevc)
                emit_ffn_s2((sc * 2 + ch) % 2)
                prevc = (sc, ch)
        emit_C_s3(*prevc)
        s.wait("gpsimd", "ODMA")

        @block.sync
        def _(e):
            s.replay("sync", e, sems)

        @block.vector
        def _(e):
            s.replay("vector", e, sems)

        @block.scalar
        def _(e):
            s.replay("scalar", e, sems)

        @block.tensor
        def _(e):
            s.replay("tensor", e, sems)

        @block.gpsimd
        def _(e):
            s.replay("gpsimd", e, sems)

    return nc


def kernel(L0_attn_x, L0_attn_W_in, L0_attn_conv_w, L0_attn_conv_b, L0_attn_W_x,
           L0_attn_W_dt, L0_attn_b_dt, L0_attn_A_log, L0_attn_D, L0_attn_W_out,
           L0_ffn_weight, L0_ffn_bias):
    x = np.ascontiguousarray(np.asarray(L0_attn_x, dtype=np.float32))
    W_in = np.ascontiguousarray(np.asarray(L0_attn_W_in, dtype=np.float32))
    conv_w = np.ascontiguousarray(np.asarray(L0_attn_conv_w, dtype=np.float32))
    conv_b = np.ascontiguousarray(np.asarray(L0_attn_conv_b, dtype=np.float32))
    W_x = np.ascontiguousarray(np.asarray(L0_attn_W_x, dtype=np.float32))
    W_dt = np.ascontiguousarray(np.asarray(L0_attn_W_dt, dtype=np.float32))
    b_dt = np.ascontiguousarray(np.asarray(L0_attn_b_dt, dtype=np.float32))
    A_log = np.ascontiguousarray(np.asarray(L0_attn_A_log, dtype=np.float32))
    Dv = np.ascontiguousarray(np.asarray(L0_attn_D, dtype=np.float32))
    W_out = np.ascontiguousarray(np.asarray(L0_attn_W_out, dtype=np.float32))
    fw = np.ascontiguousarray(np.asarray(L0_ffn_weight, dtype=np.float32))
    fb = np.ascontiguousarray(np.asarray(L0_ffn_bias, dtype=np.float32))
    ident = np.eye(128, dtype=ml_dtypes.bfloat16)

    in_maps = []
    for r in range(8):
        b, h = r >> 1, r & 1
        sl = slice(h * DH, (h + 1) * DH)
        in_maps.append({
            "x_b": x[b],
            "win_sl": np.ascontiguousarray(
                np.concatenate([W_in[:, h * DH:(h + 1) * DH],
                                W_in[:, DIN + h * DH:DIN + (h + 1) * DH]],
                               axis=1)),
            "convw_sl": np.ascontiguousarray(conv_w[sl].T),
            "convb_sl": np.ascontiguousarray(conv_b[sl, None]),
            "wx_sl": np.ascontiguousarray(W_x[sl]),
            "wdt_sl": np.ascontiguousarray(W_dt[:, sl]),
            "bdt_sl": np.ascontiguousarray(b_dt[sl, None]),
            "alog_sl": np.ascontiguousarray(A_log[sl]),
            "dvec_sl": np.ascontiguousarray(Dv[sl, None]),
            "wout_sl": np.ascontiguousarray(W_out[sl]),
            "ffnw": fw,
            "ffnb": fb[:, None],
            "ident": ident,
        })

    nc = build_program()
    res = run_bass_kernel_spmd(nc, in_maps, core_ids=list(range(8)))

    out = np.empty((4, T, DMODEL), dtype=np.float32)
    for r in range(8):
        b, h = r >> 1, r & 1
        o = res.results[r]["out_r"]
        for k in range(NSC):
            out[b, k * TSC + h * 1024:k * TSC + (h + 1) * 1024, :] = \
                o[k * 1024:(k + 1) * 1024, :]
    return out


# version 08
# speedup vs baseline: 1.0013x; 1.0013x over previous
import numpy as np
import ml_dtypes
import concourse.bass as bass
import concourse.mybir as mybir
from concourse.bass_utils import run_bass_kernel_spmd

F32 = mybir.dt.float32
BF16 = mybir.dt.bfloat16
FP8 = mybir.dt.float8e4
U8 = mybir.dt.uint8
AF = mybir.ActivationFunctionType
ALU = mybir.AluOpType
AXX = mybir.AxisListType.X

DMODEL = 1024
DIN = 2048
DH = 1024
NST = 16
T = 8192
NSC = 4
TSC = T // NSC
NCH = 4
C = TSC // NCH
NTILE = C // 128
KD = DMODEL // 128
MD = DH // 128
NBQ = DMODEL // 32

ENGS = ("sync", "vector", "scalar", "tensor", "gpsimd")
DEF_SEM = {"vector": "VE", "scalar": "AC", "tensor": "TE"}


class Sched:
    def __init__(self):
        self.q = {e: [] for e in ENGS}
        self.cnt = {}
        self.last = {}

    def op(self, eng, fn, inc="default", k=1, sw=True):
        if inc == "default":
            inc = DEF_SEM.get(eng)
        if sw and eng in ("vector", "scalar"):
            prev = self.last.get(eng)
            if prev is not None:
                self.q[eng].append(["wait", prev[0], prev[1]])
        self.q[eng].append(["op", fn, inc, k])
        if inc is not None:
            self.cnt[inc] = self.cnt.get(inc, 0) + k
            self.last[eng] = (inc, self.cnt[inc])

    def dma(self, eng, fn, inc):
        self.op(eng, fn, inc, 16)

    def nop(self, eng, cycles=512):
        pass

    def wait(self, eng, sem, val=None):
        if val is None:
            val = self.cnt.get(sem, 0)
        self.q[eng].append(["wait", sem, val])

    def n(self, sem):
        return self.cnt.get(sem, 0)

    def replay(self, eng_name, eng, sems):
        for item in self.q[eng_name]:
            if item[0] == "op":
                _, fn, inc, k = item
                inst = fn(eng)
                if inc is not None:
                    inst.then_inc(sems[inc], k)
            else:
                _, sem, val = item
                if val > 0:
                    eng.wait_ge(sems[sem], val)


def build_program():
    nc = bass.Bass("TRN2", target_bir_lowering=False)

    x_in = nc.dram_tensor("x_b", [T, DMODEL], F32, kind="ExternalInput").ap()
    win_in = nc.dram_tensor("win_sl", [DMODEL, 2 * DH], F32, kind="ExternalInput").ap()
    convw_in = nc.dram_tensor("convw_sl", [4, DH], F32, kind="ExternalInput").ap()
    convb_in = nc.dram_tensor("convb_sl", [DH, 1], F32, kind="ExternalInput").ap()
    wx_in = nc.dram_tensor("wx_sl", [DH, 2 * NST], F32, kind="ExternalInput").ap()
    wdt_in = nc.dram_tensor("wdt_sl", [NST, DH], F32, kind="ExternalInput").ap()
    bdt_in = nc.dram_tensor("bdt_sl", [DH, 1], F32, kind="ExternalInput").ap()
    alog_in = nc.dram_tensor("alog_sl", [DH, NST], F32, kind="ExternalInput").ap()
    dvec_in = nc.dram_tensor("dvec_sl", [DH, 1], F32, kind="ExternalInput").ap()
    wout_in = nc.dram_tensor("wout_sl", [DH, DMODEL], F32, kind="ExternalInput").ap()
    fw_in = nc.dram_tensor("ffnw", [DMODEL, DMODEL], F32, kind="ExternalInput").ap()
    fb_in = nc.dram_tensor("ffnb", [DMODEL, 1], F32, kind="ExternalInput").ap()
    ident_in = nc.dram_tensor("ident", [128, 128], BF16, kind="ExternalInput").ap()
    out_d = nc.dram_tensor("out_r", [NSC * 1024, DMODEL], F32,
                           kind="ExternalOutput").ap()

    pay_in = [nc.dram_tensor(f"pay_in{k}", [2, 1024, DMODEL], F32, kind="Internal")
              for k in range(NSC)]
    pay_out = [nc.dram_tensor(f"pay_out{k}", [1024, DMODEL], F32, kind="Internal")
               for k in range(NSC)]
    ar_in = [nc.dram_tensor(f"ar_in{k}", [2 * NST, TSC], F32, kind="Internal")
             for k in range(NSC)]
    ar_out = [nc.dram_tensor(f"ar_out{k}", [2 * NST, TSC], F32, kind="Internal")
              for k in range(NSC)]
    xconv_d = nc.dram_tensor("xconv_d", [DH, T], BF16, kind="Internal")
    zsil_d = nc.dram_tensor("zsil_d", [DH, T], BF16, kind="Internal")
    y3_d = nc.dram_tensor("y3_d", [DH, T], BF16, kind="Internal")
    Bd = nc.dram_tensor("Bd", [NST, TSC], BF16, kind="Internal")

    s = Sched()

    from contextlib import ExitStack
    with ExitStack() as st:
        win_s = st.enter_context(nc.sbuf_tensor("win_s", [128, KD * 2 * DH], BF16))
        wout_s = st.enter_context(nc.sbuf_tensor("wout_s", [128, MD * DMODEL], BF16))
        wd_s = st.enter_context(nc.sbuf_tensor("wd_s", [128, KD * DMODEL], BF16))
        wx_s = st.enter_context(nc.sbuf_tensor("wx_s", [128, MD * 2 * NST], BF16))
        wdt_s = st.enter_context(nc.sbuf_tensor("wdt_s", [NST, DH], BF16))
        ident = st.enter_context(nc.sbuf_tensor("ident_s", [128, 128], BF16))
        convw_c = st.enter_context(nc.sbuf_tensor("convw_c", [128, 4 * MD], F32))
        convb_c = st.enter_context(nc.sbuf_tensor("convb_c", [128, MD], F32))
        bdt_c = st.enter_context(nc.sbuf_tensor("bdt_c", [128, MD], F32))
        dvec_c = st.enter_context(nc.sbuf_tensor("dvec_c", [128, MD], F32))
        acol_c = st.enter_context(nc.sbuf_tensor("acol_c", [128, MD * NST], F32))
        fbias_c = st.enter_context(nc.sbuf_tensor("fbias_c", [128, KD], F32))
        qsc = st.enter_context(nc.sbuf_tensor("qsc", [128, NBQ], F32))
        qrcp = st.enter_context(nc.sbuf_tensor("qrcp", [128, NBQ], F32))
        qu = st.enter_context(nc.sbuf_tensor("qu", [128, DMODEL], F32))
        qau = st.enter_context(nc.sbuf_tensor("qau", [128, DMODEL], F32))
        qsg = st.enter_context(nc.sbuf_tensor("qsg", [128, DMODEL], F32))
        qb8 = st.enter_context(nc.sbuf_tensor("qb8", [128, DMODEL], FP8))
        qc8 = st.enter_context(nc.sbuf_tensor("qc8", [128, DMODEL], FP8))
        qm = st.enter_context(nc.sbuf_tensor("qm", [128, DMODEL], U8))
        xa = st.enter_context(nc.sbuf_tensor("xa", [128, DMODEL], F32))
        x1t = st.enter_context(nc.sbuf_tensor("x1t", [128, DMODEL], F32))
        xdt = st.enter_context(nc.sbuf_tensor("xdt", [128, DMODEL], BF16))
        lns1 = st.enter_context(nc.sbuf_tensor("lns1", [128, NTILE], F32))
        lns2 = st.enter_context(nc.sbuf_tensor("lns2", [128, NTILE], F32))
        lnm = st.enter_context(nc.sbuf_tensor("lnm", [128, NTILE], F32))
        lnv = st.enter_context(nc.sbuf_tensor("lnv", [128, NTILE], F32))
        lnr = st.enter_context(nc.sbuf_tensor("lnr", [128, NTILE], F32))
        xdT = st.enter_context(nc.sbuf_tensor("xdT", [128, KD * C], BF16))
        fout = st.enter_context(nc.sbuf_tensor("fout", [128, KD * C], BF16))
        xn2T = st.enter_context(nc.sbuf_tensor("xn2T", [128, KD * C], BF16))
        xproj = st.enter_context(nc.sbuf_tensor("xproj", [128, MD * (4 + C)], BF16))
        zs2 = st.enter_context(nc.sbuf_tensor("zs2", [128, MD * C], BF16))
        dtin16 = st.enter_context(nc.sbuf_tensor("dtin16", [NST, TSC], BF16))
        cacc = st.enter_context(nc.sbuf_tensor("cacc", [128, 2 * C], F32))
        xcb = st.enter_context(nc.sbuf_tensor("xcb", [128, 2 * C], BF16))
        xcg = st.enter_context(nc.sbuf_tensor("xcg", [128, 2 * TSC], BF16))
        zsg = st.enter_context(nc.sbuf_tensor("zsg", [128, 2 * TSC], BF16))
        dts = st.enter_context(nc.sbuf_tensor("dts", [128, 2 * TSC], BF16))
        wgb = st.enter_context(nc.sbuf_tensor("wgb", [128, TSC], BF16))
        dA2 = st.enter_context(nc.sbuf_tensor("dA2", [128, 2 * TSC], F32))
        ub2 = st.enter_context(nc.sbuf_tensor("ub2", [128, 2 * TSC], BF16))
        Bb2 = st.enter_context(nc.sbuf_tensor("Bb2", [128, 2 * TSC], BF16))
        ygb = st.enter_context(nc.sbuf_tensor("ygb", [128, TSC], BF16))
        state = st.enter_context(nc.sbuf_tensor("state", [128, MD * NST], F32))
        ps_t = st.enter_context(nc.psum_tensor("ps_t", [128, 1024], BF16))
        ps_t2 = ps_t
        ps_f0 = st.enter_context(nc.psum_tensor("ps_f0", [128, 512], F32))
        ps_f1 = st.enter_context(nc.psum_tensor("ps_f1", [128, 512], F32))
        ps_dt = st.enter_context(nc.psum_tensor("ps_dt", [128, 512], F32))
        ps_x = ps_dt
        ps_y = st.enter_context(nc.psum_tensor("ps_y", [128, TSC], F32))
        sem_LD = st.enter_context(nc.semaphore("LD"))
        sem_GD = st.enter_context(nc.semaphore("GD"))
        sem_VE = st.enter_context(nc.semaphore("VE"))
        sem_AC = st.enter_context(nc.semaphore("AC"))
        sem_TE = st.enter_context(nc.semaphore("TE"))
        sem_CC = st.enter_context(nc.semaphore("CC"))
        sem_DA = st.enter_context(nc.semaphore("DA"))
        sem_SCN = st.enter_context(nc.semaphore("SCN"))
        sem_ODMA = st.enter_context(nc.semaphore("ODMA"))
        sem_PL = st.enter_context(nc.semaphore("PL"))
        block = st.enter_context(nc.Block())
        xh = qsg
        payt = qau
        sems = {"LD": sem_LD, "GD": sem_GD, "VE": sem_VE, "AC": sem_AC,
                "TE": sem_TE, "CC": sem_CC, "DA": sem_DA, "SCN": sem_SCN,
                "ODMA": sem_ODMA, "PL": sem_PL}
        ps_f = [ps_f0, ps_f1]
        ps_r = [ps_f0, ps_f1, ps_y[:, 0:512], ps_y[:, 512:1024]]

        def barrier():
            for e in ENGS:
                for sname in ("LD", "GD", "VE", "AC", "TE", "CC", "DA",
                              "SCN", "PL"):
                    s.wait(e, sname)

        def emit_quant(src, dst, wscale=False):
            v3 = src[:].rearrange("p (s n) -> p s n", n=32)
            s.op("vector", lambda v: v.tensor_reduce(
                out=qsc[:], in_=v3, axis=AXX, op=ALU.max,
                apply_absolute_value=True))
            sc2 = (1.0 / 7.5) * (0.5 if wscale else 1.0)
            s.op("vector", lambda v: v.tensor_scalar(
                out=qsc[:], in0=qsc[:], scalar1=sc2, op0=ALU.mult,
                scalar2=(0.5e-30 if wscale else 1e-30), op1=ALU.max))
            s.op("vector", lambda v: v.reciprocal(out=qrcp[:], in_=qsc[:]))
            if wscale:
                s.op("vector", lambda v: v.tensor_single_scalar(
                    out=qrcp[:], in_=qrcp[:], scalar=0.5, op=ALU.mult))
            s.op("vector", lambda v: v.tensor_tensor(
                out=qu[:].rearrange("p (s n) -> p s n", n=32), in0=v3,
                in1=qrcp[:].broadcast_to([128, NBQ, 32]), op=ALU.mult))
            s.wait("scalar", "VE")
            s.wait("scalar", "LD")
            s.op("scalar", lambda e: e.activation(out=qau[:], in_=qu[:],
                                                  func=AF.Abs, bias=0.0),
                 sw=False)
            s.op("scalar", lambda e: e.activation(out=qsg[:], in_=qu[:],
                                                  func=AF.Sign, bias=0.0),
                 sw=False)
            s.op("scalar", lambda e: e.activation(out=qb8[:], in_=qau[:],
                                                  func=AF.Copy, bias=0.0),
                 sw=False)
            s.op("scalar", lambda e: e.activation(out=qc8[:], in_=qau[:],
                                                  func=AF.Copy, bias=1.0),
                 sw=False)
            s.op("scalar", lambda e: e.activation(out=qu[:], in_=qc8[:],
                                                  func=AF.Copy, bias=-1.0),
                 sw=False)
            s.wait("vector", "AC")
            s.op("vector", lambda v: v.tensor_single_scalar(
                out=qm[:], in_=qau[:], scalar=1.0, op=ALU.is_lt), sw=False)
            s.op("vector", lambda v: v.tensor_copy(out=qau[:], in_=qb8[:]),
                 sw=False)
            s.op("vector", lambda v: v.copy_predicated(out=qau[:], mask=qm[:],
                                                       data=qu[:]), sw=False)
            s.op("vector", lambda v: v.tensor_mul(out=qsg[:], in0=qau[:],
                                                  in1=qsg[:]), sw=False)
            s.op("vector", lambda v: v.tensor_tensor(
                out=dst[:].rearrange("p (s n) -> p s n", n=32),
                in0=qsg[:].rearrange("p (s n) -> p s n", n=32),
                in1=qsc[:].broadcast_to([128, NBQ, 32]), op=ALU.mult),
                sw=False)

        def emit_ln_stats(src3, sq3):
            s.op("vector", lambda v: v.tensor_reduce(
                out=lns1[:, 0:NTILE], in_=src3, axis=AXX, op=ALU.add))
            s.op("scalar", lambda e: e.activation(
                out=sq3, in_=src3, func=AF.Square, bias=0.0))
            s.wait("vector", "AC")
            s.op("vector", lambda v: v.tensor_reduce(
                out=lns2[:, 0:NTILE], in_=sq3, axis=AXX, op=ALU.add),
                sw=False)
            s.op("vector", lambda v: v.tensor_single_scalar(
                out=lnm[:, 0:NTILE], in_=lns1[:, 0:NTILE],
                scalar=1.0 / DMODEL, op=ALU.mult))
            s.op("vector", lambda v: v.tensor_single_scalar(
                out=lnv[:, 0:NTILE], in_=lns2[:, 0:NTILE],
                scalar=1.0 / DMODEL, op=ALU.mult))
            s.op("vector", lambda v: v.tensor_mul(
                out=lns1[:, 0:NTILE], in0=lnm[:, 0:NTILE],
                in1=lnm[:, 0:NTILE]))
            s.op("vector", lambda v: v.tensor_sub(
                out=lnv[:, 0:NTILE], in0=lnv[:, 0:NTILE],
                in1=lns1[:, 0:NTILE]))
            s.op("vector", lambda v: v.tensor_single_scalar(
                out=lnv[:, 0:NTILE], in_=lnv[:, 0:NTILE], scalar=1e-5,
                op=ALU.add))
            s.wait("scalar", "VE")
            s.op("scalar", lambda e: e.activation(
                out=lnv[:, 0:NTILE], in_=lnv[:, 0:NTILE], func=AF.Ln,
                bias=0.0))
            s.wait("vector", "AC")
            s.op("vector", lambda v: v.tensor_single_scalar(
                out=lnv[:, 0:NTILE], in_=lnv[:, 0:NTILE], scalar=-0.5,
                op=ALU.mult))
            s.wait("scalar", "VE")
            s.op("scalar", lambda e: e.activation(
                out=lnr[:, 0:NTILE], in_=lnv[:, 0:NTILE], func=AF.Exp,
                bias=0.0))

        xck = dA2
        xck3 = dA2[:].rearrange("p (t d) -> p t d", d=DMODEL)
        sqA3 = Bb2[:].rearrange("p (t d) -> p t d", d=DMODEL)
        sqB3 = xcg[:].rearrange("p (t d) -> p t d", d=DMODEL)
        x1ck = ub2
        x1ck3 = ub2[:].rearrange("p (t d) -> p t d", d=DMODEL)

        def xdT_r(par, kt):
            if par == 0:
                return xdT[:, kt * C:(kt + 1) * C]
            if kt < 4:
                return wgb[:, kt * C:(kt + 1) * C]
            return ygb[:, (kt - 4) * C:(kt - 3) * C]

        def xdT_w(par, ti):
            if par == 0:
                s.op("scalar", lambda e, ti=ti: e.activation(
                    out=xdT[:].rearrange("p (k c) -> p k c", k=KD)
                    [:, :, ti * 128:(ti + 1) * 128],
                    in_=ps_t[:].rearrange("p (k c) -> p k c", k=KD),
                    func=AF.Copy, bias=0.0))
            else:
                s.op("scalar", lambda e, ti=ti: e.activation(
                    out=wgb[:].rearrange("p (k c) -> p k c", k=4)
                    [:, :, ti * 128:(ti + 1) * 128],
                    in_=ps_t[:, 0:512].rearrange("p (k c) -> p k c", k=4),
                    func=AF.Copy, bias=0.0))
                s.op("scalar", lambda e, ti=ti: e.activation(
                    out=ygb[:].rearrange("p (k c) -> p k c", k=4)
                    [:, :, ti * 128:(ti + 1) * 128],
                    in_=ps_t[:, 512:1024].rearrange("p (k c) -> p k c", k=4),
                    func=AF.Copy, bias=0.0), sw=False)

        def emit_ffn_s1(src_dram3, idx, par):
            s.wait("sync", "VE")
            s.wait("sync", "AC")
            s.dma("sync", lambda e: e.dma_start(
                out=xck3, in_=src_dram3[:, idx:idx + NTILE, :]), "LD")
            s.wait("vector", "LD")
            s.wait("scalar", "LD")
            s.wait("vector", "GD")
            emit_ln_stats(xck3, sqA3)
            for ti in range(NTILE):
                s.wait("vector", "AC")
                s.op("vector", lambda v, ti=ti: v.tensor_scalar(
                    out=xh[:], in0=xck3[:, ti, :], scalar1=lnm[:, ti:ti + 1],
                    op0=ALU.subtract, scalar2=lnr[:, ti:ti + 1],
                    op1=ALU.mult))
                emit_quant(xh, xdt)
                s.wait("tensor", "VE")
                s.wait("tensor", "AC")
                for kt in range(KD):
                    s.op("tensor", lambda e, kt=kt: e.transpose(
                        out=ps_t[:, kt * 128:(kt + 1) * 128],
                        in_=xdt[:, kt * 128:(kt + 1) * 128], identity=ident[:]))
                s.wait("scalar", "TE")
                if par == 1:
                    s.wait("scalar", "VE")
                    s.wait("scalar", "LD")
                xdT_w(par, ti)

        def emit_ffn_s2(par):
            s.wait("tensor", "AC")
            for mt in range(KD):
                pb = ps_r[mt % 4]
                if mt >= 4:
                    s.wait("tensor", "AC")
                for kt in range(KD):
                    s.op("tensor", lambda e, mt=mt, kt=kt, pb=pb: e.matmul(
                        out=pb[:], lhsT=wd_s[:, kt * DMODEL + mt * 128:
                                             kt * DMODEL + mt * 128 + 128],
                        rhs=xdT_r(par, kt),
                        start=(kt == 0), stop=(kt == KD - 1)))
                s.wait("scalar", "TE")
                s.op("scalar", lambda e, mt=mt, pb=pb: e.activation(
                    out=fout[:, mt * C:(mt + 1) * C], in_=pb[:],
                    func=AF.Identity, bias=fbias_c[:, mt:mt + 1]), sw=False)

        def emit_ffn_s3(epilogue):
            for ti in range(NTILE):
                s.wait("tensor", "AC")
                s.wait("tensor", "VE")
                for mt in range(KD):
                    s.op("tensor", lambda e, mt=mt, ti=ti: e.transpose(
                        out=ps_t2[:, mt * 128:(mt + 1) * 128],
                        in_=fout[:, mt * C + ti * 128:mt * C + ti * 128 + 128],
                        identity=ident[:]))
                s.wait("vector", "TE")
                epilogue(ti)

        x3src = x_in.rearrange("(t p) d -> p t d", p=128)

        def emit_A_s1(sc, ch):
            emit_ffn_s1(x3src, (sc * TSC + ch * C) // 128, ch % 2)

        def emit_A_s3(sc, ch):
            g0 = sc * TSC + ch * C

            def epilogue(ti):
                s.wait("sync", "VE")
                s.dma("sync", lambda e, ti=ti: e.dma_start(
                    out=xa[:],
                    in_=x_in[g0 + ti * 128:g0 + (ti + 1) * 128, :]), "LD")
                if ti == 0:
                    s.wait("vector", "AC")
                s.wait("vector", "LD")
                s.op("vector", lambda v: v.tensor_add(
                    out=x1t[:], in0=ps_t2[:], in1=xa[:]))
                s.wait("vector", "LD")
                s.op("vector", lambda v: v.tensor_single_scalar(
                    out=qu[:], in_=x1t[:], scalar=0.5, op=ALU.mult))
                tloc = ch * C + ti * 128
                slot, j0 = divmod(tloc, 1024)
                s.wait("sync", "VE")
                s.dma("sync", lambda e, slot=slot, j0=j0: e.dma_start(
                    out=pay_in[sc][slot, j0:j0 + 128, :], in_=qu[:]), "LD")
                s.op("vector", lambda v, ti=ti: v.tensor_copy(
                    out=x1ck[:, ti * DMODEL:(ti + 1) * DMODEL], in_=x1t[:]),
                    sw=False)

            emit_ffn_s3(epilogue)

        def emit_A_ln2(sc, ch):
            s.wait("scalar", "VE")
            emit_ln_stats(x1ck3, sqB3)
            for ti in range(NTILE):
                s.wait("vector", "AC")
                s.op("vector", lambda v, ti=ti: v.tensor_scalar(
                    out=xdt[:], in0=x1ck3[:, ti, :],
                    scalar1=lnm[:, ti:ti + 1], op0=ALU.subtract,
                    scalar2=lnr[:, ti:ti + 1], op1=ALU.mult))
                s.wait("tensor", "VE")
                s.wait("tensor", "AC")
                for kt in range(KD):
                    s.op("tensor", lambda e, kt=kt: e.transpose(
                        out=ps_t[:, kt * 128:(kt + 1) * 128],
                        in_=xdt[:, kt * 128:(kt + 1) * 128], identity=ident[:]))
                s.wait("scalar", "TE")
                s.op("scalar", lambda e, ti=ti: e.activation(
                    out=xn2T[:].rearrange("p (k c) -> p k c", k=KD)
                    [:, :, ti * 128:(ti + 1) * 128],
                    in_=ps_t[:].rearrange("p (k c) -> p k c", k=KD),
                    func=AF.Copy, bias=0.0))

        def emit_phaseB1_chunk(sc, ch):
            s.wait("tensor", "AC")
            s.wait("tensor", "VE")
            s.wait("scalar", "VE")
            s.wait("scalar", "GD")
            s.wait("scalar", "LD")
            ac_xp = [0] * MD
            for mt in range(2 * MD):
                pb = ps_r[mt % 4]
                if mt >= 4:
                    s.wait("tensor", "AC")
                for kt in range(KD):
                    s.op("tensor", lambda e, mt=mt, kt=kt, pb=pb: e.matmul(
                        out=pb[:], lhsT=win_s[:, kt * 2 * DH + mt * 128:
                                              kt * 2 * DH + mt * 128 + 128],
                        rhs=xn2T[:, kt * C:(kt + 1) * C],
                        start=(kt == 0), stop=(kt == KD - 1)))
                s.wait("scalar", "TE")
                if mt < MD:
                    g = mt
                    s.op("scalar", lambda e, g=g, pb=pb: e.activation(
                        out=xproj[:, g * (4 + C) + 4:g * (4 + C) + 4 + C],
                        in_=pb[:], func=AF.Copy, bias=0.0))
                    ac_xp[g] = s.n("AC")
                else:
                    g = mt - MD
                    s.op("scalar", lambda e, g=g, pb=pb: e.activation(
                        out=zs2[:, g * C:(g + 1) * C],
                        in_=pb[:], func=AF.Silu, bias=0.0))
            s.wait("gpsimd", "AC")
            for g in range(MD):
                s.dma("gpsimd", lambda e, g=g: e.dma_start(
                    out=zsil_d[g * 128:(g + 1) * 128,
                               sc * TSC + ch * C:sc * TSC + ch * C + C],
                    in_=zs2[:, g * C:(g + 1) * C]), "GD")
            for g in range(MD):
                par = g % 2
                xp = xproj[:, g * (4 + C):(g + 1) * (4 + C)]
                ca = cacc[:, par * C:(par + 1) * C]
                s.wait("vector", "AC", ac_xp[g])
                if g >= 2:
                    s.wait("vector", "AC")
                s.op("vector", lambda v, xp=xp, ca=ca, g=g: v.tensor_single_scalar(
                    out=ca, in_=xp[:, 1:1 + C],
                    scalar=convw_c[:, 0 * MD + g:0 * MD + g + 1], op=ALU.mult),
                    sw=False)
                for tap in range(1, 4):
                    s.op("vector", lambda v, xp=xp, ca=ca, g=g, tap=tap:
                         v.scalar_tensor_tensor(
                             out=ca, in0=xp[:, 1 + tap:1 + tap + C],
                             scalar=convw_c[:, tap * MD + g:tap * MD + g + 1],
                             in1=ca, op0=ALU.mult, op1=ALU.add))
                s.op("vector", lambda v, xp=xp: v.tensor_copy(
                    out=xp[:, 1:4], in_=xp[:, 1 + C:4 + C]), sw=False)
                s.wait("scalar", "VE")
                s.wait("scalar", "TE")
                s.wait("scalar", "GD")
                s.op("scalar", lambda e, g=g, ca=ca, par=par: e.activation(
                    out=xcb[:, par * C:(par + 1) * C], in_=ca,
                    func=AF.Silu, bias=convb_c[:, g:g + 1]))
                s.wait("tensor", "AC")
                s.op("tensor", lambda e, g=g, par=par: e.matmul(
                    out=ps_x[0:32, :], lhsT=wx_s[:, g * 2 * NST:(g + 1) * 2 * NST],
                    rhs=xcb[:, par * C:(par + 1) * C],
                    start=(g == 0), stop=(g == MD - 1)))
                s.wait("gpsimd", "AC")
                s.dma("gpsimd", lambda e, g=g, par=par: e.dma_start(
                    out=xconv_d[g * 128:(g + 1) * 128,
                                sc * TSC + ch * C:sc * TSC + ch * C + C],
                    in_=xcb[:, par * C:(par + 1) * C]), "GD")
            s.wait("vector", "TE")
            s.op("vector", lambda v: v.tensor_copy(
                out=cacc[0:32, 0:C], in_=ps_x[0:32, :]))
            s.wait("sync", "VE")
            s.dma("sync", lambda e: e.dma_start(
                out=ar_in[sc][:, ch * C:(ch + 1) * C],
                in_=cacc[0:32, 0:C]), "LD")

        def emit_AR(sc):
            s.wait("gpsimd", "LD")
            s.op("gpsimd", lambda g: g.collective_compute(
                "AllReduce", ALU.add,
                replica_groups=[[0, 1], [2, 3], [4, 5], [6, 7]],
                ins=[ar_in[sc].ap().opt()],
                outs=[ar_out[sc].ap().opt()]), "CC")

        def emit_phaseB2_scroll(sc):
            t0 = sc * TSC
            s.wait("gpsimd", "CC")
            s.dma("gpsimd", lambda g: g.dma_start(
                out=dtin16[:, :], in_=ar_out[sc][0:NST, :]), "GD")
            s.dma("gpsimd", lambda g: g.dma_start(
                out=zs2[0:NST, 0:TSC], in_=ar_out[sc][NST:2 * NST, :]), "GD")
            s.wait("sync", "GD")
            s.dma("sync", lambda e: e.dma_start(out=Bd[:, :],
                                                in_=zs2[0:NST, 0:TSC]), "LD")
            s.wait("sync", "VE")
            s.wait("sync", "AC")
            s.dma("sync", lambda e: e.dma_start(
                out=xcg[:, 0:TSC], in_=xconv_d[0:128, t0:t0 + TSC]), "LD")
            s.dma("sync", lambda e: e.dma_start(
                out=zsg[:, 0:TSC], in_=zsil_d[0:128, t0:t0 + TSC]), "LD")
            ld_xc0 = s.n("LD")

            NL = MD * NST
            scn_a = [0] * NL
            te_a = [0] * NL
            pl_a = [0] * NL
            da_a = [0] * NL
            ld_a = [0] * NL
            ac_tail = [0] * NL
            ve_wmul = [0] * MD
            ve_evac = [0] * MD
            ac_evac = [0] * MD
            ld_y3 = [0] * MD
            ld_xc = [0] * MD
            ld_xc[0] = ld_xc0

            def emit_bcast(L):
                if L >= NL:
                    return
                bi = L % 2
                if L >= 2:
                    s.wait("sync", "PL", pl_a[L - 2])
                n = L % NST
                s.dma("sync", lambda e, n=n, bi=bi: e.dma_start(
                    out=Bb2[:, bi * TSC:(bi + 1) * TSC],
                    in_=Bd.ap()[None, n, :].broadcast_to([128, TSC])), "LD")
                ld_a[L] = s.n("LD")

            def emit_evac(g):
                pg = g % 2
                L15 = g * NST + NST - 1
                s.wait("scalar", "TE", te_a[L15])
                if g >= 1:
                    s.wait("scalar", "LD", ld_y3[g - 1])
                for qq in range(4):
                    s.op("scalar", lambda e, qq=qq: e.activation(
                        out=ygb[:, qq * 512:(qq + 1) * 512],
                        in_=ps_y[:, qq * 512:(qq + 1) * 512],
                        func=AF.Copy, bias=0.0), sw=(qq == 0))
                ac_evac[g] = s.n("AC")
                s.wait("vector", "AC", ac_evac[g])
                s.op("vector", lambda v, g=g, pg=pg: v.scalar_tensor_tensor(
                    out=ygb[:], in0=xcg[:, pg * TSC:(pg + 1) * TSC],
                    scalar=dvec_c[:, g:g + 1], in1=ygb[:],
                    op0=ALU.mult, op1=ALU.add), sw=False)
                s.op("vector", lambda v, pg=pg: v.tensor_mul(
                    out=ygb[:], in0=ygb[:], in1=zsg[:, pg * TSC:(pg + 1) * TSC]))
                ve_evac[g] = s.n("VE")
                s.wait("sync", "VE", ve_evac[g])
                s.dma("sync", lambda e, g=g: e.dma_start(
                    out=y3_d[g * 128:(g + 1) * 128, t0:t0 + TSC], in_=ygb[:]),
                    "LD")
                ld_y3[g] = s.n("LD")

            for L in range(NL):
                g, n = divmod(L, NST)
                bi = L % 2
                pg = g % 2
                idx = L
                if n == 0:
                    if g == 0:
                        emit_bcast(0)
                        emit_bcast(1)
                    else:
                        emit_evac(g - 1)
                    s.wait("tensor", "GD")
                    for q in range(4):
                        pb = ps_f[q % 2]
                        par = q % 2
                        qsl = slice(par * C, (par + 1) * C)
                        dsl = slice(pg * TSC + q * C, pg * TSC + (q + 1) * C)
                        s.wait("tensor", "AC")
                        if q >= 2:
                            s.wait("tensor", "VE")
                        s.op("tensor", lambda e, g=g, q=q, pb=pb: e.matmul(
                            out=pb[:], lhsT=wdt_s[:, g * 128:(g + 1) * 128],
                            rhs=dtin16[:, q * C:(q + 1) * C],
                            start=True, stop=True))
                        s.wait("scalar", "TE")
                        if q >= 2:
                            s.wait("scalar", "VE")
                        s.op("scalar", lambda e, g=g, pb=pb, qsl=qsl:
                             e.activation(out=qu[:, qsl], in_=pb[:],
                                          func=AF.Abs,
                                          bias=bdt_c[:, g:g + 1]), sw=False)
                        s.op("scalar", lambda e, qsl=qsl: e.activation(
                            out=qu[:, qsl], in_=qu[:, qsl],
                            func=AF.Exp, bias=0.0, scale=-1.0))
                        s.op("scalar", lambda e, qsl=qsl: e.activation(
                            out=qu[:, qsl], in_=qu[:, qsl],
                            func=AF.Ln, bias=1.0))
                        s.op("scalar", lambda e, g=g, pb=pb, dsl=dsl:
                             e.activation(out=dts[:, dsl], in_=pb[:],
                                          func=AF.Relu,
                                          bias=bdt_c[:, g:g + 1]), sw=False)
                        s.wait("vector", "AC")
                        s.op("vector", lambda v, dsl=dsl, qsl=qsl:
                             v.tensor_tensor(out=dts[:, dsl],
                                             in0=dts[:, dsl],
                                             in1=qu[:, qsl], op=ALU.add),
                             sw=False)
                    if g == 0:
                        s.wait("vector", "TE")
                    s.wait("vector", "LD", ld_xc[g])
                    s.op("vector", lambda v, pg=pg: v.tensor_mul(
                        out=wgb[:], in0=dts[:, pg * TSC:(pg + 1) * TSC],
                        in1=xcg[:, pg * TSC:(pg + 1) * TSC]), sw=True)
                    ve_wmul[g] = s.n("VE")
                    ve_dts = s.n("VE")
                    s.wait("scalar", "VE", ve_dts)
                if L >= 2:
                    s.wait("scalar", "SCN", scn_a[L - 2])
                s.op("scalar", lambda e, pg=pg, bi=bi, idx=idx: e.activation(
                    out=dA2[:, bi * TSC:(bi + 1) * TSC],
                    in_=dts[:, pg * TSC:(pg + 1) * TSC],
                    func=AF.Exp, bias=0.0,
                    scale=acol_c[:, idx:idx + 1]), inc="DA", sw=False)
                da_a[L] = s.n("DA")
                if sc < NSC - 1 and L >= 1:
                    s.wait("scalar", "SCN", scn_a[L - 1])
                    s.op("scalar", lambda e, Lp=L - 1: e.activation(
                        out=state[:, Lp:Lp + 1],
                        in_=ub2[:, ((Lp % 2) + 1) * TSC - 1:
                                ((Lp % 2) + 1) * TSC],
                        func=AF.Copy, bias=0.0), sw=False)
                ac_tail[L - 1] = s.n("AC")
                s.wait("vector", "LD", ld_a[L])
                if L >= 2:
                    s.wait("vector", "TE", te_a[L - 2])
                    s.wait("vector", "AC", ac_tail[L - 2])
                s.op("vector", lambda v, bi=bi: v.tensor_mul(
                    out=ub2[:, bi * TSC:(bi + 1) * TSC], in0=wgb[:],
                    in1=Bb2[:, bi * TSC:(bi + 1) * TSC]), inc="PL", sw=False)
                pl_a[L] = s.n("PL")
                emit_bcast(L + 2)
                if n == 2 and g + 1 < MD:
                    npg = (g + 1) % 2
                    if g >= 1:
                        s.wait("sync", "VE", ve_evac[g - 1])
                    s.dma("sync", lambda e, g=g, npg=npg: e.dma_start(
                        out=xcg[:, npg * TSC:(npg + 1) * TSC],
                        in_=xconv_d[(g + 1) * 128:(g + 2) * 128,
                                    t0:t0 + TSC]), "LD")
                    s.dma("sync", lambda e, g=g, npg=npg: e.dma_start(
                        out=zsg[:, npg * TSC:(npg + 1) * TSC],
                        in_=zsil_d[(g + 1) * 128:(g + 2) * 128,
                                   t0:t0 + TSC]), "LD")
                    ld_xc[g + 1] = s.n("LD")
                s.wait("vector", "DA", da_a[L])
                s.op("vector", lambda v, bi=bi, idx=idx: v.tensor_tensor_scan(
                    out=ub2[:, bi * TSC:(bi + 1) * TSC],
                    data0=dA2[:, bi * TSC:(bi + 1) * TSC],
                    data1=ub2[:, bi * TSC:(bi + 1) * TSC],
                    initial=state[:, idx:idx + 1],
                    op0=ALU.mult, op1=ALU.add), inc="SCN", sw=False)
                scn_a[L] = s.n("SCN")
                s.wait("tensor", "SCN", scn_a[L])
                if n == 0 and g >= 1:
                    s.wait("tensor", "AC", ac_evac[g - 1])
                for qq in range(4):
                    s.op("tensor", lambda e, bi=bi, qq=qq, n=n: e.matmul(
                        out=ps_y[:, qq * 512:(qq + 1) * 512],
                        lhsT=ident[:],
                        rhs=ub2[:, bi * TSC + qq * 512:
                                bi * TSC + (qq + 1) * 512],
                        start=(n == 0), stop=(n == NST - 1)))
                te_a[L] = s.n("TE")
            if sc < NSC - 1:
                s.wait("scalar", "SCN", scn_a[NL - 1])
                s.op("scalar", lambda e: e.activation(
                    out=state[:, NL - 1:NL],
                    in_=ub2[:, (((NL - 1) % 2) + 1) * TSC - 1:
                            (((NL - 1) % 2) + 1) * TSC],
                    func=AF.Copy, bias=0.0), sw=False)
            emit_evac(MD - 1)

            for ti in range(NTILE):
                s.wait("sync", "TE")
                s.wait("sync", "LD")
                s.dma("sync", lambda e, ti=ti: e.dma_start(
                    out=xdT[:].rearrange("p (g c) -> p g c", g=MD),
                    in_=y3_d.ap().rearrange("(g p) t -> p g t", p=128)
                    [:, :, t0 + ti * C:t0 + (ti + 1) * C]), "LD")
                s.wait("tensor", "LD")
                s.wait("tensor", "AC")
                for mt in range(KD):
                    pb = ps_r[mt % 4]
                    if mt >= 4:
                        s.wait("tensor", "AC")
                    for kt in range(MD):
                        s.op("tensor", lambda e, mt=mt, kt=kt, pb=pb: e.matmul(
                            out=pb[:], lhsT=wout_s[:, kt * DMODEL + mt * 128:
                                                   kt * DMODEL + mt * 128 + 128],
                            rhs=xdT[:, kt * C:(kt + 1) * C],
                            start=(kt == 0), stop=(kt == MD - 1)))
                    s.wait("scalar", "TE")
                    s.op("scalar", lambda e, mt=mt, pb=pb: e.activation(
                        out=fout[:, mt * C:(mt + 1) * C], in_=pb[:],
                        func=AF.Copy, bias=0.0), sw=False)
                for tj in range(NTILE):
                    stg = [qsg, qau][tj % 2]
                    s.wait("tensor", "AC")
                    for mt in range(KD):
                        s.op("tensor", lambda e, mt=mt, tj=tj: e.transpose(
                            out=ps_t2[:, mt * 128:(mt + 1) * 128],
                            in_=fout[:, mt * C + tj * 128:
                                     mt * C + tj * 128 + 128],
                            identity=ident[:]))
                    s.wait("scalar", "TE")
                    s.wait("scalar", "GD")
                    s.op("scalar", lambda e, stg=stg: e.activation(
                        out=stg[:], in_=ps_t2[:], func=AF.Copy, bias=0.0),
                        sw=False)
                    tloc = ti * C + tj * 128
                    slot, j0 = divmod(tloc, 1024)
                    s.wait("gpsimd", "AC")
                    s.dma("gpsimd", lambda g, slot=slot, j0=j0, stg=stg:
                          g.dma_start(
                              out=pay_in[sc][slot, j0:j0 + 128, :],
                              in_=stg[:], accum_op=ALU.add), "GD")

        def emit_RS(sc):
            s.wait("gpsimd", "GD")
            s.wait("gpsimd", "LD")
            s.op("gpsimd", lambda g: g.collective_compute(
                "ReduceScatter", ALU.add,
                replica_groups=[[0, 1], [2, 3], [4, 5], [6, 7]],
                ins=[pay_in[sc].ap().opt()],
                outs=[pay_out[sc].ap().opt()]), "CC")

        def emit_C_s1(sc, ch):
            p3 = pay_out[sc].ap().rearrange("(t p) d -> p t d", p=128)
            s.wait("sync", "CC", 2 * sc + 2)
            emit_ffn_s1(p3, ch * NTILE, (sc * 2 + ch) % 2)

        def emit_C_s3(sc, ch):
            def epilogue(ti):
                s.wait("sync", "VE")
                s.dma("sync", lambda e, ti=ti: e.dma_start(
                    out=xa[:],
                    in_=pay_out[sc][ch * C + ti * 128:ch * C + (ti + 1) * 128,
                                    :]), "LD")
                s.wait("vector", "ODMA")
                s.wait("vector", "LD")
                s.op("vector", lambda v: v.tensor_add(
                    out=x1t[:], in0=ps_t2[:], in1=xa[:]), sw=False)
                j0 = sc * 1024 + ch * C + ti * 128
                s.wait("sync", "VE")
                s.dma("sync", lambda e, j0=j0: e.dma_start(
                    out=out_d[j0:j0 + 128, :], in_=x1t[:]), "ODMA")

            emit_ffn_s3(epilogue)

        def emit_prep():
            def ld(out_ap, in_ap):
                s.dma("gpsimd", lambda g, o=out_ap, i=in_ap:
                      g.dma_start(out=o, in_=i), "GD")
            for kt in range(KD):
                ld(win_s[:, kt * 2 * DH:(kt + 1) * 2 * DH],
                   win_in[kt * 128:(kt + 1) * 128, :])
            for kt in range(MD):
                ld(wout_s[:, kt * DMODEL:(kt + 1) * DMODEL],
                   wout_in[kt * 128:(kt + 1) * 128, :])
                ld(wx_s[:, kt * 2 * NST:(kt + 1) * 2 * NST],
                   wx_in[kt * 128:(kt + 1) * 128, :])
            ld(wdt_s[:, :], wdt_in)
            ld(ident[:, :], ident_in)
            for kt in range(MD):
                for tap in range(4):
                    ld(convw_c[:, tap * MD + kt:tap * MD + kt + 1],
                       convw_in[tap, kt * 128:(kt + 1) * 128][:, None])
                ld(convb_c[:, kt:kt + 1], convb_in[kt * 128:(kt + 1) * 128, :])
                ld(bdt_c[:, kt:kt + 1], bdt_in[kt * 128:(kt + 1) * 128, :])
                ld(dvec_c[:, kt:kt + 1], dvec_in[kt * 128:(kt + 1) * 128, :])
                ld(acol_c[:, kt * NST:(kt + 1) * NST],
                   alog_in[kt * 128:(kt + 1) * 128, :])
            for mt in range(KD):
                ld(fbias_c[:, mt:mt + 1], fb_in[mt * 128:(mt + 1) * 128, :])
            s.op("vector", lambda v: v.memset(state[:], 0.0))
            s.op("vector", lambda v: v.memset(xproj[:], 0.0))
            s.wait("scalar", "GD")
            s.op("scalar", lambda e: e.activation(out=acol_c[:], in_=acol_c[:],
                                                  func=AF.Exp, bias=0.0))
            s.wait("vector", "AC")
            s.op("vector", lambda v: v.tensor_single_scalar(
                out=acol_c[:], in_=acol_c[:], scalar=-1.0, op=ALU.mult))
            s.op("vector", lambda v: v.tensor_single_scalar(
                out=fbias_c[:], in_=fbias_c[:], scalar=0.5, op=ALU.mult))
            for mt in range(KD):
                s.dma("sync", lambda e, mt=mt: e.dma_start(
                    out=xa[:], in_=fw_in[mt * 128:(mt + 1) * 128, :]), "LD")
                s.wait("vector", "LD")
                s.wait("scalar", "LD")
                emit_quant(xa, xdt, wscale=True)
                s.wait("tensor", "VE")
                for kt in range(KD):
                    s.op("tensor", lambda e, kt=kt: e.transpose(
                        out=ps_t[:, kt * 128:(kt + 1) * 128],
                        in_=xdt[:, kt * 128:(kt + 1) * 128], identity=ident[:]))
                s.wait("scalar", "TE")
                for kt in range(KD):
                    s.op("scalar", lambda e, kt=kt, mt=mt: e.activation(
                        out=wd_s[:, kt * DMODEL + mt * 128:
                                 kt * DMODEL + mt * 128 + 128],
                        in_=ps_t[:, kt * 128:(kt + 1) * 128],
                        func=AF.Copy, bias=0.0))
                barrier()

        emit_prep()
        for sc in range(NSC):
            for ch in range(NCH):
                emit_A_s1(sc, ch)
                if ch > 0:
                    emit_A_s3(sc, ch - 1)
                    emit_A_ln2(sc, ch - 1)
                emit_ffn_s2(ch % 2)
                if ch > 0:
                    emit_phaseB1_chunk(sc, ch - 1)
            emit_A_s3(sc, NCH - 1)
            emit_A_ln2(sc, NCH - 1)
            emit_phaseB1_chunk(sc, NCH - 1)
            emit_AR(sc)
            emit_phaseB2_scroll(sc)
            emit_RS(sc)
        prevc = None
        for sc in range(NSC):
            for ch in range(2):
                emit_C_s1(sc, ch)
                if prevc is not None:
                    emit_C_s3(*prevc)
                emit_ffn_s2((sc * 2 + ch) % 2)
                prevc = (sc, ch)
        emit_C_s3(*prevc)
        s.wait("gpsimd", "ODMA")

        @block.sync
        def _(e):
            s.replay("sync", e, sems)

        @block.vector
        def _(e):
            s.replay("vector", e, sems)

        @block.scalar
        def _(e):
            s.replay("scalar", e, sems)

        @block.tensor
        def _(e):
            s.replay("tensor", e, sems)

        @block.gpsimd
        def _(e):
            s.replay("gpsimd", e, sems)

    return nc


def kernel(L0_attn_x, L0_attn_W_in, L0_attn_conv_w, L0_attn_conv_b, L0_attn_W_x,
           L0_attn_W_dt, L0_attn_b_dt, L0_attn_A_log, L0_attn_D, L0_attn_W_out,
           L0_ffn_weight, L0_ffn_bias):
    x = np.ascontiguousarray(np.asarray(L0_attn_x, dtype=np.float32))
    W_in = np.ascontiguousarray(np.asarray(L0_attn_W_in, dtype=np.float32))
    conv_w = np.ascontiguousarray(np.asarray(L0_attn_conv_w, dtype=np.float32))
    conv_b = np.ascontiguousarray(np.asarray(L0_attn_conv_b, dtype=np.float32))
    W_x = np.ascontiguousarray(np.asarray(L0_attn_W_x, dtype=np.float32))
    W_dt = np.ascontiguousarray(np.asarray(L0_attn_W_dt, dtype=np.float32))
    b_dt = np.ascontiguousarray(np.asarray(L0_attn_b_dt, dtype=np.float32))
    A_log = np.ascontiguousarray(np.asarray(L0_attn_A_log, dtype=np.float32))
    Dv = np.ascontiguousarray(np.asarray(L0_attn_D, dtype=np.float32))
    W_out = np.ascontiguousarray(np.asarray(L0_attn_W_out, dtype=np.float32))
    fw = np.ascontiguousarray(np.asarray(L0_ffn_weight, dtype=np.float32))
    fb = np.ascontiguousarray(np.asarray(L0_ffn_bias, dtype=np.float32))
    ident = np.eye(128, dtype=ml_dtypes.bfloat16)

    in_maps = []
    for r in range(8):
        b, h = r >> 1, r & 1
        sl = slice(h * DH, (h + 1) * DH)
        in_maps.append({
            "x_b": x[b],
            "win_sl": np.ascontiguousarray(
                np.concatenate([W_in[:, h * DH:(h + 1) * DH],
                                W_in[:, DIN + h * DH:DIN + (h + 1) * DH]],
                               axis=1)),
            "convw_sl": np.ascontiguousarray(conv_w[sl].T),
            "convb_sl": np.ascontiguousarray(conv_b[sl, None]),
            "wx_sl": np.ascontiguousarray(W_x[sl]),
            "wdt_sl": np.ascontiguousarray(W_dt[:, sl]),
            "bdt_sl": np.ascontiguousarray(b_dt[sl, None]),
            "alog_sl": np.ascontiguousarray(A_log[sl]),
            "dvec_sl": np.ascontiguousarray(Dv[sl, None]),
            "wout_sl": np.ascontiguousarray(W_out[sl]),
            "ffnw": fw,
            "ffnb": fb[:, None],
            "ident": ident,
        })

    nc = build_program()
    res = run_bass_kernel_spmd(nc, in_maps, core_ids=list(range(8)))

    out = np.empty((4, T, DMODEL), dtype=np.float32)
    for r in range(8):
        b, h = r >> 1, r & 1
        o = res.results[r]["out_r"]
        for k in range(NSC):
            out[b, k * TSC + h * 1024:k * TSC + (h + 1) * 1024, :] = \
                o[k * 1024:(k + 1) * 1024, :]
    return out


# version 09
# speedup vs baseline: 1.0477x; 1.0463x over previous
import numpy as np
import ml_dtypes
import concourse.bass as bass
import concourse.mybir as mybir
from concourse.bass_utils import run_bass_kernel_spmd

F32 = mybir.dt.float32
BF16 = mybir.dt.bfloat16
FP8 = mybir.dt.float8e4
U8 = mybir.dt.uint8
AF = mybir.ActivationFunctionType
ALU = mybir.AluOpType
AXX = mybir.AxisListType.X

DMODEL = 1024
DIN = 2048
DH = 1024
NST = 16
T = 8192
NSC = 4
TSC = T // NSC
NCH = 4
C = TSC // NCH
NTILE = C // 128
KD = DMODEL // 128
MD = DH // 128
NBQ = DMODEL // 32

ENGS = ("sync", "vector", "scalar", "tensor", "gpsimd")
DEF_SEM = {"vector": "VE", "scalar": "AC", "tensor": "TE"}


class Sched:
    def __init__(self):
        self.q = {e: [] for e in ENGS}
        self.cnt = {}
        self.last = {}

    def op(self, eng, fn, inc="default", k=1, sw=True):
        if inc == "default":
            inc = DEF_SEM.get(eng)
        if sw and eng in ("vector", "scalar"):
            prev = self.last.get(eng)
            if prev is not None:
                self.q[eng].append(["wait", prev[0], prev[1]])
        self.q[eng].append(["op", fn, inc, k])
        if inc is not None:
            self.cnt[inc] = self.cnt.get(inc, 0) + k
            self.last[eng] = (inc, self.cnt[inc])

    def dma(self, eng, fn, inc):
        self.op(eng, fn, inc, 16)

    def nop(self, eng, cycles=512):
        pass

    def wait(self, eng, sem, val=None):
        if val is None:
            val = self.cnt.get(sem, 0)
        self.q[eng].append(["wait", sem, val])

    def n(self, sem):
        return self.cnt.get(sem, 0)

    def replay(self, eng_name, eng, sems):
        for item in self.q[eng_name]:
            if item[0] == "op":
                _, fn, inc, k = item
                inst = fn(eng)
                if inc is not None:
                    inst.then_inc(sems[inc], k)
            else:
                _, sem, val = item
                if val > 0:
                    eng.wait_ge(sems[sem], val)


def build_program():
    nc = bass.Bass("TRN2", target_bir_lowering=False)

    x_in = nc.dram_tensor("x_b", [T, DMODEL], F32, kind="ExternalInput").ap()
    win_in = nc.dram_tensor("win_sl", [DMODEL, 2 * DH], F32, kind="ExternalInput").ap()
    convw_in = nc.dram_tensor("convw_sl", [4, DH], F32, kind="ExternalInput").ap()
    convb_in = nc.dram_tensor("convb_sl", [DH, 1], F32, kind="ExternalInput").ap()
    wx_in = nc.dram_tensor("wx_sl", [DH, 2 * NST], F32, kind="ExternalInput").ap()
    wdt_in = nc.dram_tensor("wdt_sl", [NST, DH], F32, kind="ExternalInput").ap()
    bdt_in = nc.dram_tensor("bdt_sl", [DH, 1], F32, kind="ExternalInput").ap()
    alog_in = nc.dram_tensor("alog_sl", [DH, NST], F32, kind="ExternalInput").ap()
    dvec_in = nc.dram_tensor("dvec_sl", [DH, 1], F32, kind="ExternalInput").ap()
    wout_in = nc.dram_tensor("wout_sl", [DH, DMODEL], F32, kind="ExternalInput").ap()
    fw_in = nc.dram_tensor("ffnw", [DMODEL, DMODEL], F32, kind="ExternalInput").ap()
    fb_in = nc.dram_tensor("ffnb", [DMODEL, 1], F32, kind="ExternalInput").ap()
    ident_in = nc.dram_tensor("ident", [128, 128], BF16, kind="ExternalInput").ap()
    out_d = nc.dram_tensor("out_r", [NSC * 1024, DMODEL], F32,
                           kind="ExternalOutput").ap()

    pay_in = [nc.dram_tensor(f"pay_in{k}", [2, 1024, DMODEL], F32, kind="Internal")
              for k in range(NSC)]
    pay_out = [nc.dram_tensor(f"pay_out{k}", [1024, DMODEL], F32, kind="Internal")
               for k in range(NSC)]
    ar_in = [nc.dram_tensor(f"ar_in{k}", [2 * NST, TSC], F32, kind="Internal")
             for k in range(NSC)]
    ar_out = [nc.dram_tensor(f"ar_out{k}", [2 * NST, TSC], F32, kind="Internal")
              for k in range(NSC)]
    xconv_d = nc.dram_tensor("xconv_d", [DH, T], BF16, kind="Internal")
    zsil_d = nc.dram_tensor("zsil_d", [DH, T], BF16, kind="Internal")
    y3_d = nc.dram_tensor("y3_d", [DH, T], BF16, kind="Internal")
    Bd = nc.dram_tensor("Bd", [NST, TSC], BF16, kind="Internal")

    s = Sched()

    from contextlib import ExitStack
    with ExitStack() as st:
        win_s = st.enter_context(nc.sbuf_tensor("win_s", [128, KD * 2 * DH], BF16))
        wout_s = st.enter_context(nc.sbuf_tensor("wout_s", [128, MD * DMODEL], BF16))
        wd_s = st.enter_context(nc.sbuf_tensor("wd_s", [128, KD * DMODEL], BF16))
        wx_s = st.enter_context(nc.sbuf_tensor("wx_s", [128, MD * 2 * NST], BF16))
        wdt_s = st.enter_context(nc.sbuf_tensor("wdt_s", [NST, DH], BF16))
        ident = st.enter_context(nc.sbuf_tensor("ident_s", [128, 128], BF16))
        convw_c = st.enter_context(nc.sbuf_tensor("convw_c", [128, 4 * MD], F32))
        convb_c = st.enter_context(nc.sbuf_tensor("convb_c", [128, MD], F32))
        bdt_c = st.enter_context(nc.sbuf_tensor("bdt_c", [128, MD], F32))
        dvec_c = st.enter_context(nc.sbuf_tensor("dvec_c", [128, MD], F32))
        acol_c = st.enter_context(nc.sbuf_tensor("acol_c", [128, MD * NST], F32))
        fbias_c = st.enter_context(nc.sbuf_tensor("fbias_c", [128, KD], F32))
        qsc = st.enter_context(nc.sbuf_tensor("qsc", [128, NBQ], F32))
        qrcp = st.enter_context(nc.sbuf_tensor("qrcp", [128, NBQ], F32))
        qu = st.enter_context(nc.sbuf_tensor("qu", [128, DMODEL], F32))
        qau = st.enter_context(nc.sbuf_tensor("qau", [128, DMODEL], F32))
        qsg = st.enter_context(nc.sbuf_tensor("qsg", [128, DMODEL], F32))
        qb8 = st.enter_context(nc.sbuf_tensor("qb8", [128, DMODEL], FP8))
        qc8 = st.enter_context(nc.sbuf_tensor("qc8", [128, DMODEL], FP8))
        qm = st.enter_context(nc.sbuf_tensor("qm", [128, DMODEL], U8))
        xa = st.enter_context(nc.sbuf_tensor("xa", [128, DMODEL], F32))
        x1t = st.enter_context(nc.sbuf_tensor("x1t", [128, DMODEL], F32))
        xdt = st.enter_context(nc.sbuf_tensor("xdt", [128, DMODEL], BF16))
        lns1 = st.enter_context(nc.sbuf_tensor("lns1", [128, NTILE], F32))
        lns2 = st.enter_context(nc.sbuf_tensor("lns2", [128, NTILE], F32))
        lnm = st.enter_context(nc.sbuf_tensor("lnm", [128, NTILE], F32))
        lnv = st.enter_context(nc.sbuf_tensor("lnv", [128, NTILE], F32))
        lnr = st.enter_context(nc.sbuf_tensor("lnr", [128, NTILE], F32))
        xdT = st.enter_context(nc.sbuf_tensor("xdT", [128, KD * C], BF16))
        fout = st.enter_context(nc.sbuf_tensor("fout", [128, KD * C], BF16))
        xn2T = st.enter_context(nc.sbuf_tensor("xn2T", [128, KD * C], BF16))
        xproj = st.enter_context(nc.sbuf_tensor("xproj", [128, MD * (4 + C)], BF16))
        zs2 = st.enter_context(nc.sbuf_tensor("zs2", [128, MD * C], BF16))
        dtin16 = st.enter_context(nc.sbuf_tensor("dtin16", [NST, TSC], BF16))
        cacc = st.enter_context(nc.sbuf_tensor("cacc", [128, 2 * C], F32))
        xcb = st.enter_context(nc.sbuf_tensor("xcb", [128, 2 * C], BF16))
        xcg = st.enter_context(nc.sbuf_tensor("xcg", [128, 2 * TSC], BF16))
        zsg = st.enter_context(nc.sbuf_tensor("zsg", [128, 2 * TSC], BF16))
        dts = st.enter_context(nc.sbuf_tensor("dts", [128, 2 * TSC], BF16))
        wgb = st.enter_context(nc.sbuf_tensor("wgb", [128, TSC], BF16))
        dA2 = st.enter_context(nc.sbuf_tensor("dA2", [128, 2 * TSC], F32))
        ub2 = st.enter_context(nc.sbuf_tensor("ub2", [128, 2 * TSC], BF16))
        Bb2 = st.enter_context(nc.sbuf_tensor("Bb2", [128, 2 * TSC], BF16))
        ygb = st.enter_context(nc.sbuf_tensor("ygb", [128, TSC], BF16))
        state = st.enter_context(nc.sbuf_tensor("state", [128, MD * NST], F32))
        ps_t = st.enter_context(nc.psum_tensor("ps_t", [128, 1024], BF16))
        ps_t2 = ps_t
        ps_f0 = st.enter_context(nc.psum_tensor("ps_f0", [128, 512], F32))
        ps_f1 = st.enter_context(nc.psum_tensor("ps_f1", [128, 512], F32))
        ps_dt = st.enter_context(nc.psum_tensor("ps_dt", [128, 512], F32))
        ps_x = ps_dt
        ps_y = st.enter_context(nc.psum_tensor("ps_y", [128, TSC], F32))
        sem_LD = st.enter_context(nc.semaphore("LD"))
        sem_GD = st.enter_context(nc.semaphore("GD"))
        sem_VE = st.enter_context(nc.semaphore("VE"))
        sem_AC = st.enter_context(nc.semaphore("AC"))
        sem_TE = st.enter_context(nc.semaphore("TE"))
        sem_CC = st.enter_context(nc.semaphore("CC"))
        sem_DA = st.enter_context(nc.semaphore("DA"))
        sem_SCN = st.enter_context(nc.semaphore("SCN"))
        sem_ODMA = st.enter_context(nc.semaphore("ODMA"))
        sem_PL = st.enter_context(nc.semaphore("PL"))
        block = st.enter_context(nc.Block())
        xh = qsg
        payt = qau
        sems = {"LD": sem_LD, "GD": sem_GD, "VE": sem_VE, "AC": sem_AC,
                "TE": sem_TE, "CC": sem_CC, "DA": sem_DA, "SCN": sem_SCN,
                "ODMA": sem_ODMA, "PL": sem_PL}
        ps_f = [ps_f0, ps_f1]
        ps_r = [ps_f0, ps_f1, ps_y[:, 0:512], ps_y[:, 512:1024]]

        def barrier():
            for e in ENGS:
                for sname in ("LD", "GD", "VE", "AC", "TE", "CC", "DA",
                              "SCN", "PL"):
                    s.wait(e, sname)

        def emit_quant(src, dst, wscale=False):
            v3 = src[:].rearrange("p (s n) -> p s n", n=32)
            s.op("vector", lambda v: v.tensor_reduce(
                out=qsc[:], in_=v3, axis=AXX, op=ALU.max,
                apply_absolute_value=True))
            sc2 = (1.0 / 7.5) * (0.5 if wscale else 1.0)
            s.op("vector", lambda v: v.tensor_scalar(
                out=qsc[:], in0=qsc[:], scalar1=sc2, op0=ALU.mult,
                scalar2=(0.5e-30 if wscale else 1e-30), op1=ALU.max))
            s.op("vector", lambda v: v.reciprocal(out=qrcp[:], in_=qsc[:]))
            if wscale:
                s.op("vector", lambda v: v.tensor_single_scalar(
                    out=qrcp[:], in_=qrcp[:], scalar=0.5, op=ALU.mult))
            s.op("vector", lambda v: v.tensor_tensor(
                out=qu[:].rearrange("p (s n) -> p s n", n=32), in0=v3,
                in1=qrcp[:].broadcast_to([128, NBQ, 32]), op=ALU.mult))
            s.wait("scalar", "VE")
            s.wait("scalar", "LD")
            s.op("scalar", lambda e: e.activation(out=qau[:], in_=qu[:],
                                                  func=AF.Abs, bias=0.0),
                 sw=False)
            s.op("scalar", lambda e: e.activation(out=qsg[:], in_=qu[:],
                                                  func=AF.Sign, bias=0.0),
                 sw=False)
            s.op("scalar", lambda e: e.activation(out=qb8[:], in_=qau[:],
                                                  func=AF.Copy, bias=0.0),
                 sw=False)
            s.op("scalar", lambda e: e.activation(out=qc8[:], in_=qau[:],
                                                  func=AF.Copy, bias=1.0),
                 sw=False)
            s.op("scalar", lambda e: e.activation(out=qu[:], in_=qc8[:],
                                                  func=AF.Copy, bias=-1.0),
                 sw=False)
            s.wait("vector", "AC")
            s.op("vector", lambda v: v.tensor_single_scalar(
                out=qm[:], in_=qau[:], scalar=1.0, op=ALU.is_lt), sw=False)
            s.op("vector", lambda v: v.tensor_copy(out=qau[:], in_=qb8[:]),
                 sw=False)
            s.op("vector", lambda v: v.copy_predicated(out=qau[:], mask=qm[:],
                                                       data=qu[:]), sw=False)
            s.op("vector", lambda v: v.tensor_mul(out=qsg[:], in0=qau[:],
                                                  in1=qsg[:]), sw=False)
            s.op("vector", lambda v: v.tensor_tensor(
                out=dst[:].rearrange("p (s n) -> p s n", n=32),
                in0=qsg[:].rearrange("p (s n) -> p s n", n=32),
                in1=qsc[:].broadcast_to([128, NBQ, 32]), op=ALU.mult),
                sw=False)

        def emit_ln_stats(src3, sq3):
            s.op("vector", lambda v: v.tensor_reduce(
                out=lns1[:, 0:NTILE], in_=src3, axis=AXX, op=ALU.add))
            s.op("scalar", lambda e: e.activation(
                out=sq3, in_=src3, func=AF.Square, bias=0.0))
            s.wait("vector", "AC")
            s.op("vector", lambda v: v.tensor_reduce(
                out=lns2[:, 0:NTILE], in_=sq3, axis=AXX, op=ALU.add),
                sw=False)
            s.op("vector", lambda v: v.tensor_single_scalar(
                out=lnm[:, 0:NTILE], in_=lns1[:, 0:NTILE],
                scalar=1.0 / DMODEL, op=ALU.mult))
            s.op("vector", lambda v: v.tensor_single_scalar(
                out=lnv[:, 0:NTILE], in_=lns2[:, 0:NTILE],
                scalar=1.0 / DMODEL, op=ALU.mult))
            s.op("vector", lambda v: v.tensor_mul(
                out=lns1[:, 0:NTILE], in0=lnm[:, 0:NTILE],
                in1=lnm[:, 0:NTILE]))
            s.op("vector", lambda v: v.tensor_sub(
                out=lnv[:, 0:NTILE], in0=lnv[:, 0:NTILE],
                in1=lns1[:, 0:NTILE]))
            s.op("vector", lambda v: v.tensor_single_scalar(
                out=lnv[:, 0:NTILE], in_=lnv[:, 0:NTILE], scalar=1e-5,
                op=ALU.add))
            s.wait("scalar", "VE")
            s.op("scalar", lambda e: e.activation(
                out=lnv[:, 0:NTILE], in_=lnv[:, 0:NTILE], func=AF.Ln,
                bias=0.0))
            s.wait("vector", "AC")
            s.op("vector", lambda v: v.tensor_single_scalar(
                out=lnv[:, 0:NTILE], in_=lnv[:, 0:NTILE], scalar=-0.5,
                op=ALU.mult))
            s.wait("scalar", "VE")
            s.op("scalar", lambda e: e.activation(
                out=lnr[:, 0:NTILE], in_=lnv[:, 0:NTILE], func=AF.Exp,
                bias=0.0))

        xck = dA2
        xck3 = dA2[:].rearrange("p (t d) -> p t d", d=DMODEL)
        sqA3 = Bb2[:].rearrange("p (t d) -> p t d", d=DMODEL)
        sqB3 = xcg[:].rearrange("p (t d) -> p t d", d=DMODEL)
        x1ck = ub2
        x1ck3 = ub2[:].rearrange("p (t d) -> p t d", d=DMODEL)

        def xdT_r(par, kt):
            if par == 0:
                return xdT[:, kt * C:(kt + 1) * C]
            if kt < 4:
                return wgb[:, kt * C:(kt + 1) * C]
            return ygb[:, (kt - 4) * C:(kt - 3) * C]

        def xdT_w(par, ti):
            if par == 0:
                s.op("scalar", lambda e, ti=ti: e.activation(
                    out=xdT[:].rearrange("p (k c) -> p k c", k=KD)
                    [:, :, ti * 128:(ti + 1) * 128],
                    in_=ps_t[:].rearrange("p (k c) -> p k c", k=KD),
                    func=AF.Copy, bias=0.0))
            else:
                s.op("scalar", lambda e, ti=ti: e.activation(
                    out=wgb[:].rearrange("p (k c) -> p k c", k=4)
                    [:, :, ti * 128:(ti + 1) * 128],
                    in_=ps_t[:, 0:512].rearrange("p (k c) -> p k c", k=4),
                    func=AF.Copy, bias=0.0))
                s.op("scalar", lambda e, ti=ti: e.activation(
                    out=ygb[:].rearrange("p (k c) -> p k c", k=4)
                    [:, :, ti * 128:(ti + 1) * 128],
                    in_=ps_t[:, 512:1024].rearrange("p (k c) -> p k c", k=4),
                    func=AF.Copy, bias=0.0), sw=False)

        xck_free = [0, 0]

        def emit_ffn_s1(src_dram3, idx, par):
            s.wait("sync", "VE", xck_free[0])
            s.wait("sync", "AC", xck_free[1])
            s.dma("sync", lambda e: e.dma_start(
                out=xck3, in_=src_dram3[:, idx:idx + NTILE, :]), "LD")
            s.wait("vector", "LD")
            s.wait("scalar", "LD")
            s.wait("vector", "GD")
            emit_ln_stats(xck3, sqA3)
            xck_free[1] = s.n("AC")
            for ti in range(NTILE):
                if ti == 0:
                    s.wait("vector", "AC")
                s.op("vector", lambda v, ti=ti: v.tensor_scalar(
                    out=xh[:], in0=xck3[:, ti, :], scalar1=lnm[:, ti:ti + 1],
                    op0=ALU.subtract, scalar2=lnr[:, ti:ti + 1],
                    op1=ALU.mult))
                if ti == NTILE - 1:
                    xck_free[0] = s.n("VE")
                emit_quant(xh, xdt)
                s.wait("tensor", "VE")
                s.wait("tensor", "AC")
                for kt in range(KD):
                    s.op("tensor", lambda e, kt=kt: e.transpose(
                        out=ps_t[:, kt * 128:(kt + 1) * 128],
                        in_=xdt[:, kt * 128:(kt + 1) * 128], identity=ident[:]))
                s.wait("scalar", "TE")
                if par == 1:
                    s.wait("scalar", "VE")
                    s.wait("scalar", "LD")
                xdT_w(par, ti)

        def emit_ffn_s2(par):
            s.wait("tensor", "AC")
            for mt in range(KD):
                pb = ps_r[mt % 4]
                if mt >= 4:
                    s.wait("tensor", "AC")
                for kt in range(KD):
                    s.op("tensor", lambda e, mt=mt, kt=kt, pb=pb: e.matmul(
                        out=pb[:], lhsT=wd_s[:, kt * DMODEL + mt * 128:
                                             kt * DMODEL + mt * 128 + 128],
                        rhs=xdT_r(par, kt),
                        start=(kt == 0), stop=(kt == KD - 1)))
                s.wait("scalar", "TE")
                s.op("scalar", lambda e, mt=mt, pb=pb: e.activation(
                    out=fout[:, mt * C:(mt + 1) * C], in_=pb[:],
                    func=AF.Identity, bias=fbias_c[:, mt:mt + 1]), sw=False)

        def emit_ffn_s3(epilogue):
            for ti in range(NTILE):
                s.wait("tensor", "AC")
                s.wait("tensor", "VE")
                for mt in range(KD):
                    s.op("tensor", lambda e, mt=mt, ti=ti: e.transpose(
                        out=ps_t2[:, mt * 128:(mt + 1) * 128],
                        in_=fout[:, mt * C + ti * 128:mt * C + ti * 128 + 128],
                        identity=ident[:]))
                s.wait("vector", "TE")
                epilogue(ti)

        x3src = x_in.rearrange("(t p) d -> p t d", p=128)

        def emit_A_s1(sc, ch):
            emit_ffn_s1(x3src, (sc * TSC + ch * C) // 128, ch % 2)

        def emit_A_s3(sc, ch):
            g0 = sc * TSC + ch * C

            def epilogue(ti):
                s.wait("sync", "VE")
                s.dma("sync", lambda e, ti=ti: e.dma_start(
                    out=xa[:],
                    in_=x_in[g0 + ti * 128:g0 + (ti + 1) * 128, :]), "LD")
                if ti == 0:
                    s.wait("vector", "AC")
                s.wait("vector", "LD")
                s.op("vector", lambda v: v.tensor_add(
                    out=x1t[:], in0=ps_t2[:], in1=xa[:]))
                s.wait("vector", "LD")
                s.op("vector", lambda v: v.tensor_single_scalar(
                    out=qu[:], in_=x1t[:], scalar=0.5, op=ALU.mult))
                tloc = ch * C + ti * 128
                slot, j0 = divmod(tloc, 1024)
                s.wait("sync", "VE")
                s.dma("sync", lambda e, slot=slot, j0=j0: e.dma_start(
                    out=pay_in[sc][slot, j0:j0 + 128, :], in_=qu[:]), "LD")
                s.op("vector", lambda v, ti=ti: v.tensor_copy(
                    out=x1ck[:, ti * DMODEL:(ti + 1) * DMODEL], in_=x1t[:]),
                    sw=False)

            emit_ffn_s3(epilogue)

        def emit_A_ln2(sc, ch):
            s.wait("scalar", "VE")
            emit_ln_stats(x1ck3, sqB3)
            for ti in range(NTILE):
                s.wait("vector", "AC")
                s.op("vector", lambda v, ti=ti: v.tensor_scalar(
                    out=xdt[:], in0=x1ck3[:, ti, :],
                    scalar1=lnm[:, ti:ti + 1], op0=ALU.subtract,
                    scalar2=lnr[:, ti:ti + 1], op1=ALU.mult))
                s.wait("tensor", "VE")
                s.wait("tensor", "AC")
                for kt in range(KD):
                    s.op("tensor", lambda e, kt=kt: e.transpose(
                        out=ps_t[:, kt * 128:(kt + 1) * 128],
                        in_=xdt[:, kt * 128:(kt + 1) * 128], identity=ident[:]))
                s.wait("scalar", "TE")
                s.op("scalar", lambda e, ti=ti: e.activation(
                    out=xn2T[:].rearrange("p (k c) -> p k c", k=KD)
                    [:, :, ti * 128:(ti + 1) * 128],
                    in_=ps_t[:].rearrange("p (k c) -> p k c", k=KD),
                    func=AF.Copy, bias=0.0))

        def emit_phaseB1_chunk(sc, ch):
            s.wait("tensor", "AC")
            s.wait("tensor", "VE")
            s.wait("scalar", "VE")
            s.wait("scalar", "GD")
            s.wait("scalar", "LD")
            ac_xp = [0] * MD
            for mt in range(2 * MD):
                pb = ps_r[mt % 4]
                if mt >= 4:
                    s.wait("tensor", "AC")
                for kt in range(KD):
                    s.op("tensor", lambda e, mt=mt, kt=kt, pb=pb: e.matmul(
                        out=pb[:], lhsT=win_s[:, kt * 2 * DH + mt * 128:
                                              kt * 2 * DH + mt * 128 + 128],
                        rhs=xn2T[:, kt * C:(kt + 1) * C],
                        start=(kt == 0), stop=(kt == KD - 1)))
                s.wait("scalar", "TE")
                if mt < MD:
                    g = mt
                    s.op("scalar", lambda e, g=g, pb=pb: e.activation(
                        out=xproj[:, g * (4 + C) + 4:g * (4 + C) + 4 + C],
                        in_=pb[:], func=AF.Copy, bias=0.0))
                    ac_xp[g] = s.n("AC")
                else:
                    g = mt - MD
                    s.op("scalar", lambda e, g=g, pb=pb: e.activation(
                        out=zs2[:, g * C:(g + 1) * C],
                        in_=pb[:], func=AF.Silu, bias=0.0))
            s.wait("gpsimd", "AC")
            for g in range(MD):
                s.dma("gpsimd", lambda e, g=g: e.dma_start(
                    out=zsil_d[g * 128:(g + 1) * 128,
                               sc * TSC + ch * C:sc * TSC + ch * C + C],
                    in_=zs2[:, g * C:(g + 1) * C]), "GD")
            for g in range(MD):
                par = g % 2
                xp = xproj[:, g * (4 + C):(g + 1) * (4 + C)]
                ca = cacc[:, par * C:(par + 1) * C]
                s.wait("vector", "AC", ac_xp[g])
                if g >= 2:
                    s.wait("vector", "AC")
                s.op("vector", lambda v, xp=xp, ca=ca, g=g: v.tensor_single_scalar(
                    out=ca, in_=xp[:, 1:1 + C],
                    scalar=convw_c[:, 0 * MD + g:0 * MD + g + 1], op=ALU.mult),
                    sw=False)
                for tap in range(1, 4):
                    s.op("vector", lambda v, xp=xp, ca=ca, g=g, tap=tap:
                         v.scalar_tensor_tensor(
                             out=ca, in0=xp[:, 1 + tap:1 + tap + C],
                             scalar=convw_c[:, tap * MD + g:tap * MD + g + 1],
                             in1=ca, op0=ALU.mult, op1=ALU.add))
                s.op("vector", lambda v, xp=xp: v.tensor_copy(
                    out=xp[:, 1:4], in_=xp[:, 1 + C:4 + C]), sw=False)
                s.wait("scalar", "VE")
                s.wait("scalar", "TE")
                s.wait("scalar", "GD")
                s.op("scalar", lambda e, g=g, ca=ca, par=par: e.activation(
                    out=xcb[:, par * C:(par + 1) * C], in_=ca,
                    func=AF.Silu, bias=convb_c[:, g:g + 1]))
                s.wait("tensor", "AC")
                s.op("tensor", lambda e, g=g, par=par: e.matmul(
                    out=ps_x[0:32, :], lhsT=wx_s[:, g * 2 * NST:(g + 1) * 2 * NST],
                    rhs=xcb[:, par * C:(par + 1) * C],
                    start=(g == 0), stop=(g == MD - 1)))
                s.wait("gpsimd", "AC")
                s.dma("gpsimd", lambda e, g=g, par=par: e.dma_start(
                    out=xconv_d[g * 128:(g + 1) * 128,
                                sc * TSC + ch * C:sc * TSC + ch * C + C],
                    in_=xcb[:, par * C:(par + 1) * C]), "GD")
            s.wait("vector", "TE")
            s.op("vector", lambda v: v.tensor_copy(
                out=cacc[0:32, 0:C], in_=ps_x[0:32, :]))
            s.wait("sync", "VE")
            s.dma("sync", lambda e: e.dma_start(
                out=ar_in[sc][:, ch * C:(ch + 1) * C],
                in_=cacc[0:32, 0:C]), "LD")

        def emit_AR(sc):
            s.wait("gpsimd", "LD")
            s.op("gpsimd", lambda g: g.collective_compute(
                "AllReduce", ALU.add,
                replica_groups=[[0, 1], [2, 3], [4, 5], [6, 7]],
                ins=[ar_in[sc].ap().opt()],
                outs=[ar_out[sc].ap().opt()]), "CC")

        def emit_phaseB2_scroll(sc):
            t0 = sc * TSC
            s.wait("gpsimd", "CC")
            s.dma("gpsimd", lambda g: g.dma_start(
                out=dtin16[:, :], in_=ar_out[sc][0:NST, :]), "GD")
            s.dma("gpsimd", lambda g: g.dma_start(
                out=zs2[0:NST, 0:TSC], in_=ar_out[sc][NST:2 * NST, :]), "GD")
            s.wait("sync", "GD")
            s.dma("sync", lambda e: e.dma_start(out=Bd[:, :],
                                                in_=zs2[0:NST, 0:TSC]), "LD")
            s.wait("sync", "VE")
            s.wait("sync", "AC")
            s.dma("sync", lambda e: e.dma_start(
                out=xcg[:, 0:TSC], in_=xconv_d[0:128, t0:t0 + TSC]), "LD")
            s.dma("sync", lambda e: e.dma_start(
                out=zsg[:, 0:TSC], in_=zsil_d[0:128, t0:t0 + TSC]), "LD")
            ld_xc0 = s.n("LD")

            NL = MD * NST
            scn_a = [0] * NL
            te_a = [0] * NL
            pl_a = [0] * NL
            da_a = [0] * NL
            ld_a = [0] * NL
            ac_tail = [0] * NL
            ve_wmul = [0] * MD
            ve_evac = [0] * MD
            ac_evac = [0] * MD
            ld_y3 = [0] * MD
            ld_xc = [0] * MD
            ld_xc[0] = ld_xc0

            def emit_bcast(L):
                if L >= NL:
                    return
                bi = L % 2
                if L >= 2:
                    s.wait("sync", "PL", pl_a[L - 2])
                n = L % NST
                s.dma("sync", lambda e, n=n, bi=bi: e.dma_start(
                    out=Bb2[:, bi * TSC:(bi + 1) * TSC],
                    in_=Bd.ap()[None, n, :].broadcast_to([128, TSC])), "LD")
                ld_a[L] = s.n("LD")

            def emit_evac(g):
                pg = g % 2
                L15 = g * NST + NST - 1
                s.wait("scalar", "TE", te_a[L15])
                if g >= 1:
                    s.wait("scalar", "LD", ld_y3[g - 1])
                for qq in range(4):
                    s.op("scalar", lambda e, qq=qq: e.activation(
                        out=ygb[:, qq * 512:(qq + 1) * 512],
                        in_=ps_y[:, qq * 512:(qq + 1) * 512],
                        func=AF.Copy, bias=0.0), sw=(qq == 0))
                ac_evac[g] = s.n("AC")
                s.wait("vector", "AC", ac_evac[g])
                s.op("vector", lambda v, g=g, pg=pg: v.scalar_tensor_tensor(
                    out=ygb[:], in0=xcg[:, pg * TSC:(pg + 1) * TSC],
                    scalar=dvec_c[:, g:g + 1], in1=ygb[:],
                    op0=ALU.mult, op1=ALU.add), sw=False)
                s.op("vector", lambda v, pg=pg: v.tensor_mul(
                    out=ygb[:], in0=ygb[:], in1=zsg[:, pg * TSC:(pg + 1) * TSC]))
                ve_evac[g] = s.n("VE")
                s.wait("sync", "VE", ve_evac[g])
                s.dma("sync", lambda e, g=g: e.dma_start(
                    out=y3_d[g * 128:(g + 1) * 128, t0:t0 + TSC], in_=ygb[:]),
                    "LD")
                ld_y3[g] = s.n("LD")

            for L in range(NL):
                g, n = divmod(L, NST)
                bi = L % 2
                pg = g % 2
                idx = L
                if n == 0:
                    if g == 0:
                        emit_bcast(0)
                        emit_bcast(1)
                    else:
                        emit_evac(g - 1)
                    s.wait("tensor", "GD")
                    for q in range(4):
                        pb = ps_f[q % 2]
                        par = q % 2
                        qsl = slice(par * C, (par + 1) * C)
                        dsl = slice(pg * TSC + q * C, pg * TSC + (q + 1) * C)
                        s.wait("tensor", "AC")
                        if q >= 2:
                            s.wait("tensor", "VE")
                        s.op("tensor", lambda e, g=g, q=q, pb=pb: e.matmul(
                            out=pb[:], lhsT=wdt_s[:, g * 128:(g + 1) * 128],
                            rhs=dtin16[:, q * C:(q + 1) * C],
                            start=True, stop=True))
                        s.wait("scalar", "TE")
                        if q >= 2:
                            s.wait("scalar", "VE")
                        s.op("scalar", lambda e, g=g, pb=pb, qsl=qsl:
                             e.activation(out=qu[:, qsl], in_=pb[:],
                                          func=AF.Abs,
                                          bias=bdt_c[:, g:g + 1]), sw=False)
                        s.op("scalar", lambda e, qsl=qsl: e.activation(
                            out=qu[:, qsl], in_=qu[:, qsl],
                            func=AF.Exp, bias=0.0, scale=-1.0))
                        s.op("scalar", lambda e, qsl=qsl: e.activation(
                            out=qu[:, qsl], in_=qu[:, qsl],
                            func=AF.Ln, bias=1.0))
                        s.op("scalar", lambda e, g=g, pb=pb, dsl=dsl:
                             e.activation(out=dts[:, dsl], in_=pb[:],
                                          func=AF.Relu,
                                          bias=bdt_c[:, g:g + 1]), sw=False)
                        s.wait("vector", "AC")
                        s.op("vector", lambda v, dsl=dsl, qsl=qsl:
                             v.tensor_tensor(out=dts[:, dsl],
                                             in0=dts[:, dsl],
                                             in1=qu[:, qsl], op=ALU.add),
                             sw=False)
                    if g == 0:
                        s.wait("vector", "TE")
                    s.wait("vector", "LD", ld_xc[g])
                    s.op("vector", lambda v, pg=pg: v.tensor_mul(
                        out=wgb[:], in0=dts[:, pg * TSC:(pg + 1) * TSC],
                        in1=xcg[:, pg * TSC:(pg + 1) * TSC]), sw=True)
                    ve_wmul[g] = s.n("VE")
                    ve_dts = s.n("VE")
                    s.wait("scalar", "VE", ve_dts)
                if L >= 2:
                    s.wait("scalar", "SCN", scn_a[L - 2])
                s.op("scalar", lambda e, pg=pg, bi=bi, idx=idx: e.activation(
                    out=dA2[:, bi * TSC:(bi + 1) * TSC],
                    in_=dts[:, pg * TSC:(pg + 1) * TSC],
                    func=AF.Exp, bias=0.0,
                    scale=acol_c[:, idx:idx + 1]), inc="DA", sw=False)
                da_a[L] = s.n("DA")
                if sc < NSC - 1 and L >= 1:
                    s.wait("scalar", "SCN", scn_a[L - 1])
                    s.op("scalar", lambda e, Lp=L - 1: e.activation(
                        out=state[:, Lp:Lp + 1],
                        in_=ub2[:, ((Lp % 2) + 1) * TSC - 1:
                                ((Lp % 2) + 1) * TSC],
                        func=AF.Copy, bias=0.0), sw=False)
                ac_tail[L - 1] = s.n("AC")
                s.wait("vector", "LD", ld_a[L])
                if L >= 2:
                    s.wait("vector", "TE", te_a[L - 2])
                    s.wait("vector", "AC", ac_tail[L - 2])
                s.op("vector", lambda v, bi=bi: v.tensor_mul(
                    out=ub2[:, bi * TSC:(bi + 1) * TSC], in0=wgb[:],
                    in1=Bb2[:, bi * TSC:(bi + 1) * TSC]), inc="PL", sw=False)
                pl_a[L] = s.n("PL")
                emit_bcast(L + 2)
                if n == 2 and g + 1 < MD:
                    npg = (g + 1) % 2
                    if g >= 1:
                        s.wait("sync", "VE", ve_evac[g - 1])
                    s.dma("sync", lambda e, g=g, npg=npg: e.dma_start(
                        out=xcg[:, npg * TSC:(npg + 1) * TSC],
                        in_=xconv_d[(g + 1) * 128:(g + 2) * 128,
                                    t0:t0 + TSC]), "LD")
                    s.dma("sync", lambda e, g=g, npg=npg: e.dma_start(
                        out=zsg[:, npg * TSC:(npg + 1) * TSC],
                        in_=zsil_d[(g + 1) * 128:(g + 2) * 128,
                                   t0:t0 + TSC]), "LD")
                    ld_xc[g + 1] = s.n("LD")
                s.wait("vector", "DA", da_a[L])
                s.op("vector", lambda v, bi=bi, idx=idx: v.tensor_tensor_scan(
                    out=ub2[:, bi * TSC:(bi + 1) * TSC],
                    data0=dA2[:, bi * TSC:(bi + 1) * TSC],
                    data1=ub2[:, bi * TSC:(bi + 1) * TSC],
                    initial=state[:, idx:idx + 1],
                    op0=ALU.mult, op1=ALU.add), inc="SCN", sw=False)
                scn_a[L] = s.n("SCN")
                s.wait("tensor", "SCN", scn_a[L])
                if n == 0 and g >= 1:
                    s.wait("tensor", "AC", ac_evac[g - 1])
                for qq in range(4):
                    s.op("tensor", lambda e, bi=bi, qq=qq, n=n: e.matmul(
                        out=ps_y[:, qq * 512:(qq + 1) * 512],
                        lhsT=ident[:],
                        rhs=ub2[:, bi * TSC + qq * 512:
                                bi * TSC + (qq + 1) * 512],
                        start=(n == 0), stop=(n == NST - 1)))
                te_a[L] = s.n("TE")
            if sc < NSC - 1:
                s.wait("scalar", "SCN", scn_a[NL - 1])
                s.op("scalar", lambda e: e.activation(
                    out=state[:, NL - 1:NL],
                    in_=ub2[:, (((NL - 1) % 2) + 1) * TSC - 1:
                            (((NL - 1) % 2) + 1) * TSC],
                    func=AF.Copy, bias=0.0), sw=False)
            emit_evac(MD - 1)

            for ti in range(NTILE):
                s.wait("sync", "TE")
                s.wait("sync", "LD")
                s.dma("sync", lambda e, ti=ti: e.dma_start(
                    out=xdT[:].rearrange("p (g c) -> p g c", g=MD),
                    in_=y3_d.ap().rearrange("(g p) t -> p g t", p=128)
                    [:, :, t0 + ti * C:t0 + (ti + 1) * C]), "LD")
                s.wait("tensor", "LD")
                s.wait("tensor", "AC")
                for mt in range(KD):
                    pb = ps_r[mt % 4]
                    if mt >= 4:
                        s.wait("tensor", "AC")
                    for kt in range(MD):
                        s.op("tensor", lambda e, mt=mt, kt=kt, pb=pb: e.matmul(
                            out=pb[:], lhsT=wout_s[:, kt * DMODEL + mt * 128:
                                                   kt * DMODEL + mt * 128 + 128],
                            rhs=xdT[:, kt * C:(kt + 1) * C],
                            start=(kt == 0), stop=(kt == MD - 1)))
                    s.wait("scalar", "TE")
                    s.op("scalar", lambda e, mt=mt, pb=pb: e.activation(
                        out=fout[:, mt * C:(mt + 1) * C], in_=pb[:],
                        func=AF.Copy, bias=0.0), sw=False)
                for tj in range(NTILE):
                    stg = [qsg, qau][tj % 2]
                    s.wait("tensor", "AC")
                    for mt in range(KD):
                        s.op("tensor", lambda e, mt=mt, tj=tj: e.transpose(
                            out=ps_t2[:, mt * 128:(mt + 1) * 128],
                            in_=fout[:, mt * C + tj * 128:
                                     mt * C + tj * 128 + 128],
                            identity=ident[:]))
                    s.wait("scalar", "TE")
                    s.wait("scalar", "GD")
                    s.op("scalar", lambda e, stg=stg: e.activation(
                        out=stg[:], in_=ps_t2[:], func=AF.Copy, bias=0.0),
                        sw=False)
                    tloc = ti * C + tj * 128
                    slot, j0 = divmod(tloc, 1024)
                    s.wait("gpsimd", "AC")
                    s.dma("gpsimd", lambda g, slot=slot, j0=j0, stg=stg:
                          g.dma_start(
                              out=pay_in[sc][slot, j0:j0 + 128, :],
                              in_=stg[:], accum_op=ALU.add), "GD")

        def emit_RS(sc):
            s.wait("gpsimd", "GD")
            s.wait("gpsimd", "LD")
            s.op("gpsimd", lambda g: g.collective_compute(
                "ReduceScatter", ALU.add,
                replica_groups=[[0, 1], [2, 3], [4, 5], [6, 7]],
                ins=[pay_in[sc].ap().opt()],
                outs=[pay_out[sc].ap().opt()]), "CC")

        def emit_C_s1(sc, ch):
            p3 = pay_out[sc].ap().rearrange("(t p) d -> p t d", p=128)
            s.wait("sync", "CC", 2 * sc + 2)
            emit_ffn_s1(p3, ch * NTILE, (sc * 2 + ch) % 2)

        def emit_C_s3(sc, ch):
            def epilogue(ti):
                s.wait("sync", "VE")
                s.dma("sync", lambda e, ti=ti: e.dma_start(
                    out=xa[:],
                    in_=pay_out[sc][ch * C + ti * 128:ch * C + (ti + 1) * 128,
                                    :]), "LD")
                s.wait("vector", "ODMA")
                s.wait("vector", "LD")
                s.op("vector", lambda v: v.tensor_add(
                    out=x1t[:], in0=ps_t2[:], in1=xa[:]), sw=False)
                j0 = sc * 1024 + ch * C + ti * 128
                s.wait("sync", "VE")
                s.dma("sync", lambda e, j0=j0: e.dma_start(
                    out=out_d[j0:j0 + 128, :], in_=x1t[:]), "ODMA")

            emit_ffn_s3(epilogue)

        def emit_prep():
            def ld(out_ap, in_ap):
                s.dma("gpsimd", lambda g, o=out_ap, i=in_ap:
                      g.dma_start(out=o, in_=i), "GD")
            for kt in range(KD):
                ld(win_s[:, kt * 2 * DH:(kt + 1) * 2 * DH],
                   win_in[kt * 128:(kt + 1) * 128, :])
            for kt in range(MD):
                ld(wout_s[:, kt * DMODEL:(kt + 1) * DMODEL],
                   wout_in[kt * 128:(kt + 1) * 128, :])
                ld(wx_s[:, kt * 2 * NST:(kt + 1) * 2 * NST],
                   wx_in[kt * 128:(kt + 1) * 128, :])
            ld(wdt_s[:, :], wdt_in)
            ld(ident[:, :], ident_in)
            for kt in range(MD):
                for tap in range(4):
                    ld(convw_c[:, tap * MD + kt:tap * MD + kt + 1],
                       convw_in[tap, kt * 128:(kt + 1) * 128][:, None])
                ld(convb_c[:, kt:kt + 1], convb_in[kt * 128:(kt + 1) * 128, :])
                ld(bdt_c[:, kt:kt + 1], bdt_in[kt * 128:(kt + 1) * 128, :])
                ld(dvec_c[:, kt:kt + 1], dvec_in[kt * 128:(kt + 1) * 128, :])
                ld(acol_c[:, kt * NST:(kt + 1) * NST],
                   alog_in[kt * 128:(kt + 1) * 128, :])
            for mt in range(KD):
                ld(fbias_c[:, mt:mt + 1], fb_in[mt * 128:(mt + 1) * 128, :])
            s.op("vector", lambda v: v.memset(state[:], 0.0))
            s.op("vector", lambda v: v.memset(xproj[:], 0.0))
            s.wait("scalar", "GD")
            s.op("scalar", lambda e: e.activation(out=acol_c[:], in_=acol_c[:],
                                                  func=AF.Exp, bias=0.0))
            s.wait("vector", "AC")
            s.op("vector", lambda v: v.tensor_single_scalar(
                out=acol_c[:], in_=acol_c[:], scalar=-1.0, op=ALU.mult))
            s.op("vector", lambda v: v.tensor_single_scalar(
                out=fbias_c[:], in_=fbias_c[:], scalar=0.5, op=ALU.mult))
            for mt in range(KD):
                s.dma("sync", lambda e, mt=mt: e.dma_start(
                    out=xa[:], in_=fw_in[mt * 128:(mt + 1) * 128, :]), "LD")
                s.wait("vector", "LD")
                s.wait("scalar", "LD")
                emit_quant(xa, xdt, wscale=True)
                s.wait("tensor", "VE")
                for kt in range(KD):
                    s.op("tensor", lambda e, kt=kt: e.transpose(
                        out=ps_t[:, kt * 128:(kt + 1) * 128],
                        in_=xdt[:, kt * 128:(kt + 1) * 128], identity=ident[:]))
                s.wait("scalar", "TE")
                for kt in range(KD):
                    s.op("scalar", lambda e, kt=kt, mt=mt: e.activation(
                        out=wd_s[:, kt * DMODEL + mt * 128:
                                 kt * DMODEL + mt * 128 + 128],
                        in_=ps_t[:, kt * 128:(kt + 1) * 128],
                        func=AF.Copy, bias=0.0))
                barrier()

        emit_prep()
        for sc in range(NSC):
            for ch in range(NCH):
                emit_A_s1(sc, ch)
                if ch > 0:
                    emit_A_s3(sc, ch - 1)
                    emit_A_ln2(sc, ch - 1)
                    emit_phaseB1_chunk(sc, ch - 1)
                emit_ffn_s2(ch % 2)
            emit_A_s3(sc, NCH - 1)
            emit_A_ln2(sc, NCH - 1)
            emit_phaseB1_chunk(sc, NCH - 1)
            emit_AR(sc)
            emit_phaseB2_scroll(sc)
            emit_RS(sc)
        prevc = None
        for sc in range(NSC):
            for ch in range(2):
                emit_C_s1(sc, ch)
                if prevc is not None:
                    emit_C_s3(*prevc)
                emit_ffn_s2((sc * 2 + ch) % 2)
                prevc = (sc, ch)
        emit_C_s3(*prevc)
        s.wait("gpsimd", "ODMA")

        @block.sync
        def _(e):
            s.replay("sync", e, sems)

        @block.vector
        def _(e):
            s.replay("vector", e, sems)

        @block.scalar
        def _(e):
            s.replay("scalar", e, sems)

        @block.tensor
        def _(e):
            s.replay("tensor", e, sems)

        @block.gpsimd
        def _(e):
            s.replay("gpsimd", e, sems)

    return nc


def kernel(L0_attn_x, L0_attn_W_in, L0_attn_conv_w, L0_attn_conv_b, L0_attn_W_x,
           L0_attn_W_dt, L0_attn_b_dt, L0_attn_A_log, L0_attn_D, L0_attn_W_out,
           L0_ffn_weight, L0_ffn_bias):
    x = np.ascontiguousarray(np.asarray(L0_attn_x, dtype=np.float32))
    W_in = np.ascontiguousarray(np.asarray(L0_attn_W_in, dtype=np.float32))
    conv_w = np.ascontiguousarray(np.asarray(L0_attn_conv_w, dtype=np.float32))
    conv_b = np.ascontiguousarray(np.asarray(L0_attn_conv_b, dtype=np.float32))
    W_x = np.ascontiguousarray(np.asarray(L0_attn_W_x, dtype=np.float32))
    W_dt = np.ascontiguousarray(np.asarray(L0_attn_W_dt, dtype=np.float32))
    b_dt = np.ascontiguousarray(np.asarray(L0_attn_b_dt, dtype=np.float32))
    A_log = np.ascontiguousarray(np.asarray(L0_attn_A_log, dtype=np.float32))
    Dv = np.ascontiguousarray(np.asarray(L0_attn_D, dtype=np.float32))
    W_out = np.ascontiguousarray(np.asarray(L0_attn_W_out, dtype=np.float32))
    fw = np.ascontiguousarray(np.asarray(L0_ffn_weight, dtype=np.float32))
    fb = np.ascontiguousarray(np.asarray(L0_ffn_bias, dtype=np.float32))
    ident = np.eye(128, dtype=ml_dtypes.bfloat16)

    in_maps = []
    for r in range(8):
        b, h = r >> 1, r & 1
        sl = slice(h * DH, (h + 1) * DH)
        in_maps.append({
            "x_b": x[b],
            "win_sl": np.ascontiguousarray(
                np.concatenate([W_in[:, h * DH:(h + 1) * DH],
                                W_in[:, DIN + h * DH:DIN + (h + 1) * DH]],
                               axis=1)),
            "convw_sl": np.ascontiguousarray(conv_w[sl].T),
            "convb_sl": np.ascontiguousarray(conv_b[sl, None]),
            "wx_sl": np.ascontiguousarray(W_x[sl]),
            "wdt_sl": np.ascontiguousarray(W_dt[:, sl]),
            "bdt_sl": np.ascontiguousarray(b_dt[sl, None]),
            "alog_sl": np.ascontiguousarray(A_log[sl]),
            "dvec_sl": np.ascontiguousarray(Dv[sl, None]),
            "wout_sl": np.ascontiguousarray(W_out[sl]),
            "ffnw": fw,
            "ffnb": fb[:, None],
            "ident": ident,
        })

    nc = build_program()
    res = run_bass_kernel_spmd(nc, in_maps, core_ids=list(range(8)))

    out = np.empty((4, T, DMODEL), dtype=np.float32)
    for r in range(8):
        b, h = r >> 1, r & 1
        o = res.results[r]["out_r"]
        for k in range(NSC):
            out[b, k * TSC + h * 1024:k * TSC + (h + 1) * 1024, :] = \
                o[k * 1024:(k + 1) * 1024, :]
    return out


# version 10
# speedup vs baseline: 1.0633x; 1.0149x over previous
import numpy as np
import ml_dtypes
import concourse.bass as bass
import concourse.mybir as mybir
from concourse.bass_utils import run_bass_kernel_spmd

F32 = mybir.dt.float32
BF16 = mybir.dt.bfloat16
FP8 = mybir.dt.float8e4
U8 = mybir.dt.uint8
AF = mybir.ActivationFunctionType
ALU = mybir.AluOpType
AXX = mybir.AxisListType.X

DMODEL = 1024
DIN = 2048
DH = 1024
NST = 16
T = 8192
NSC = 4
TSC = T // NSC
NCH = 4
C = TSC // NCH
NTILE = C // 128
KD = DMODEL // 128
MD = DH // 128
NBQ = DMODEL // 32

ENGS = ("sync", "vector", "scalar", "tensor", "gpsimd")
DEF_SEM = {"vector": "VE", "scalar": "AC", "tensor": "TE"}


class Sched:
    def __init__(self):
        self.q = {e: [] for e in ENGS}
        self.cnt = {}
        self.last = {}

    def op(self, eng, fn, inc="default", k=1, sw=True):
        if inc == "default":
            inc = DEF_SEM.get(eng)
        if sw and eng in ("vector", "scalar"):
            prev = self.last.get(eng)
            if prev is not None:
                self.q[eng].append(["wait", prev[0], prev[1]])
        self.q[eng].append(["op", fn, inc, k])
        if inc is not None:
            self.cnt[inc] = self.cnt.get(inc, 0) + k
            self.last[eng] = (inc, self.cnt[inc])

    def dma(self, eng, fn, inc):
        self.op(eng, fn, inc, 16)

    def nop(self, eng, cycles=512):
        pass

    def wait(self, eng, sem, val=None):
        if val is None:
            val = self.cnt.get(sem, 0)
        self.q[eng].append(["wait", sem, val])

    def n(self, sem):
        return self.cnt.get(sem, 0)

    def replay(self, eng_name, eng, sems):
        for item in self.q[eng_name]:
            if item[0] == "op":
                _, fn, inc, k = item
                inst = fn(eng)
                if inc is not None:
                    inst.then_inc(sems[inc], k)
            else:
                _, sem, val = item
                if val > 0:
                    eng.wait_ge(sems[sem], val)


def build_program():
    nc = bass.Bass("TRN2", target_bir_lowering=False)

    x_in = nc.dram_tensor("x_b", [T, DMODEL], F32, kind="ExternalInput").ap()
    win_in = nc.dram_tensor("win_sl", [DMODEL, 2 * DH], F32, kind="ExternalInput").ap()
    convw_in = nc.dram_tensor("convw_sl", [4, DH], F32, kind="ExternalInput").ap()
    convb_in = nc.dram_tensor("convb_sl", [DH, 1], F32, kind="ExternalInput").ap()
    wx_in = nc.dram_tensor("wx_sl", [DH, 2 * NST], F32, kind="ExternalInput").ap()
    wdt_in = nc.dram_tensor("wdt_sl", [NST, DH], F32, kind="ExternalInput").ap()
    bdt_in = nc.dram_tensor("bdt_sl", [DH, 1], F32, kind="ExternalInput").ap()
    alog_in = nc.dram_tensor("alog_sl", [DH, NST], F32, kind="ExternalInput").ap()
    dvec_in = nc.dram_tensor("dvec_sl", [DH, 1], F32, kind="ExternalInput").ap()
    wout_in = nc.dram_tensor("wout_sl", [DH, DMODEL], F32, kind="ExternalInput").ap()
    fw_in = nc.dram_tensor("ffnw", [DMODEL, DMODEL], F32, kind="ExternalInput").ap()
    fb_in = nc.dram_tensor("ffnb", [DMODEL, 1], F32, kind="ExternalInput").ap()
    ident_in = nc.dram_tensor("ident", [128, 128], BF16, kind="ExternalInput").ap()
    out_d = nc.dram_tensor("out_r", [NSC * 1024, DMODEL], F32,
                           kind="ExternalOutput").ap()

    pay_in = [nc.dram_tensor(f"pay_in{k}", [2, 1024, DMODEL], F32, kind="Internal")
              for k in range(NSC)]
    pay_out = [nc.dram_tensor(f"pay_out{k}", [1024, DMODEL], F32, kind="Internal")
               for k in range(NSC)]
    ar_in = [nc.dram_tensor(f"ar_in{k}", [2 * NST, TSC], F32, kind="Internal")
             for k in range(NSC)]
    ar_out = [nc.dram_tensor(f"ar_out{k}", [2 * NST, TSC], F32, kind="Internal")
              for k in range(NSC)]
    xconv_d = nc.dram_tensor("xconv_d", [DH, T], BF16, kind="Internal")
    zsil_d = nc.dram_tensor("zsil_d", [DH, T], BF16, kind="Internal")
    y3_d = nc.dram_tensor("y3_d", [DH, T], BF16, kind="Internal")
    Bd = nc.dram_tensor("Bd", [NST, TSC], BF16, kind="Internal")

    s = Sched()

    from contextlib import ExitStack
    with ExitStack() as st:
        win_s = st.enter_context(nc.sbuf_tensor("win_s", [128, KD * 2 * DH], BF16))
        wout_s = st.enter_context(nc.sbuf_tensor("wout_s", [128, MD * DMODEL], BF16))
        wd_s = st.enter_context(nc.sbuf_tensor("wd_s", [128, KD * DMODEL], BF16))
        wx_s = st.enter_context(nc.sbuf_tensor("wx_s", [128, MD * 2 * NST], BF16))
        wdt_s = st.enter_context(nc.sbuf_tensor("wdt_s", [NST, DH], BF16))
        ident = st.enter_context(nc.sbuf_tensor("ident_s", [128, 128], BF16))
        convw_c = st.enter_context(nc.sbuf_tensor("convw_c", [128, 4 * MD], F32))
        convb_c = st.enter_context(nc.sbuf_tensor("convb_c", [128, MD], F32))
        bdt_c = st.enter_context(nc.sbuf_tensor("bdt_c", [128, MD], F32))
        dvec_c = st.enter_context(nc.sbuf_tensor("dvec_c", [128, MD], F32))
        acol_c = st.enter_context(nc.sbuf_tensor("acol_c", [128, MD * NST], F32))
        fbias_c = st.enter_context(nc.sbuf_tensor("fbias_c", [128, KD], F32))
        qsc = st.enter_context(nc.sbuf_tensor("qsc", [128, NBQ], F32))
        qrcp = st.enter_context(nc.sbuf_tensor("qrcp", [128, NBQ], F32))
        qu = st.enter_context(nc.sbuf_tensor("qu", [128, DMODEL], F32))
        qau = st.enter_context(nc.sbuf_tensor("qau", [128, DMODEL], F32))
        qsg = st.enter_context(nc.sbuf_tensor("qsg", [128, DMODEL], F32))
        qb8 = st.enter_context(nc.sbuf_tensor("qb8", [128, DMODEL], FP8))
        qc8 = st.enter_context(nc.sbuf_tensor("qc8", [128, DMODEL], FP8))
        qm = st.enter_context(nc.sbuf_tensor("qm", [128, DMODEL], U8))
        xa = st.enter_context(nc.sbuf_tensor("xa", [128, DMODEL], F32))
        x1t = st.enter_context(nc.sbuf_tensor("x1t", [128, DMODEL], F32))
        xdt = st.enter_context(nc.sbuf_tensor("xdt", [128, DMODEL], BF16))
        lns1 = st.enter_context(nc.sbuf_tensor("lns1", [128, NTILE], F32))
        lns2 = st.enter_context(nc.sbuf_tensor("lns2", [128, NTILE], F32))
        lnm = st.enter_context(nc.sbuf_tensor("lnm", [128, NTILE], F32))
        lnv = st.enter_context(nc.sbuf_tensor("lnv", [128, NTILE], F32))
        lnr = st.enter_context(nc.sbuf_tensor("lnr", [128, NTILE], F32))
        xdT = st.enter_context(nc.sbuf_tensor("xdT", [128, KD * C], BF16))
        fout = st.enter_context(nc.sbuf_tensor("fout", [128, KD * C], BF16))
        xn2T = st.enter_context(nc.sbuf_tensor("xn2T", [128, KD * C], BF16))
        xproj = st.enter_context(nc.sbuf_tensor("xproj", [128, MD * (4 + C)], BF16))
        zs2 = st.enter_context(nc.sbuf_tensor("zs2", [128, MD * C], BF16))
        dtin16 = st.enter_context(nc.sbuf_tensor("dtin16", [NST, TSC], BF16))
        cacc = st.enter_context(nc.sbuf_tensor("cacc", [128, 2 * C], F32))
        xcb = st.enter_context(nc.sbuf_tensor("xcb", [128, 2 * C], BF16))
        xcg = st.enter_context(nc.sbuf_tensor("xcg", [128, 2 * TSC], BF16))
        zsg = st.enter_context(nc.sbuf_tensor("zsg", [128, 2 * TSC], BF16))
        dts = st.enter_context(nc.sbuf_tensor("dts", [128, 2 * TSC], BF16))
        wgb = st.enter_context(nc.sbuf_tensor("wgb", [128, TSC], BF16))
        dA2 = st.enter_context(nc.sbuf_tensor("dA2", [128, 2 * TSC], F32))
        ub2 = st.enter_context(nc.sbuf_tensor("ub2", [128, 2 * TSC], BF16))
        Bb2 = st.enter_context(nc.sbuf_tensor("Bb2", [128, 2 * TSC], BF16))
        ygb = st.enter_context(nc.sbuf_tensor("ygb", [128, TSC], BF16))
        state = st.enter_context(nc.sbuf_tensor("state", [128, MD * NST], F32))
        ps_t = st.enter_context(nc.psum_tensor("ps_t", [128, 1024], BF16))
        ps_t2 = ps_t
        ps_f0 = st.enter_context(nc.psum_tensor("ps_f0", [128, 512], F32))
        ps_f1 = st.enter_context(nc.psum_tensor("ps_f1", [128, 512], F32))
        ps_dt = st.enter_context(nc.psum_tensor("ps_dt", [128, 512], F32))
        ps_x = ps_dt
        ps_y = st.enter_context(nc.psum_tensor("ps_y", [128, TSC], F32))
        sem_LD = st.enter_context(nc.semaphore("LD"))
        sem_GD = st.enter_context(nc.semaphore("GD"))
        sem_VE = st.enter_context(nc.semaphore("VE"))
        sem_AC = st.enter_context(nc.semaphore("AC"))
        sem_TE = st.enter_context(nc.semaphore("TE"))
        sem_CC = st.enter_context(nc.semaphore("CC"))
        sem_DA = st.enter_context(nc.semaphore("DA"))
        sem_SCN = st.enter_context(nc.semaphore("SCN"))
        sem_ODMA = st.enter_context(nc.semaphore("ODMA"))
        sem_PL = st.enter_context(nc.semaphore("PL"))
        block = st.enter_context(nc.Block())
        xh = qsg
        payt = qau
        sems = {"LD": sem_LD, "GD": sem_GD, "VE": sem_VE, "AC": sem_AC,
                "TE": sem_TE, "CC": sem_CC, "DA": sem_DA, "SCN": sem_SCN,
                "ODMA": sem_ODMA, "PL": sem_PL}
        ps_f = [ps_f0, ps_f1]
        ps_r = [ps_f0, ps_f1, ps_y[:, 0:512], ps_y[:, 512:1024]]

        def barrier():
            for e in ENGS:
                for sname in ("LD", "GD", "VE", "AC", "TE", "CC", "DA",
                              "SCN", "PL"):
                    s.wait(e, sname)

        def emit_quant(src, dst, wscale=False):
            v3 = src[:].rearrange("p (s n) -> p s n", n=32)
            s.op("vector", lambda v: v.tensor_reduce(
                out=qsc[:], in_=v3, axis=AXX, op=ALU.max,
                apply_absolute_value=True))
            sc2 = (1.0 / 7.5) * (0.5 if wscale else 1.0)
            s.op("vector", lambda v: v.tensor_scalar(
                out=qsc[:], in0=qsc[:], scalar1=sc2, op0=ALU.mult,
                scalar2=(0.5e-30 if wscale else 1e-30), op1=ALU.max))
            s.op("vector", lambda v: v.reciprocal(out=qrcp[:], in_=qsc[:]))
            if wscale:
                s.op("vector", lambda v: v.tensor_single_scalar(
                    out=qrcp[:], in_=qrcp[:], scalar=0.5, op=ALU.mult))
            s.op("vector", lambda v: v.tensor_tensor(
                out=qu[:].rearrange("p (s n) -> p s n", n=32), in0=v3,
                in1=qrcp[:].broadcast_to([128, NBQ, 32]), op=ALU.mult))
            s.wait("scalar", "VE")
            s.wait("scalar", "LD")
            s.op("scalar", lambda e: e.activation(out=qau[:], in_=qu[:],
                                                  func=AF.Abs, bias=0.0),
                 sw=False)
            s.op("scalar", lambda e: e.activation(out=qsg[:], in_=qu[:],
                                                  func=AF.Sign, bias=0.0),
                 sw=False)
            s.op("scalar", lambda e: e.activation(out=qb8[:], in_=qau[:],
                                                  func=AF.Copy, bias=0.0),
                 sw=False)
            s.op("scalar", lambda e: e.activation(out=qc8[:], in_=qau[:],
                                                  func=AF.Copy, bias=1.0),
                 sw=False)
            s.op("scalar", lambda e: e.activation(out=qu[:], in_=qc8[:],
                                                  func=AF.Copy, bias=-1.0),
                 sw=False)
            s.wait("vector", "AC")
            s.op("vector", lambda v: v.tensor_single_scalar(
                out=qm[:], in_=qau[:], scalar=1.0, op=ALU.is_lt), sw=False)
            s.op("vector", lambda v: v.tensor_copy(out=qau[:], in_=qb8[:]),
                 sw=False)
            s.op("vector", lambda v: v.copy_predicated(out=qau[:], mask=qm[:],
                                                       data=qu[:]), sw=False)
            s.op("vector", lambda v: v.tensor_mul(out=qsg[:], in0=qau[:],
                                                  in1=qsg[:]), sw=False)
            s.op("vector", lambda v: v.tensor_tensor(
                out=dst[:].rearrange("p (s n) -> p s n", n=32),
                in0=qsg[:].rearrange("p (s n) -> p s n", n=32),
                in1=qsc[:].broadcast_to([128, NBQ, 32]), op=ALU.mult),
                sw=False)

        def emit_ln_stats(src3, sq3):
            s.op("vector", lambda v: v.tensor_reduce(
                out=lns1[:, 0:NTILE], in_=src3, axis=AXX, op=ALU.add))
            s.op("scalar", lambda e: e.activation(
                out=sq3, in_=src3, func=AF.Square, bias=0.0))
            s.wait("vector", "AC")
            s.op("vector", lambda v: v.tensor_reduce(
                out=lns2[:, 0:NTILE], in_=sq3, axis=AXX, op=ALU.add),
                sw=False)
            s.op("vector", lambda v: v.tensor_single_scalar(
                out=lnm[:, 0:NTILE], in_=lns1[:, 0:NTILE],
                scalar=1.0 / DMODEL, op=ALU.mult))
            s.op("vector", lambda v: v.tensor_single_scalar(
                out=lnv[:, 0:NTILE], in_=lns2[:, 0:NTILE],
                scalar=1.0 / DMODEL, op=ALU.mult))
            s.op("vector", lambda v: v.tensor_mul(
                out=lns1[:, 0:NTILE], in0=lnm[:, 0:NTILE],
                in1=lnm[:, 0:NTILE]))
            s.op("vector", lambda v: v.tensor_sub(
                out=lnv[:, 0:NTILE], in0=lnv[:, 0:NTILE],
                in1=lns1[:, 0:NTILE]))
            s.op("vector", lambda v: v.tensor_single_scalar(
                out=lnv[:, 0:NTILE], in_=lnv[:, 0:NTILE], scalar=1e-5,
                op=ALU.add))
            s.wait("scalar", "VE")
            s.op("scalar", lambda e: e.activation(
                out=lnv[:, 0:NTILE], in_=lnv[:, 0:NTILE], func=AF.Ln,
                bias=0.0))
            s.wait("vector", "AC")
            s.op("vector", lambda v: v.tensor_single_scalar(
                out=lnv[:, 0:NTILE], in_=lnv[:, 0:NTILE], scalar=-0.5,
                op=ALU.mult))
            s.wait("scalar", "VE")
            s.op("scalar", lambda e: e.activation(
                out=lnr[:, 0:NTILE], in_=lnv[:, 0:NTILE], func=AF.Exp,
                bias=0.0))

        xck = dA2
        xck3 = dA2[:].rearrange("p (t d) -> p t d", d=DMODEL)
        sqA3 = Bb2[:].rearrange("p (t d) -> p t d", d=DMODEL)
        sqB3 = xcg[:].rearrange("p (t d) -> p t d", d=DMODEL)
        x1ck = ub2
        x1ck3 = ub2[:].rearrange("p (t d) -> p t d", d=DMODEL)

        def xdT_r(par, kt):
            if par == 0:
                return xdT[:, kt * C:(kt + 1) * C]
            if kt < 4:
                return wgb[:, kt * C:(kt + 1) * C]
            return ygb[:, (kt - 4) * C:(kt - 3) * C]

        def xdT_w(par, ti):
            if par == 0:
                s.op("scalar", lambda e, ti=ti: e.activation(
                    out=xdT[:].rearrange("p (k c) -> p k c", k=KD)
                    [:, :, ti * 128:(ti + 1) * 128],
                    in_=ps_t[:].rearrange("p (k c) -> p k c", k=KD),
                    func=AF.Copy, bias=0.0))
            else:
                s.op("scalar", lambda e, ti=ti: e.activation(
                    out=wgb[:].rearrange("p (k c) -> p k c", k=4)
                    [:, :, ti * 128:(ti + 1) * 128],
                    in_=ps_t[:, 0:512].rearrange("p (k c) -> p k c", k=4),
                    func=AF.Copy, bias=0.0))
                s.op("scalar", lambda e, ti=ti: e.activation(
                    out=ygb[:].rearrange("p (k c) -> p k c", k=4)
                    [:, :, ti * 128:(ti + 1) * 128],
                    in_=ps_t[:, 512:1024].rearrange("p (k c) -> p k c", k=4),
                    func=AF.Copy, bias=0.0), sw=False)

        xck_free = [0, 0]

        def emit_ffn_s1(src_dram3, idx, par):
            s.wait("sync", "VE", xck_free[0])
            s.wait("sync", "AC", xck_free[1])
            s.dma("sync", lambda e: e.dma_start(
                out=xck3, in_=src_dram3[:, idx:idx + NTILE, :]), "LD")
            s.wait("vector", "LD")
            s.wait("scalar", "LD")
            s.wait("vector", "GD")
            emit_ln_stats(xck3, sqA3)
            xck_free[1] = s.n("AC")
            for ti in range(NTILE):
                if ti == 0:
                    s.wait("vector", "AC")
                s.op("vector", lambda v, ti=ti: v.tensor_scalar(
                    out=xh[:], in0=xck3[:, ti, :], scalar1=lnm[:, ti:ti + 1],
                    op0=ALU.subtract, scalar2=lnr[:, ti:ti + 1],
                    op1=ALU.mult))
                if ti == NTILE - 1:
                    xck_free[0] = s.n("VE")
                emit_quant(xh, xdt)
                s.wait("tensor", "VE")
                s.wait("tensor", "AC")
                for kt in range(KD):
                    s.op("tensor", lambda e, kt=kt: e.transpose(
                        out=ps_t[:, kt * 128:(kt + 1) * 128],
                        in_=xdt[:, kt * 128:(kt + 1) * 128], identity=ident[:]))
                s.wait("scalar", "TE")
                if par == 1:
                    s.wait("scalar", "VE")
                    s.wait("scalar", "LD")
                xdT_w(par, ti)

        def emit_ffn_s2(par):
            s.wait("tensor", "AC")
            for mt in range(KD):
                pb = ps_r[mt % 4]
                if mt >= 4:
                    s.wait("tensor", "AC")
                for kt in range(KD):
                    s.op("tensor", lambda e, mt=mt, kt=kt, pb=pb: e.matmul(
                        out=pb[:], lhsT=wd_s[:, kt * DMODEL + mt * 128:
                                             kt * DMODEL + mt * 128 + 128],
                        rhs=xdT_r(par, kt),
                        start=(kt == 0), stop=(kt == KD - 1)))
                s.wait("scalar", "TE")
                s.op("scalar", lambda e, mt=mt, pb=pb: e.activation(
                    out=fout[:, mt * C:(mt + 1) * C], in_=pb[:],
                    func=AF.Identity, bias=fbias_c[:, mt:mt + 1]), sw=False)

        def emit_ffn_s3(epilogue):
            for ti in range(NTILE):
                s.wait("tensor", "AC")
                s.wait("tensor", "VE")
                for mt in range(KD):
                    s.op("tensor", lambda e, mt=mt, ti=ti: e.transpose(
                        out=ps_t2[:, mt * 128:(mt + 1) * 128],
                        in_=fout[:, mt * C + ti * 128:mt * C + ti * 128 + 128],
                        identity=ident[:]))
                s.wait("vector", "TE")
                epilogue(ti)

        x3src = x_in.rearrange("(t p) d -> p t d", p=128)

        def emit_A_s1(sc, ch):
            emit_ffn_s1(x3src, (sc * TSC + ch * C) // 128, ch % 2)

        def emit_A_s3(sc, ch):
            g0 = sc * TSC + ch * C

            def epilogue(ti):
                s.wait("sync", "VE")
                s.dma("sync", lambda e, ti=ti: e.dma_start(
                    out=xa[:],
                    in_=x_in[g0 + ti * 128:g0 + (ti + 1) * 128, :]), "LD")
                if ti == 0:
                    s.wait("vector", "AC")
                s.wait("vector", "LD")
                s.op("vector", lambda v: v.tensor_add(
                    out=x1t[:], in0=ps_t2[:], in1=xa[:]))
                s.wait("vector", "LD")
                s.op("vector", lambda v: v.tensor_single_scalar(
                    out=qu[:], in_=x1t[:], scalar=0.5, op=ALU.mult))
                tloc = ch * C + ti * 128
                slot, j0 = divmod(tloc, 1024)
                s.wait("sync", "VE")
                s.dma("sync", lambda e, slot=slot, j0=j0: e.dma_start(
                    out=pay_in[sc][slot, j0:j0 + 128, :], in_=qu[:]), "LD")
                s.op("vector", lambda v, ti=ti: v.tensor_copy(
                    out=x1ck[:, ti * DMODEL:(ti + 1) * DMODEL], in_=x1t[:]),
                    sw=False)

            emit_ffn_s3(epilogue)

        def emit_A_ln2(sc, ch):
            s.wait("scalar", "VE")
            emit_ln_stats(x1ck3, sqB3)
            for ti in range(NTILE):
                s.wait("vector", "AC")
                s.op("vector", lambda v, ti=ti: v.tensor_scalar(
                    out=xdt[:], in0=x1ck3[:, ti, :],
                    scalar1=lnm[:, ti:ti + 1], op0=ALU.subtract,
                    scalar2=lnr[:, ti:ti + 1], op1=ALU.mult))
                s.wait("tensor", "VE")
                s.wait("tensor", "AC")
                for kt in range(KD):
                    s.op("tensor", lambda e, kt=kt: e.transpose(
                        out=ps_t[:, kt * 128:(kt + 1) * 128],
                        in_=xdt[:, kt * 128:(kt + 1) * 128], identity=ident[:]))
                s.wait("scalar", "TE")
                s.op("scalar", lambda e, ti=ti: e.activation(
                    out=xn2T[:].rearrange("p (k c) -> p k c", k=KD)
                    [:, :, ti * 128:(ti + 1) * 128],
                    in_=ps_t[:].rearrange("p (k c) -> p k c", k=KD),
                    func=AF.Copy, bias=0.0))

        def emit_phaseB1_chunk(sc, ch):
            s.wait("tensor", "AC")
            s.wait("tensor", "VE")
            s.wait("scalar", "VE")
            s.wait("scalar", "GD")
            s.wait("scalar", "LD")
            ac_xp = [0] * MD
            ac_cs = [0] * MD
            for mt in range(MD):
                pb = ps_r[mt % 4]
                if mt >= 4:
                    s.wait("tensor", "AC")
                for kt in range(KD):
                    s.op("tensor", lambda e, mt=mt, kt=kt, pb=pb: e.matmul(
                        out=pb[:], lhsT=win_s[:, kt * 2 * DH + mt * 128:
                                              kt * 2 * DH + mt * 128 + 128],
                        rhs=xn2T[:, kt * C:(kt + 1) * C],
                        start=(kt == 0), stop=(kt == KD - 1)))
                s.wait("scalar", "TE")
                s.op("scalar", lambda e, g=mt, pb=pb: e.activation(
                    out=xproj[:, g * (4 + C) + 4:g * (4 + C) + 4 + C],
                    in_=pb[:], func=AF.Copy, bias=0.0), sw=False)
                ac_xp[mt] = s.n("AC")
            for g in range(MD):
                par = g % 2
                xp = xproj[:, g * (4 + C):(g + 1) * (4 + C)]
                ca = cacc[:, par * C:(par + 1) * C]
                s.wait("vector", "AC", ac_xp[g])
                if g >= 2:
                    s.wait("vector", "AC", ac_cs[g - 2])
                s.op("vector", lambda v, xp=xp, ca=ca, g=g: v.tensor_single_scalar(
                    out=ca, in_=xp[:, 1:1 + C],
                    scalar=convw_c[:, 0 * MD + g:0 * MD + g + 1], op=ALU.mult),
                    sw=False)
                for tap in range(1, 4):
                    s.op("vector", lambda v, xp=xp, ca=ca, g=g, tap=tap:
                         v.scalar_tensor_tensor(
                             out=ca, in0=xp[:, 1 + tap:1 + tap + C],
                             scalar=convw_c[:, tap * MD + g:tap * MD + g + 1],
                             in1=ca, op0=ALU.mult, op1=ALU.add))
                s.op("vector", lambda v, xp=xp: v.tensor_copy(
                    out=xp[:, 1:4], in_=xp[:, 1 + C:4 + C]), sw=False)
                s.wait("scalar", "VE")
                s.wait("scalar", "TE")
                s.wait("scalar", "GD")
                s.op("scalar", lambda e, g=g, ca=ca, par=par: e.activation(
                    out=xcb[:, par * C:(par + 1) * C], in_=ca,
                    func=AF.Silu, bias=convb_c[:, g:g + 1]))
                ac_cs[g] = s.n("AC")
                s.wait("tensor", "AC")
                s.op("tensor", lambda e, g=g, par=par: e.matmul(
                    out=ps_x[0:32, :], lhsT=wx_s[:, g * 2 * NST:(g + 1) * 2 * NST],
                    rhs=xcb[:, par * C:(par + 1) * C],
                    start=(g == 0), stop=(g == MD - 1)))
                s.wait("gpsimd", "AC")
                s.dma("gpsimd", lambda e, g=g, par=par: e.dma_start(
                    out=xconv_d[g * 128:(g + 1) * 128,
                                sc * TSC + ch * C:sc * TSC + ch * C + C],
                    in_=xcb[:, par * C:(par + 1) * C]), "GD")
            for mt in range(MD, 2 * MD):
                pb = ps_r[mt % 4]
                s.wait("tensor", "AC")
                for kt in range(KD):
                    s.op("tensor", lambda e, mt=mt, kt=kt, pb=pb: e.matmul(
                        out=pb[:], lhsT=win_s[:, kt * 2 * DH + mt * 128:
                                              kt * 2 * DH + mt * 128 + 128],
                        rhs=xn2T[:, kt * C:(kt + 1) * C],
                        start=(kt == 0), stop=(kt == KD - 1)))
                s.wait("scalar", "TE")
                s.op("scalar", lambda e, g=mt - MD, pb=pb: e.activation(
                    out=zs2[:, g * C:(g + 1) * C],
                    in_=pb[:], func=AF.Silu, bias=0.0), sw=False)
            s.wait("gpsimd", "AC")
            for g in range(MD):
                s.dma("gpsimd", lambda e, g=g: e.dma_start(
                    out=zsil_d[g * 128:(g + 1) * 128,
                               sc * TSC + ch * C:sc * TSC + ch * C + C],
                    in_=zs2[:, g * C:(g + 1) * C]), "GD")
            s.wait("vector", "TE")
            s.op("vector", lambda v: v.tensor_copy(
                out=cacc[0:32, 0:C], in_=ps_x[0:32, :]))
            s.wait("sync", "VE")
            s.dma("sync", lambda e: e.dma_start(
                out=ar_in[sc][:, ch * C:(ch + 1) * C],
                in_=cacc[0:32, 0:C]), "LD")

        def emit_AR(sc):
            s.wait("gpsimd", "LD")
            s.op("gpsimd", lambda g: g.collective_compute(
                "AllReduce", ALU.add,
                replica_groups=[[0, 1], [2, 3], [4, 5], [6, 7]],
                ins=[ar_in[sc].ap().opt()],
                outs=[ar_out[sc].ap().opt()]), "CC")

        def emit_phaseB2_scroll(sc):
            t0 = sc * TSC
            s.wait("gpsimd", "CC")
            s.dma("gpsimd", lambda g: g.dma_start(
                out=dtin16[:, :], in_=ar_out[sc][0:NST, :]), "GD")
            s.dma("gpsimd", lambda g: g.dma_start(
                out=zs2[0:NST, 0:TSC], in_=ar_out[sc][NST:2 * NST, :]), "GD")
            s.wait("sync", "GD")
            s.dma("sync", lambda e: e.dma_start(out=Bd[:, :],
                                                in_=zs2[0:NST, 0:TSC]), "LD")
            s.wait("sync", "VE")
            s.wait("sync", "AC")
            s.dma("sync", lambda e: e.dma_start(
                out=xcg[:, 0:TSC], in_=xconv_d[0:128, t0:t0 + TSC]), "LD")
            s.dma("sync", lambda e: e.dma_start(
                out=zsg[:, 0:TSC], in_=zsil_d[0:128, t0:t0 + TSC]), "LD")
            ld_xc0 = s.n("LD")

            NL = MD * NST
            scn_a = [0] * NL
            te_a = [0] * NL
            pl_a = [0] * NL
            da_a = [0] * NL
            ld_a = [0] * NL
            ac_tail = [0] * NL
            ve_wmul = [0] * MD
            ve_evac = [0] * MD
            ac_evac = [0] * MD
            ld_y3 = [0] * MD
            ld_xc = [0] * MD
            ld_xc[0] = ld_xc0

            def emit_bcast(L):
                if L >= NL:
                    return
                bi = L % 2
                if L >= 2:
                    s.wait("sync", "PL", pl_a[L - 2])
                n = L % NST
                s.dma("sync", lambda e, n=n, bi=bi: e.dma_start(
                    out=Bb2[:, bi * TSC:(bi + 1) * TSC],
                    in_=Bd.ap()[None, n, :].broadcast_to([128, TSC])), "LD")
                ld_a[L] = s.n("LD")

            def emit_evac(g):
                pg = g % 2
                L15 = g * NST + NST - 1
                s.wait("scalar", "TE", te_a[L15])
                if g >= 1:
                    s.wait("scalar", "LD", ld_y3[g - 1])
                for qq in range(4):
                    s.op("scalar", lambda e, qq=qq: e.activation(
                        out=ygb[:, qq * 512:(qq + 1) * 512],
                        in_=ps_y[:, qq * 512:(qq + 1) * 512],
                        func=AF.Copy, bias=0.0), sw=(qq == 0))
                ac_evac[g] = s.n("AC")
                s.wait("vector", "AC", ac_evac[g])
                s.op("vector", lambda v, g=g, pg=pg: v.scalar_tensor_tensor(
                    out=ygb[:], in0=xcg[:, pg * TSC:(pg + 1) * TSC],
                    scalar=dvec_c[:, g:g + 1], in1=ygb[:],
                    op0=ALU.mult, op1=ALU.add), sw=False)
                s.op("vector", lambda v, pg=pg: v.tensor_mul(
                    out=ygb[:], in0=ygb[:], in1=zsg[:, pg * TSC:(pg + 1) * TSC]))
                ve_evac[g] = s.n("VE")
                s.wait("sync", "VE", ve_evac[g])
                s.dma("sync", lambda e, g=g: e.dma_start(
                    out=y3_d[g * 128:(g + 1) * 128, t0:t0 + TSC], in_=ygb[:]),
                    "LD")
                ld_y3[g] = s.n("LD")

            for L in range(NL):
                g, n = divmod(L, NST)
                bi = L % 2
                pg = g % 2
                idx = L
                if n == 0:
                    if g == 0:
                        emit_bcast(0)
                        emit_bcast(1)
                    else:
                        emit_evac(g - 1)
                    s.wait("tensor", "GD")
                    for q in range(4):
                        pb = ps_f[q % 2]
                        par = q % 2
                        qsl = slice(par * C, (par + 1) * C)
                        dsl = slice(pg * TSC + q * C, pg * TSC + (q + 1) * C)
                        s.wait("tensor", "AC")
                        if q >= 2:
                            s.wait("tensor", "VE")
                        s.op("tensor", lambda e, g=g, q=q, pb=pb: e.matmul(
                            out=pb[:], lhsT=wdt_s[:, g * 128:(g + 1) * 128],
                            rhs=dtin16[:, q * C:(q + 1) * C],
                            start=True, stop=True))
                        s.wait("scalar", "TE")
                        if q >= 2:
                            s.wait("scalar", "VE")
                        s.op("scalar", lambda e, g=g, pb=pb, qsl=qsl:
                             e.activation(out=qu[:, qsl], in_=pb[:],
                                          func=AF.Abs,
                                          bias=bdt_c[:, g:g + 1]), sw=False)
                        s.op("scalar", lambda e, qsl=qsl: e.activation(
                            out=qu[:, qsl], in_=qu[:, qsl],
                            func=AF.Exp, bias=0.0, scale=-1.0))
                        s.op("scalar", lambda e, qsl=qsl: e.activation(
                            out=qu[:, qsl], in_=qu[:, qsl],
                            func=AF.Ln, bias=1.0))
                        s.op("scalar", lambda e, g=g, pb=pb, dsl=dsl:
                             e.activation(out=dts[:, dsl], in_=pb[:],
                                          func=AF.Relu,
                                          bias=bdt_c[:, g:g + 1]), sw=False)
                        s.wait("vector", "AC")
                        s.op("vector", lambda v, dsl=dsl, qsl=qsl:
                             v.tensor_tensor(out=dts[:, dsl],
                                             in0=dts[:, dsl],
                                             in1=qu[:, qsl], op=ALU.add),
                             sw=False)
                    if g == 0:
                        s.wait("vector", "TE")
                    s.wait("vector", "LD", ld_xc[g])
                    s.op("vector", lambda v, pg=pg: v.tensor_mul(
                        out=wgb[:], in0=dts[:, pg * TSC:(pg + 1) * TSC],
                        in1=xcg[:, pg * TSC:(pg + 1) * TSC]), sw=True)
                    ve_wmul[g] = s.n("VE")
                    ve_dts = s.n("VE")
                    s.wait("scalar", "VE", ve_dts)
                if L >= 2:
                    s.wait("scalar", "SCN", scn_a[L - 2])
                s.op("scalar", lambda e, pg=pg, bi=bi, idx=idx: e.activation(
                    out=dA2[:, bi * TSC:(bi + 1) * TSC],
                    in_=dts[:, pg * TSC:(pg + 1) * TSC],
                    func=AF.Exp, bias=0.0,
                    scale=acol_c[:, idx:idx + 1]), inc="DA", sw=False)
                da_a[L] = s.n("DA")
                if sc < NSC - 1 and L >= 1:
                    s.wait("scalar", "SCN", scn_a[L - 1])
                    s.op("scalar", lambda e, Lp=L - 1: e.activation(
                        out=state[:, Lp:Lp + 1],
                        in_=ub2[:, ((Lp % 2) + 1) * TSC - 1:
                                ((Lp % 2) + 1) * TSC],
                        func=AF.Copy, bias=0.0), sw=False)
                ac_tail[L - 1] = s.n("AC")
                s.wait("vector", "LD", ld_a[L])
                if L >= 2:
                    s.wait("vector", "TE", te_a[L - 2])
                    s.wait("vector", "AC", ac_tail[L - 2])
                s.op("vector", lambda v, bi=bi: v.tensor_mul(
                    out=ub2[:, bi * TSC:(bi + 1) * TSC], in0=wgb[:],
                    in1=Bb2[:, bi * TSC:(bi + 1) * TSC]), inc="PL", sw=False)
                pl_a[L] = s.n("PL")
                emit_bcast(L + 2)
                if n == 2 and g + 1 < MD:
                    npg = (g + 1) % 2
                    if g >= 1:
                        s.wait("sync", "VE", ve_evac[g - 1])
                    s.dma("sync", lambda e, g=g, npg=npg: e.dma_start(
                        out=xcg[:, npg * TSC:(npg + 1) * TSC],
                        in_=xconv_d[(g + 1) * 128:(g + 2) * 128,
                                    t0:t0 + TSC]), "LD")
                    s.dma("sync", lambda e, g=g, npg=npg: e.dma_start(
                        out=zsg[:, npg * TSC:(npg + 1) * TSC],
                        in_=zsil_d[(g + 1) * 128:(g + 2) * 128,
                                   t0:t0 + TSC]), "LD")
                    ld_xc[g + 1] = s.n("LD")
                s.wait("vector", "DA", da_a[L])
                s.op("vector", lambda v, bi=bi, idx=idx: v.tensor_tensor_scan(
                    out=ub2[:, bi * TSC:(bi + 1) * TSC],
                    data0=dA2[:, bi * TSC:(bi + 1) * TSC],
                    data1=ub2[:, bi * TSC:(bi + 1) * TSC],
                    initial=state[:, idx:idx + 1],
                    op0=ALU.mult, op1=ALU.add), inc="SCN", sw=False)
                scn_a[L] = s.n("SCN")
                s.wait("tensor", "SCN", scn_a[L])
                if n == 0 and g >= 1:
                    s.wait("tensor", "AC", ac_evac[g - 1])
                for qq in range(4):
                    s.op("tensor", lambda e, bi=bi, qq=qq, n=n: e.matmul(
                        out=ps_y[:, qq * 512:(qq + 1) * 512],
                        lhsT=ident[:],
                        rhs=ub2[:, bi * TSC + qq * 512:
                                bi * TSC + (qq + 1) * 512],
                        start=(n == 0), stop=(n == NST - 1)))
                te_a[L] = s.n("TE")
            if sc < NSC - 1:
                s.wait("scalar", "SCN", scn_a[NL - 1])
                s.op("scalar", lambda e: e.activation(
                    out=state[:, NL - 1:NL],
                    in_=ub2[:, (((NL - 1) % 2) + 1) * TSC - 1:
                            (((NL - 1) % 2) + 1) * TSC],
                    func=AF.Copy, bias=0.0), sw=False)
            emit_evac(MD - 1)

            for ti in range(NTILE):
                s.wait("sync", "TE")
                s.wait("sync", "LD")
                s.dma("sync", lambda e, ti=ti: e.dma_start(
                    out=xdT[:].rearrange("p (g c) -> p g c", g=MD),
                    in_=y3_d.ap().rearrange("(g p) t -> p g t", p=128)
                    [:, :, t0 + ti * C:t0 + (ti + 1) * C]), "LD")
                s.wait("tensor", "LD")
                s.wait("tensor", "AC")
                for mt in range(KD):
                    pb = ps_r[mt % 4]
                    if mt >= 4:
                        s.wait("tensor", "AC")
                    for kt in range(MD):
                        s.op("tensor", lambda e, mt=mt, kt=kt, pb=pb: e.matmul(
                            out=pb[:], lhsT=wout_s[:, kt * DMODEL + mt * 128:
                                                   kt * DMODEL + mt * 128 + 128],
                            rhs=xdT[:, kt * C:(kt + 1) * C],
                            start=(kt == 0), stop=(kt == MD - 1)))
                    s.wait("scalar", "TE")
                    s.op("scalar", lambda e, mt=mt, pb=pb: e.activation(
                        out=fout[:, mt * C:(mt + 1) * C], in_=pb[:],
                        func=AF.Copy, bias=0.0), sw=False)
                for tj in range(NTILE):
                    stg = [qsg, qau][tj % 2]
                    s.wait("tensor", "AC")
                    for mt in range(KD):
                        s.op("tensor", lambda e, mt=mt, tj=tj: e.transpose(
                            out=ps_t2[:, mt * 128:(mt + 1) * 128],
                            in_=fout[:, mt * C + tj * 128:
                                     mt * C + tj * 128 + 128],
                            identity=ident[:]))
                    s.wait("scalar", "TE")
                    s.wait("scalar", "GD")
                    s.op("scalar", lambda e, stg=stg: e.activation(
                        out=stg[:], in_=ps_t2[:], func=AF.Copy, bias=0.0),
                        sw=False)
                    tloc = ti * C + tj * 128
                    slot, j0 = divmod(tloc, 1024)
                    s.wait("gpsimd", "AC")
                    s.dma("gpsimd", lambda g, slot=slot, j0=j0, stg=stg:
                          g.dma_start(
                              out=pay_in[sc][slot, j0:j0 + 128, :],
                              in_=stg[:], accum_op=ALU.add), "GD")

        def emit_RS(sc):
            s.wait("gpsimd", "GD")
            s.wait("gpsimd", "LD")
            s.op("gpsimd", lambda g: g.collective_compute(
                "ReduceScatter", ALU.add,
                replica_groups=[[0, 1], [2, 3], [4, 5], [6, 7]],
                ins=[pay_in[sc].ap().opt()],
                outs=[pay_out[sc].ap().opt()]), "CC")

        def emit_C_s1(sc, ch):
            p3 = pay_out[sc].ap().rearrange("(t p) d -> p t d", p=128)
            s.wait("sync", "CC", 2 * sc + 2)
            emit_ffn_s1(p3, ch * NTILE, (sc * 2 + ch) % 2)

        def emit_C_s3(sc, ch):
            def epilogue(ti):
                s.wait("sync", "VE")
                s.dma("sync", lambda e, ti=ti: e.dma_start(
                    out=xa[:],
                    in_=pay_out[sc][ch * C + ti * 128:ch * C + (ti + 1) * 128,
                                    :]), "LD")
                s.wait("vector", "ODMA")
                s.wait("vector", "LD")
                s.op("vector", lambda v: v.tensor_add(
                    out=x1t[:], in0=ps_t2[:], in1=xa[:]), sw=False)
                j0 = sc * 1024 + ch * C + ti * 128
                s.wait("sync", "VE")
                s.dma("sync", lambda e, j0=j0: e.dma_start(
                    out=out_d[j0:j0 + 128, :], in_=x1t[:]), "ODMA")

            emit_ffn_s3(epilogue)

        def emit_prep():
            def ld(out_ap, in_ap):
                s.dma("gpsimd", lambda g, o=out_ap, i=in_ap:
                      g.dma_start(out=o, in_=i), "GD")
            for kt in range(KD):
                ld(win_s[:, kt * 2 * DH:(kt + 1) * 2 * DH],
                   win_in[kt * 128:(kt + 1) * 128, :])
            for kt in range(MD):
                ld(wout_s[:, kt * DMODEL:(kt + 1) * DMODEL],
                   wout_in[kt * 128:(kt + 1) * 128, :])
                ld(wx_s[:, kt * 2 * NST:(kt + 1) * 2 * NST],
                   wx_in[kt * 128:(kt + 1) * 128, :])
            ld(wdt_s[:, :], wdt_in)
            ld(ident[:, :], ident_in)
            for kt in range(MD):
                for tap in range(4):
                    ld(convw_c[:, tap * MD + kt:tap * MD + kt + 1],
                       convw_in[tap, kt * 128:(kt + 1) * 128][:, None])
                ld(convb_c[:, kt:kt + 1], convb_in[kt * 128:(kt + 1) * 128, :])
                ld(bdt_c[:, kt:kt + 1], bdt_in[kt * 128:(kt + 1) * 128, :])
                ld(dvec_c[:, kt:kt + 1], dvec_in[kt * 128:(kt + 1) * 128, :])
                ld(acol_c[:, kt * NST:(kt + 1) * NST],
                   alog_in[kt * 128:(kt + 1) * 128, :])
            for mt in range(KD):
                ld(fbias_c[:, mt:mt + 1], fb_in[mt * 128:(mt + 1) * 128, :])
            s.op("vector", lambda v: v.memset(state[:], 0.0))
            s.op("vector", lambda v: v.memset(xproj[:], 0.0))
            s.wait("scalar", "GD")
            s.op("scalar", lambda e: e.activation(out=acol_c[:], in_=acol_c[:],
                                                  func=AF.Exp, bias=0.0))
            s.wait("vector", "AC")
            s.op("vector", lambda v: v.tensor_single_scalar(
                out=acol_c[:], in_=acol_c[:], scalar=-1.0, op=ALU.mult))
            s.op("vector", lambda v: v.tensor_single_scalar(
                out=fbias_c[:], in_=fbias_c[:], scalar=0.5, op=ALU.mult))
            for mt in range(KD):
                s.dma("sync", lambda e, mt=mt: e.dma_start(
                    out=xa[:], in_=fw_in[mt * 128:(mt + 1) * 128, :]), "LD")
                s.wait("vector", "LD")
                s.wait("scalar", "LD")
                emit_quant(xa, xdt, wscale=True)
                s.wait("tensor", "VE")
                for kt in range(KD):
                    s.op("tensor", lambda e, kt=kt: e.transpose(
                        out=ps_t[:, kt * 128:(kt + 1) * 128],
                        in_=xdt[:, kt * 128:(kt + 1) * 128], identity=ident[:]))
                s.wait("scalar", "TE")
                for kt in range(KD):
                    s.op("scalar", lambda e, kt=kt, mt=mt: e.activation(
                        out=wd_s[:, kt * DMODEL + mt * 128:
                                 kt * DMODEL + mt * 128 + 128],
                        in_=ps_t[:, kt * 128:(kt + 1) * 128],
                        func=AF.Copy, bias=0.0))
                barrier()

        emit_prep()
        for sc in range(NSC):
            for ch in range(NCH):
                emit_A_s1(sc, ch)
                if ch > 0:
                    emit_A_s3(sc, ch - 1)
                    emit_A_ln2(sc, ch - 1)
                    emit_phaseB1_chunk(sc, ch - 1)
                emit_ffn_s2(ch % 2)
            emit_A_s3(sc, NCH - 1)
            emit_A_ln2(sc, NCH - 1)
            emit_phaseB1_chunk(sc, NCH - 1)
            emit_AR(sc)
            emit_phaseB2_scroll(sc)
            emit_RS(sc)
        prevc = None
        for sc in range(NSC):
            for ch in range(2):
                emit_C_s1(sc, ch)
                if prevc is not None:
                    emit_C_s3(*prevc)
                emit_ffn_s2((sc * 2 + ch) % 2)
                prevc = (sc, ch)
        emit_C_s3(*prevc)
        s.wait("gpsimd", "ODMA")

        @block.sync
        def _(e):
            s.replay("sync", e, sems)

        @block.vector
        def _(e):
            s.replay("vector", e, sems)

        @block.scalar
        def _(e):
            s.replay("scalar", e, sems)

        @block.tensor
        def _(e):
            s.replay("tensor", e, sems)

        @block.gpsimd
        def _(e):
            s.replay("gpsimd", e, sems)

    return nc


def kernel(L0_attn_x, L0_attn_W_in, L0_attn_conv_w, L0_attn_conv_b, L0_attn_W_x,
           L0_attn_W_dt, L0_attn_b_dt, L0_attn_A_log, L0_attn_D, L0_attn_W_out,
           L0_ffn_weight, L0_ffn_bias):
    x = np.ascontiguousarray(np.asarray(L0_attn_x, dtype=np.float32))
    W_in = np.ascontiguousarray(np.asarray(L0_attn_W_in, dtype=np.float32))
    conv_w = np.ascontiguousarray(np.asarray(L0_attn_conv_w, dtype=np.float32))
    conv_b = np.ascontiguousarray(np.asarray(L0_attn_conv_b, dtype=np.float32))
    W_x = np.ascontiguousarray(np.asarray(L0_attn_W_x, dtype=np.float32))
    W_dt = np.ascontiguousarray(np.asarray(L0_attn_W_dt, dtype=np.float32))
    b_dt = np.ascontiguousarray(np.asarray(L0_attn_b_dt, dtype=np.float32))
    A_log = np.ascontiguousarray(np.asarray(L0_attn_A_log, dtype=np.float32))
    Dv = np.ascontiguousarray(np.asarray(L0_attn_D, dtype=np.float32))
    W_out = np.ascontiguousarray(np.asarray(L0_attn_W_out, dtype=np.float32))
    fw = np.ascontiguousarray(np.asarray(L0_ffn_weight, dtype=np.float32))
    fb = np.ascontiguousarray(np.asarray(L0_ffn_bias, dtype=np.float32))
    ident = np.eye(128, dtype=ml_dtypes.bfloat16)

    in_maps = []
    for r in range(8):
        b, h = r >> 1, r & 1
        sl = slice(h * DH, (h + 1) * DH)
        in_maps.append({
            "x_b": x[b],
            "win_sl": np.ascontiguousarray(
                np.concatenate([W_in[:, h * DH:(h + 1) * DH],
                                W_in[:, DIN + h * DH:DIN + (h + 1) * DH]],
                               axis=1)),
            "convw_sl": np.ascontiguousarray(conv_w[sl].T),
            "convb_sl": np.ascontiguousarray(conv_b[sl, None]),
            "wx_sl": np.ascontiguousarray(W_x[sl]),
            "wdt_sl": np.ascontiguousarray(W_dt[:, sl]),
            "bdt_sl": np.ascontiguousarray(b_dt[sl, None]),
            "alog_sl": np.ascontiguousarray(A_log[sl]),
            "dvec_sl": np.ascontiguousarray(Dv[sl, None]),
            "wout_sl": np.ascontiguousarray(W_out[sl]),
            "ffnw": fw,
            "ffnb": fb[:, None],
            "ident": ident,
        })

    nc = build_program()
    res = run_bass_kernel_spmd(nc, in_maps, core_ids=list(range(8)))

    out = np.empty((4, T, DMODEL), dtype=np.float32)
    for r in range(8):
        b, h = r >> 1, r & 1
        o = res.results[r]["out_r"]
        for k in range(NSC):
            out[b, k * TSC + h * 1024:k * TSC + (h + 1) * 1024, :] = \
                o[k * 1024:(k + 1) * 1024, :]
    return out


# version 11 (final)
# speedup vs baseline: 1.0749x; 1.0109x over previous
import numpy as np
import ml_dtypes
import concourse.bass as bass
import concourse.mybir as mybir
from concourse.bass_utils import run_bass_kernel_spmd

F32 = mybir.dt.float32
BF16 = mybir.dt.bfloat16
FP8 = mybir.dt.float8e4
U8 = mybir.dt.uint8
AF = mybir.ActivationFunctionType
ALU = mybir.AluOpType
AXX = mybir.AxisListType.X

DMODEL = 1024
DIN = 2048
DH = 1024
NST = 16
T = 8192
NSC = 4
TSC = T // NSC
NCH = 4
C = TSC // NCH
NTILE = C // 128
KD = DMODEL // 128
MD = DH // 128
NBQ = DMODEL // 32

ENGS = ("sync", "vector", "scalar", "tensor", "gpsimd")
DEF_SEM = {"vector": "VE", "scalar": "AC", "tensor": "TE"}


class Sched:
    def __init__(self):
        self.q = {e: [] for e in ENGS}
        self.cnt = {}
        self.last = {}

    def op(self, eng, fn, inc="default", k=1, sw=True):
        if inc == "default":
            inc = DEF_SEM.get(eng)
        if sw and eng in ("vector", "scalar"):
            prev = self.last.get(eng)
            if prev is not None:
                self.q[eng].append(["wait", prev[0], prev[1]])
        self.q[eng].append(["op", fn, inc, k])
        if inc is not None:
            self.cnt[inc] = self.cnt.get(inc, 0) + k
            self.last[eng] = (inc, self.cnt[inc])

    def dma(self, eng, fn, inc):
        self.op(eng, fn, inc, 16)

    def nop(self, eng, cycles=512):
        pass

    def wait(self, eng, sem, val=None):
        if val is None:
            val = self.cnt.get(sem, 0)
        self.q[eng].append(["wait", sem, val])

    def n(self, sem):
        return self.cnt.get(sem, 0)

    def replay(self, eng_name, eng, sems):
        for item in self.q[eng_name]:
            if item[0] == "op":
                _, fn, inc, k = item
                inst = fn(eng)
                if inc is not None:
                    inst.then_inc(sems[inc], k)
            else:
                _, sem, val = item
                if val > 0:
                    eng.wait_ge(sems[sem], val)


def build_program():
    nc = bass.Bass("TRN2", target_bir_lowering=False)

    x_in = nc.dram_tensor("x_b", [T, DMODEL], F32, kind="ExternalInput").ap()
    win_in = nc.dram_tensor("win_sl", [DMODEL, 2 * DH], F32, kind="ExternalInput").ap()
    convw_in = nc.dram_tensor("convw_sl", [4, DH], F32, kind="ExternalInput").ap()
    convb_in = nc.dram_tensor("convb_sl", [DH, 1], F32, kind="ExternalInput").ap()
    wx_in = nc.dram_tensor("wx_sl", [DH, 2 * NST], F32, kind="ExternalInput").ap()
    wdt_in = nc.dram_tensor("wdt_sl", [NST, DH], F32, kind="ExternalInput").ap()
    bdt_in = nc.dram_tensor("bdt_sl", [DH, 1], F32, kind="ExternalInput").ap()
    alog_in = nc.dram_tensor("alog_sl", [DH, NST], F32, kind="ExternalInput").ap()
    dvec_in = nc.dram_tensor("dvec_sl", [DH, 1], F32, kind="ExternalInput").ap()
    wout_in = nc.dram_tensor("wout_sl", [DH, DMODEL], F32, kind="ExternalInput").ap()
    fw_in = nc.dram_tensor("ffnw", [DMODEL, DMODEL], F32, kind="ExternalInput").ap()
    fb_in = nc.dram_tensor("ffnb", [DMODEL, 1], F32, kind="ExternalInput").ap()
    ident_in = nc.dram_tensor("ident", [128, 128], BF16, kind="ExternalInput").ap()
    out_d = nc.dram_tensor("out_r", [NSC * 1024, DMODEL], F32,
                           kind="ExternalOutput").ap()

    pay_in = [nc.dram_tensor(f"pay_in{k}", [2, 1024, DMODEL], F32, kind="Internal")
              for k in range(NSC)]
    pay_out = [nc.dram_tensor(f"pay_out{k}", [1024, DMODEL], F32, kind="Internal")
               for k in range(NSC)]
    ar_in = [nc.dram_tensor(f"ar_in{k}", [2 * NST, TSC], F32, kind="Internal")
             for k in range(NSC)]
    ar_out = [nc.dram_tensor(f"ar_out{k}", [2 * NST, TSC], F32, kind="Internal")
              for k in range(NSC)]
    xconv_d = nc.dram_tensor("xconv_d", [DH, T], BF16, kind="Internal")
    zsil_d = nc.dram_tensor("zsil_d", [DH, T], BF16, kind="Internal")
    y3_d = nc.dram_tensor("y3_d", [DH, T], BF16, kind="Internal")
    Bd = nc.dram_tensor("Bd", [NST, TSC], BF16, kind="Internal")

    s = Sched()

    from contextlib import ExitStack
    with ExitStack() as st:
        win_s = st.enter_context(nc.sbuf_tensor("win_s", [128, KD * 2 * DH], BF16))
        wout_s = st.enter_context(nc.sbuf_tensor("wout_s", [128, MD * DMODEL], BF16))
        wd_s = st.enter_context(nc.sbuf_tensor("wd_s", [128, KD * DMODEL], BF16))
        wx_s = st.enter_context(nc.sbuf_tensor("wx_s", [128, MD * 2 * NST], BF16))
        wdt_s = st.enter_context(nc.sbuf_tensor("wdt_s", [NST, DH], BF16))
        ident = st.enter_context(nc.sbuf_tensor("ident_s", [128, 128], BF16))
        convw_c = st.enter_context(nc.sbuf_tensor("convw_c", [128, 4 * MD], F32))
        convb_c = st.enter_context(nc.sbuf_tensor("convb_c", [128, MD], F32))
        bdt_c = st.enter_context(nc.sbuf_tensor("bdt_c", [128, MD], F32))
        dvec_c = st.enter_context(nc.sbuf_tensor("dvec_c", [128, MD], F32))
        acol_c = st.enter_context(nc.sbuf_tensor("acol_c", [128, MD * NST], F32))
        fbias_c = st.enter_context(nc.sbuf_tensor("fbias_c", [128, KD], F32))
        qsc = st.enter_context(nc.sbuf_tensor("qsc", [128, NBQ], F32))
        qrcp = st.enter_context(nc.sbuf_tensor("qrcp", [128, NBQ], F32))
        qu = st.enter_context(nc.sbuf_tensor("qu", [128, DMODEL], F32))
        qau = st.enter_context(nc.sbuf_tensor("qau", [128, DMODEL], F32))
        qsg = st.enter_context(nc.sbuf_tensor("qsg", [128, DMODEL], F32))
        qb8 = st.enter_context(nc.sbuf_tensor("qb8", [128, DMODEL], FP8))
        qc8 = st.enter_context(nc.sbuf_tensor("qc8", [128, DMODEL], FP8))
        qm = st.enter_context(nc.sbuf_tensor("qm", [128, DMODEL], U8))
        xa = st.enter_context(nc.sbuf_tensor("xa", [128, DMODEL], F32))
        x1t = st.enter_context(nc.sbuf_tensor("x1t", [128, DMODEL], F32))
        xdt = st.enter_context(nc.sbuf_tensor("xdt", [128, DMODEL], BF16))
        lns1 = st.enter_context(nc.sbuf_tensor("lns1", [128, NTILE], F32))
        lns2 = st.enter_context(nc.sbuf_tensor("lns2", [128, NTILE], F32))
        lnm = st.enter_context(nc.sbuf_tensor("lnm", [128, NTILE], F32))
        lnv = st.enter_context(nc.sbuf_tensor("lnv", [128, NTILE], F32))
        lnr = st.enter_context(nc.sbuf_tensor("lnr", [128, NTILE], F32))
        xdT = st.enter_context(nc.sbuf_tensor("xdT", [128, KD * C], BF16))
        fout = st.enter_context(nc.sbuf_tensor("fout", [128, KD * C], BF16))
        xn2T = st.enter_context(nc.sbuf_tensor("xn2T", [128, KD * C], BF16))
        xproj = st.enter_context(nc.sbuf_tensor("xproj", [128, MD * (4 + C)], BF16))
        zs2 = st.enter_context(nc.sbuf_tensor("zs2", [128, MD * C], BF16))
        dtin16 = st.enter_context(nc.sbuf_tensor("dtin16", [NST, TSC], BF16))
        cacc = st.enter_context(nc.sbuf_tensor("cacc", [128, 2 * C], F32))
        xcb = st.enter_context(nc.sbuf_tensor("xcb", [128, 2 * C], BF16))
        xcg = st.enter_context(nc.sbuf_tensor("xcg", [128, 2 * TSC], BF16))
        zsg = st.enter_context(nc.sbuf_tensor("zsg", [128, 2 * TSC], BF16))
        dts = st.enter_context(nc.sbuf_tensor("dts", [128, 2 * TSC], BF16))
        wgb = st.enter_context(nc.sbuf_tensor("wgb", [128, TSC], BF16))
        dA2 = st.enter_context(nc.sbuf_tensor("dA2", [128, 2 * TSC], F32))
        ub2 = st.enter_context(nc.sbuf_tensor("ub2", [128, 2 * TSC], BF16))
        Bb2 = st.enter_context(nc.sbuf_tensor("Bb2", [128, 2 * TSC], BF16))
        ygb = st.enter_context(nc.sbuf_tensor("ygb", [128, TSC], BF16))
        state = st.enter_context(nc.sbuf_tensor("state", [128, MD * NST], F32))
        ps_t = st.enter_context(nc.psum_tensor("ps_t", [128, 1024], BF16))
        ps_t2 = ps_t
        ps_f0 = st.enter_context(nc.psum_tensor("ps_f0", [128, 512], F32))
        ps_f1 = st.enter_context(nc.psum_tensor("ps_f1", [128, 512], F32))
        ps_dt = st.enter_context(nc.psum_tensor("ps_dt", [128, 512], F32))
        ps_x = ps_dt
        ps_y = st.enter_context(nc.psum_tensor("ps_y", [128, TSC], F32))
        sem_LD = st.enter_context(nc.semaphore("LD"))
        sem_GD = st.enter_context(nc.semaphore("GD"))
        sem_VE = st.enter_context(nc.semaphore("VE"))
        sem_AC = st.enter_context(nc.semaphore("AC"))
        sem_TE = st.enter_context(nc.semaphore("TE"))
        sem_CC = st.enter_context(nc.semaphore("CC"))
        sem_DA = st.enter_context(nc.semaphore("DA"))
        sem_SCN = st.enter_context(nc.semaphore("SCN"))
        sem_ODMA = st.enter_context(nc.semaphore("ODMA"))
        sem_PL = st.enter_context(nc.semaphore("PL"))
        block = st.enter_context(nc.Block())
        xh = qsg
        payt = qau
        sems = {"LD": sem_LD, "GD": sem_GD, "VE": sem_VE, "AC": sem_AC,
                "TE": sem_TE, "CC": sem_CC, "DA": sem_DA, "SCN": sem_SCN,
                "ODMA": sem_ODMA, "PL": sem_PL}
        ps_f = [ps_f0, ps_f1]
        ps_r = [ps_f0, ps_f1, ps_y[:, 0:512], ps_y[:, 512:1024]]

        def barrier():
            for e in ENGS:
                for sname in ("LD", "GD", "VE", "AC", "TE", "CC", "DA",
                              "SCN", "PL"):
                    s.wait(e, sname)

        def emit_quant(src, dst, wscale=False):
            v3 = src[:].rearrange("p (s n) -> p s n", n=32)
            s.op("vector", lambda v: v.tensor_reduce(
                out=qsc[:], in_=v3, axis=AXX, op=ALU.max,
                apply_absolute_value=True))
            sc2 = (1.0 / 7.5) * (0.5 if wscale else 1.0)
            s.op("vector", lambda v: v.tensor_scalar(
                out=qsc[:], in0=qsc[:], scalar1=sc2, op0=ALU.mult,
                scalar2=(0.5e-30 if wscale else 1e-30), op1=ALU.max))
            s.op("vector", lambda v: v.reciprocal(out=qrcp[:], in_=qsc[:]))
            if wscale:
                s.op("vector", lambda v: v.tensor_single_scalar(
                    out=qrcp[:], in_=qrcp[:], scalar=0.5, op=ALU.mult))
            s.op("vector", lambda v: v.tensor_tensor(
                out=qu[:].rearrange("p (s n) -> p s n", n=32), in0=v3,
                in1=qrcp[:].broadcast_to([128, NBQ, 32]), op=ALU.mult))
            s.wait("scalar", "VE")
            s.wait("scalar", "LD")
            s.op("scalar", lambda e: e.activation(out=qau[:], in_=qu[:],
                                                  func=AF.Abs, bias=0.0),
                 sw=False)
            s.op("scalar", lambda e: e.activation(out=qsg[:], in_=qu[:],
                                                  func=AF.Sign, bias=0.0),
                 sw=False)
            s.op("scalar", lambda e: e.activation(out=qb8[:], in_=qau[:],
                                                  func=AF.Copy, bias=0.0),
                 sw=False)
            s.op("scalar", lambda e: e.activation(out=qc8[:], in_=qau[:],
                                                  func=AF.Copy, bias=1.0),
                 sw=False)
            s.op("scalar", lambda e: e.activation(out=qu[:], in_=qc8[:],
                                                  func=AF.Copy, bias=-1.0),
                 sw=False)
            s.wait("vector", "AC")
            s.op("vector", lambda v: v.tensor_single_scalar(
                out=qm[:], in_=qau[:], scalar=1.0, op=ALU.is_lt), sw=False)
            s.op("vector", lambda v: v.tensor_copy(out=qau[:], in_=qb8[:]),
                 sw=False)
            s.op("vector", lambda v: v.copy_predicated(out=qau[:], mask=qm[:],
                                                       data=qu[:]), sw=False)
            s.op("vector", lambda v: v.tensor_mul(out=qsg[:], in0=qau[:],
                                                  in1=qsg[:]), sw=False)
            s.op("vector", lambda v: v.tensor_tensor(
                out=dst[:].rearrange("p (s n) -> p s n", n=32),
                in0=qsg[:].rearrange("p (s n) -> p s n", n=32),
                in1=qsc[:].broadcast_to([128, NBQ, 32]), op=ALU.mult),
                sw=False)

        def emit_ln_stats(src3, sq3):
            s.op("vector", lambda v: v.tensor_reduce(
                out=lns1[:, 0:NTILE], in_=src3, axis=AXX, op=ALU.add))
            s.op("scalar", lambda e: e.activation(
                out=sq3, in_=src3, func=AF.Square, bias=0.0))
            s.wait("vector", "AC")
            s.op("vector", lambda v: v.tensor_reduce(
                out=lns2[:, 0:NTILE], in_=sq3, axis=AXX, op=ALU.add),
                sw=False)
            s.op("vector", lambda v: v.tensor_single_scalar(
                out=lnm[:, 0:NTILE], in_=lns1[:, 0:NTILE],
                scalar=1.0 / DMODEL, op=ALU.mult))
            s.op("vector", lambda v: v.tensor_single_scalar(
                out=lnv[:, 0:NTILE], in_=lns2[:, 0:NTILE],
                scalar=1.0 / DMODEL, op=ALU.mult))
            s.op("vector", lambda v: v.tensor_mul(
                out=lns1[:, 0:NTILE], in0=lnm[:, 0:NTILE],
                in1=lnm[:, 0:NTILE]))
            s.op("vector", lambda v: v.tensor_sub(
                out=lnv[:, 0:NTILE], in0=lnv[:, 0:NTILE],
                in1=lns1[:, 0:NTILE]))
            s.op("vector", lambda v: v.tensor_single_scalar(
                out=lnv[:, 0:NTILE], in_=lnv[:, 0:NTILE], scalar=1e-5,
                op=ALU.add))
            s.wait("scalar", "VE")
            s.op("scalar", lambda e: e.activation(
                out=lnv[:, 0:NTILE], in_=lnv[:, 0:NTILE], func=AF.Ln,
                bias=0.0))
            s.wait("vector", "AC")
            s.op("vector", lambda v: v.tensor_single_scalar(
                out=lnv[:, 0:NTILE], in_=lnv[:, 0:NTILE], scalar=-0.5,
                op=ALU.mult))
            s.wait("scalar", "VE")
            s.op("scalar", lambda e: e.activation(
                out=lnr[:, 0:NTILE], in_=lnv[:, 0:NTILE], func=AF.Exp,
                bias=0.0))

        xck = dA2
        xck3 = dA2[:].rearrange("p (t d) -> p t d", d=DMODEL)
        sqA3 = Bb2[:].rearrange("p (t d) -> p t d", d=DMODEL)
        sqB3 = xcg[:].rearrange("p (t d) -> p t d", d=DMODEL)
        x1ck = ub2
        x1ck3 = ub2[:].rearrange("p (t d) -> p t d", d=DMODEL)

        def xdT_r(par, kt):
            if par == 0:
                return xdT[:, kt * C:(kt + 1) * C]
            if kt < 4:
                return wgb[:, kt * C:(kt + 1) * C]
            return ygb[:, (kt - 4) * C:(kt - 3) * C]

        def xdT_w(par, ti):
            if par == 0:
                s.op("scalar", lambda e, ti=ti: e.activation(
                    out=xdT[:].rearrange("p (k c) -> p k c", k=KD)
                    [:, :, ti * 128:(ti + 1) * 128],
                    in_=ps_t[:].rearrange("p (k c) -> p k c", k=KD),
                    func=AF.Copy, bias=0.0))
            else:
                s.op("scalar", lambda e, ti=ti: e.activation(
                    out=wgb[:].rearrange("p (k c) -> p k c", k=4)
                    [:, :, ti * 128:(ti + 1) * 128],
                    in_=ps_t[:, 0:512].rearrange("p (k c) -> p k c", k=4),
                    func=AF.Copy, bias=0.0))
                s.op("scalar", lambda e, ti=ti: e.activation(
                    out=ygb[:].rearrange("p (k c) -> p k c", k=4)
                    [:, :, ti * 128:(ti + 1) * 128],
                    in_=ps_t[:, 512:1024].rearrange("p (k c) -> p k c", k=4),
                    func=AF.Copy, bias=0.0), sw=False)

        xck_free = [0, 0]

        def emit_ffn_s1(src_dram3, idx, par):
            s.wait("sync", "VE", xck_free[0])
            s.wait("sync", "AC", xck_free[1])
            s.dma("sync", lambda e: e.dma_start(
                out=xck3, in_=src_dram3[:, idx:idx + NTILE, :]), "LD")
            s.wait("vector", "LD")
            s.wait("scalar", "LD")
            s.wait("vector", "GD")
            emit_ln_stats(xck3, sqA3)
            xck_free[1] = s.n("AC")
            for ti in range(NTILE):
                if ti == 0:
                    s.wait("vector", "AC")
                s.op("vector", lambda v, ti=ti: v.tensor_scalar(
                    out=xh[:], in0=xck3[:, ti, :], scalar1=lnm[:, ti:ti + 1],
                    op0=ALU.subtract, scalar2=lnr[:, ti:ti + 1],
                    op1=ALU.mult))
                if ti == NTILE - 1:
                    xck_free[0] = s.n("VE")
                emit_quant(xh, xdt)
                s.wait("tensor", "VE")
                s.wait("tensor", "AC")
                for kt in range(KD):
                    s.op("tensor", lambda e, kt=kt: e.transpose(
                        out=ps_t[:, kt * 128:(kt + 1) * 128],
                        in_=xdt[:, kt * 128:(kt + 1) * 128], identity=ident[:]))
                s.wait("scalar", "TE")
                if par == 1:
                    s.wait("scalar", "VE")
                    s.wait("scalar", "LD")
                xdT_w(par, ti)

        def emit_ffn_s2(par):
            s.wait("tensor", "AC")
            for mt in range(KD):
                pb = ps_r[mt % 4]
                if mt >= 4:
                    s.wait("tensor", "AC")
                for kt in range(KD):
                    s.op("tensor", lambda e, mt=mt, kt=kt, pb=pb: e.matmul(
                        out=pb[:], lhsT=wd_s[:, kt * DMODEL + mt * 128:
                                             kt * DMODEL + mt * 128 + 128],
                        rhs=xdT_r(par, kt),
                        start=(kt == 0), stop=(kt == KD - 1)))
                s.wait("scalar", "TE")
                s.op("scalar", lambda e, mt=mt, pb=pb: e.activation(
                    out=fout[:, mt * C:(mt + 1) * C], in_=pb[:],
                    func=AF.Identity, bias=fbias_c[:, mt:mt + 1]), sw=False)

        def emit_ffn_s3(epilogue):
            for ti in range(NTILE):
                s.wait("tensor", "AC")
                s.wait("tensor", "VE")
                for mt in range(KD):
                    s.op("tensor", lambda e, mt=mt, ti=ti: e.transpose(
                        out=ps_t2[:, mt * 128:(mt + 1) * 128],
                        in_=fout[:, mt * C + ti * 128:mt * C + ti * 128 + 128],
                        identity=ident[:]))
                s.wait("vector", "TE")
                epilogue(ti)

        x3src = x_in.rearrange("(t p) d -> p t d", p=128)

        def emit_A_s1(sc, ch):
            emit_ffn_s1(x3src, (sc * TSC + ch * C) // 128, ch % 2)

        def emit_A_s3(sc, ch):
            g0 = sc * TSC + ch * C

            def epilogue(ti):
                s.wait("sync", "VE")
                s.dma("sync", lambda e, ti=ti: e.dma_start(
                    out=xa[:],
                    in_=x_in[g0 + ti * 128:g0 + (ti + 1) * 128, :]), "LD")
                if ti == 0:
                    s.wait("vector", "AC")
                s.wait("vector", "LD")
                s.op("vector", lambda v: v.tensor_add(
                    out=x1t[:], in0=ps_t2[:], in1=xa[:]))
                s.wait("vector", "LD")
                s.op("vector", lambda v: v.tensor_single_scalar(
                    out=qu[:], in_=x1t[:], scalar=0.5, op=ALU.mult))
                tloc = ch * C + ti * 128
                slot, j0 = divmod(tloc, 1024)
                s.wait("sync", "VE")
                s.dma("sync", lambda e, slot=slot, j0=j0: e.dma_start(
                    out=pay_in[sc][slot, j0:j0 + 128, :], in_=qu[:]), "LD")
                s.op("vector", lambda v, ti=ti: v.tensor_copy(
                    out=x1ck[:, ti * DMODEL:(ti + 1) * DMODEL], in_=x1t[:]),
                    sw=False)

            emit_ffn_s3(epilogue)

        def emit_A_ln2(sc, ch):
            s.wait("scalar", "VE")
            emit_ln_stats(x1ck3, sqB3)
            for ti in range(NTILE):
                s.wait("vector", "AC")
                s.op("vector", lambda v, ti=ti: v.tensor_scalar(
                    out=xdt[:], in0=x1ck3[:, ti, :],
                    scalar1=lnm[:, ti:ti + 1], op0=ALU.subtract,
                    scalar2=lnr[:, ti:ti + 1], op1=ALU.mult))
                s.wait("tensor", "VE")
                s.wait("tensor", "AC")
                for kt in range(KD):
                    s.op("tensor", lambda e, kt=kt: e.transpose(
                        out=ps_t[:, kt * 128:(kt + 1) * 128],
                        in_=xdt[:, kt * 128:(kt + 1) * 128], identity=ident[:]))
                s.wait("scalar", "TE")
                s.op("scalar", lambda e, ti=ti: e.activation(
                    out=xn2T[:].rearrange("p (k c) -> p k c", k=KD)
                    [:, :, ti * 128:(ti + 1) * 128],
                    in_=ps_t[:].rearrange("p (k c) -> p k c", k=KD),
                    func=AF.Copy, bias=0.0))

        def emit_phaseB1_chunk(sc, ch):
            s.wait("tensor", "AC")
            s.wait("tensor", "VE")
            s.wait("scalar", "VE")
            s.wait("scalar", "GD")
            s.wait("scalar", "LD")
            ac_xp = [0] * MD
            ac_cs = [0] * MD
            for mt in range(MD):
                pb = ps_r[mt % 4]
                if mt >= 4:
                    s.wait("tensor", "AC")
                for kt in range(KD):
                    s.op("tensor", lambda e, mt=mt, kt=kt, pb=pb: e.matmul(
                        out=pb[:], lhsT=win_s[:, kt * 2 * DH + mt * 128:
                                              kt * 2 * DH + mt * 128 + 128],
                        rhs=xn2T[:, kt * C:(kt + 1) * C],
                        start=(kt == 0), stop=(kt == KD - 1)))
                s.wait("scalar", "TE")
                s.op("scalar", lambda e, g=mt, pb=pb: e.activation(
                    out=xproj[:, g * (4 + C) + 4:g * (4 + C) + 4 + C],
                    in_=pb[:], func=AF.Copy, bias=0.0), sw=False)
                ac_xp[mt] = s.n("AC")
            for g in range(MD):
                par = g % 2
                xp = xproj[:, g * (4 + C):(g + 1) * (4 + C)]
                ca = cacc[:, par * C:(par + 1) * C]
                s.wait("vector", "AC", ac_xp[g])
                if g >= 2:
                    s.wait("vector", "AC", ac_cs[g - 2])
                s.op("vector", lambda v, xp=xp, ca=ca, g=g: v.tensor_single_scalar(
                    out=ca, in_=xp[:, 1:1 + C],
                    scalar=convw_c[:, 0 * MD + g:0 * MD + g + 1], op=ALU.mult),
                    sw=False)
                for tap in range(1, 4):
                    s.op("vector", lambda v, xp=xp, ca=ca, g=g, tap=tap:
                         v.scalar_tensor_tensor(
                             out=ca, in0=xp[:, 1 + tap:1 + tap + C],
                             scalar=convw_c[:, tap * MD + g:tap * MD + g + 1],
                             in1=ca, op0=ALU.mult, op1=ALU.add))
                s.op("vector", lambda v, xp=xp: v.tensor_copy(
                    out=xp[:, 1:4], in_=xp[:, 1 + C:4 + C]), sw=False)
                s.wait("scalar", "VE")
                s.wait("scalar", "TE")
                s.wait("scalar", "GD")
                s.op("scalar", lambda e, g=g, ca=ca, par=par: e.activation(
                    out=xcb[:, par * C:(par + 1) * C], in_=ca,
                    func=AF.Silu, bias=convb_c[:, g:g + 1]))
                ac_cs[g] = s.n("AC")
                s.wait("tensor", "AC")
                s.op("tensor", lambda e, g=g, par=par: e.matmul(
                    out=ps_x[0:32, :], lhsT=wx_s[:, g * 2 * NST:(g + 1) * 2 * NST],
                    rhs=xcb[:, par * C:(par + 1) * C],
                    start=(g == 0), stop=(g == MD - 1)))
                s.wait("gpsimd", "AC")
                s.dma("gpsimd", lambda e, g=g, par=par: e.dma_start(
                    out=xconv_d[g * 128:(g + 1) * 128,
                                sc * TSC + ch * C:sc * TSC + ch * C + C],
                    in_=xcb[:, par * C:(par + 1) * C]), "GD")
            for mt in range(MD, 2 * MD):
                pb = ps_r[mt % 4]
                s.wait("tensor", "AC")
                for kt in range(KD):
                    s.op("tensor", lambda e, mt=mt, kt=kt, pb=pb: e.matmul(
                        out=pb[:], lhsT=win_s[:, kt * 2 * DH + mt * 128:
                                              kt * 2 * DH + mt * 128 + 128],
                        rhs=xn2T[:, kt * C:(kt + 1) * C],
                        start=(kt == 0), stop=(kt == KD - 1)))
                s.wait("scalar", "TE")
                s.op("scalar", lambda e, g=mt - MD, pb=pb: e.activation(
                    out=zs2[:, g * C:(g + 1) * C],
                    in_=pb[:], func=AF.Silu, bias=0.0), sw=False)
            s.wait("gpsimd", "AC")
            for g in range(MD):
                s.dma("gpsimd", lambda e, g=g: e.dma_start(
                    out=zsil_d[g * 128:(g + 1) * 128,
                               sc * TSC + ch * C:sc * TSC + ch * C + C],
                    in_=zs2[:, g * C:(g + 1) * C]), "GD")
            s.wait("vector", "TE")
            s.op("vector", lambda v: v.tensor_copy(
                out=cacc[0:32, 0:C], in_=ps_x[0:32, :]))
            s.wait("sync", "VE")
            s.dma("sync", lambda e: e.dma_start(
                out=ar_in[sc][:, ch * C:(ch + 1) * C],
                in_=cacc[0:32, 0:C]), "LD")

        def emit_AR(sc):
            s.wait("gpsimd", "LD")
            s.op("gpsimd", lambda g: g.collective_compute(
                "AllReduce", ALU.add,
                replica_groups=[[0, 1], [2, 3], [4, 5], [6, 7]],
                ins=[ar_in[sc].ap().opt()],
                outs=[ar_out[sc].ap().opt()]), "CC")

        def emit_phaseB2_scroll(sc):
            t0 = sc * TSC
            s.wait("gpsimd", "CC")
            s.dma("gpsimd", lambda g: g.dma_start(
                out=dtin16[:, :], in_=ar_out[sc][0:NST, :]), "GD")
            s.dma("gpsimd", lambda g: g.dma_start(
                out=zs2[0:NST, 0:TSC], in_=ar_out[sc][NST:2 * NST, :]), "GD")
            s.wait("sync", "GD")
            s.dma("sync", lambda e: e.dma_start(out=Bd[:, :],
                                                in_=zs2[0:NST, 0:TSC]), "LD")
            s.wait("sync", "VE")
            s.wait("sync", "AC")
            s.dma("sync", lambda e: e.dma_start(
                out=xcg[:, 0:TSC], in_=xconv_d[0:128, t0:t0 + TSC]), "LD")
            s.dma("sync", lambda e: e.dma_start(
                out=zsg[:, 0:TSC], in_=zsil_d[0:128, t0:t0 + TSC]), "LD")
            ld_xc0 = s.n("LD")

            NL = MD * NST
            scn_a = [0] * NL
            te_a = [0] * NL
            pl_a = [0] * NL
            da_a = [0] * NL
            ld_a = [0] * NL
            ac_tail = [0] * NL
            ve_wmul = [0] * MD
            ve_evac = [0] * MD
            ac_evac = [0] * MD
            ld_y3 = [0] * MD
            ld_xc = [0] * MD
            ld_xc[0] = ld_xc0

            def emit_bcast(L):
                if L >= NL:
                    return
                bi = L % 2
                if L >= 2:
                    s.wait("sync", "PL", pl_a[L - 2])
                n = L % NST
                s.dma("sync", lambda e, n=n, bi=bi: e.dma_start(
                    out=Bb2[:, bi * TSC:(bi + 1) * TSC],
                    in_=Bd.ap()[None, n, :].broadcast_to([128, TSC])), "LD")
                ld_a[L] = s.n("LD")

            def emit_evac(g):
                pg = g % 2
                L15 = g * NST + NST - 1
                s.wait("scalar", "TE", te_a[L15])
                if g >= 1:
                    s.wait("scalar", "LD", ld_y3[g - 1])
                for qq in range(4):
                    s.op("scalar", lambda e, qq=qq: e.activation(
                        out=ygb[:, qq * 512:(qq + 1) * 512],
                        in_=ps_y[:, qq * 512:(qq + 1) * 512],
                        func=AF.Copy, bias=0.0), sw=(qq == 0))
                ac_evac[g] = s.n("AC")
                s.wait("vector", "AC", ac_evac[g])
                s.op("vector", lambda v, g=g, pg=pg: v.scalar_tensor_tensor(
                    out=ygb[:], in0=xcg[:, pg * TSC:(pg + 1) * TSC],
                    scalar=dvec_c[:, g:g + 1], in1=ygb[:],
                    op0=ALU.mult, op1=ALU.add), sw=False)
                s.op("vector", lambda v, pg=pg: v.tensor_mul(
                    out=ygb[:], in0=ygb[:], in1=zsg[:, pg * TSC:(pg + 1) * TSC]))
                ve_evac[g] = s.n("VE")
                s.wait("sync", "VE", ve_evac[g])
                s.dma("sync", lambda e, g=g: e.dma_start(
                    out=y3_d[g * 128:(g + 1) * 128, t0:t0 + TSC], in_=ygb[:]),
                    "LD")
                ld_y3[g] = s.n("LD")

            for L in range(NL):
                g, n = divmod(L, NST)
                bi = L % 2
                pg = g % 2
                idx = L
                if n == 0:
                    if g == 0:
                        emit_bcast(0)
                        emit_bcast(1)
                    else:
                        emit_evac(g - 1)
                    s.wait("tensor", "GD")
                    for q in range(4):
                        pb = ps_f[q % 2]
                        par = q % 2
                        qsl = slice(par * C, (par + 1) * C)
                        dsl = slice(pg * TSC + q * C, pg * TSC + (q + 1) * C)
                        s.wait("tensor", "AC")
                        if q >= 2:
                            s.wait("tensor", "VE")
                        s.op("tensor", lambda e, g=g, q=q, pb=pb: e.matmul(
                            out=pb[:], lhsT=wdt_s[:, g * 128:(g + 1) * 128],
                            rhs=dtin16[:, q * C:(q + 1) * C],
                            start=True, stop=True))
                        s.wait("scalar", "TE")
                        if q >= 2:
                            s.wait("scalar", "VE")
                        s.op("scalar", lambda e, g=g, pb=pb, qsl=qsl:
                             e.activation(out=qu[:, qsl], in_=pb[:],
                                          func=AF.Abs,
                                          bias=bdt_c[:, g:g + 1]), sw=False)
                        s.op("scalar", lambda e, qsl=qsl: e.activation(
                            out=qu[:, qsl], in_=qu[:, qsl],
                            func=AF.Exp, bias=0.0, scale=-1.0))
                        s.op("scalar", lambda e, qsl=qsl: e.activation(
                            out=qu[:, qsl], in_=qu[:, qsl],
                            func=AF.Ln, bias=1.0))
                        s.op("scalar", lambda e, g=g, pb=pb, dsl=dsl:
                             e.activation(out=dts[:, dsl], in_=pb[:],
                                          func=AF.Relu,
                                          bias=bdt_c[:, g:g + 1]), sw=False)
                        s.wait("vector", "AC")
                        s.op("vector", lambda v, dsl=dsl, qsl=qsl:
                             v.tensor_tensor(out=dts[:, dsl],
                                             in0=dts[:, dsl],
                                             in1=qu[:, qsl], op=ALU.add),
                             sw=False)
                    if g == 0:
                        s.wait("vector", "TE")
                    s.wait("vector", "LD", ld_xc[g])
                    s.op("vector", lambda v, pg=pg: v.tensor_mul(
                        out=wgb[:], in0=dts[:, pg * TSC:(pg + 1) * TSC],
                        in1=xcg[:, pg * TSC:(pg + 1) * TSC]), sw=True)
                    ve_wmul[g] = s.n("VE")
                    ve_dts = s.n("VE")
                    s.wait("scalar", "VE", ve_dts)
                if L >= 2:
                    s.wait("scalar", "SCN", scn_a[L - 2])
                s.op("scalar", lambda e, pg=pg, bi=bi, idx=idx: e.activation(
                    out=dA2[:, bi * TSC:(bi + 1) * TSC],
                    in_=dts[:, pg * TSC:(pg + 1) * TSC],
                    func=AF.Exp, bias=0.0,
                    scale=acol_c[:, idx:idx + 1]), inc="DA", sw=False)
                da_a[L] = s.n("DA")
                if sc < NSC - 1 and L >= 1:
                    s.wait("scalar", "SCN", scn_a[L - 1])
                    s.op("scalar", lambda e, Lp=L - 1: e.activation(
                        out=state[:, Lp:Lp + 1],
                        in_=ub2[:, ((Lp % 2) + 1) * TSC - 1:
                                ((Lp % 2) + 1) * TSC],
                        func=AF.Copy, bias=0.0), sw=False)
                ac_tail[L - 1] = s.n("AC")
                s.wait("vector", "LD", ld_a[L])
                if L >= 2:
                    s.wait("vector", "TE", te_a[L - 2])
                    s.wait("vector", "AC", ac_tail[L - 2])
                s.op("vector", lambda v, bi=bi: v.tensor_mul(
                    out=ub2[:, bi * TSC:(bi + 1) * TSC], in0=wgb[:],
                    in1=Bb2[:, bi * TSC:(bi + 1) * TSC]), inc="PL", sw=False)
                pl_a[L] = s.n("PL")
                emit_bcast(L + 2)
                if n == 2 and g + 1 < MD:
                    npg = (g + 1) % 2
                    if g >= 1:
                        s.wait("sync", "VE", ve_evac[g - 1])
                    s.dma("sync", lambda e, g=g, npg=npg: e.dma_start(
                        out=xcg[:, npg * TSC:(npg + 1) * TSC],
                        in_=xconv_d[(g + 1) * 128:(g + 2) * 128,
                                    t0:t0 + TSC]), "LD")
                    s.dma("sync", lambda e, g=g, npg=npg: e.dma_start(
                        out=zsg[:, npg * TSC:(npg + 1) * TSC],
                        in_=zsil_d[(g + 1) * 128:(g + 2) * 128,
                                   t0:t0 + TSC]), "LD")
                    ld_xc[g + 1] = s.n("LD")
                s.wait("vector", "DA", da_a[L])
                s.op("vector", lambda v, bi=bi, idx=idx: v.tensor_tensor_scan(
                    out=ub2[:, bi * TSC:(bi + 1) * TSC],
                    data0=dA2[:, bi * TSC:(bi + 1) * TSC],
                    data1=ub2[:, bi * TSC:(bi + 1) * TSC],
                    initial=state[:, idx:idx + 1],
                    op0=ALU.mult, op1=ALU.add), inc="SCN", sw=False)
                scn_a[L] = s.n("SCN")
                s.wait("tensor", "SCN", scn_a[L])
                if n == 0 and g >= 1:
                    s.wait("tensor", "AC", ac_evac[g - 1])
                for qq in range(4):
                    s.op("tensor", lambda e, bi=bi, qq=qq, n=n: e.matmul(
                        out=ps_y[:, qq * 512:(qq + 1) * 512],
                        lhsT=ident[:],
                        rhs=ub2[:, bi * TSC + qq * 512:
                                bi * TSC + (qq + 1) * 512],
                        start=(n == 0), stop=(n == NST - 1)))
                te_a[L] = s.n("TE")
            if sc < NSC - 1:
                s.wait("scalar", "SCN", scn_a[NL - 1])
                s.op("scalar", lambda e: e.activation(
                    out=state[:, NL - 1:NL],
                    in_=ub2[:, (((NL - 1) % 2) + 1) * TSC - 1:
                            (((NL - 1) % 2) + 1) * TSC],
                    func=AF.Copy, bias=0.0), sw=False)
            emit_evac(MD - 1)

            for ti in range(NTILE):
                s.wait("sync", "TE")
                s.wait("sync", "LD")
                s.dma("sync", lambda e, ti=ti: e.dma_start(
                    out=xdT[:].rearrange("p (g c) -> p g c", g=MD),
                    in_=y3_d.ap().rearrange("(g p) t -> p g t", p=128)
                    [:, :, t0 + ti * C:t0 + (ti + 1) * C]), "LD")
                s.wait("tensor", "LD")
                s.wait("tensor", "AC")
                for mt in range(KD):
                    pb = ps_r[mt % 4]
                    if mt >= 4:
                        s.wait("tensor", "AC")
                    for kt in range(MD):
                        s.op("tensor", lambda e, mt=mt, kt=kt, pb=pb: e.matmul(
                            out=pb[:], lhsT=wout_s[:, kt * DMODEL + mt * 128:
                                                   kt * DMODEL + mt * 128 + 128],
                            rhs=xdT[:, kt * C:(kt + 1) * C],
                            start=(kt == 0), stop=(kt == MD - 1)))
                    s.wait("scalar", "TE")
                    s.op("scalar", lambda e, mt=mt, pb=pb: e.activation(
                        out=fout[:, mt * C:(mt + 1) * C], in_=pb[:],
                        func=AF.Copy, bias=0.0), sw=False)
                for tj in range(NTILE):
                    stg = [qsg, qau][tj % 2]
                    s.wait("tensor", "AC")
                    for mt in range(KD):
                        s.op("tensor", lambda e, mt=mt, tj=tj: e.transpose(
                            out=ps_t2[:, mt * 128:(mt + 1) * 128],
                            in_=fout[:, mt * C + tj * 128:
                                     mt * C + tj * 128 + 128],
                            identity=ident[:]))
                    s.wait("scalar", "TE")
                    s.wait("scalar", "GD")
                    s.op("scalar", lambda e, stg=stg: e.activation(
                        out=stg[:], in_=ps_t2[:], func=AF.Copy, bias=0.0),
                        sw=False)
                    tloc = ti * C + tj * 128
                    slot, j0 = divmod(tloc, 1024)
                    s.wait("gpsimd", "AC")
                    s.dma("gpsimd", lambda g, slot=slot, j0=j0, stg=stg:
                          g.dma_start(
                              out=pay_in[sc][slot, j0:j0 + 128, :],
                              in_=stg[:], accum_op=ALU.add), "GD")

        def emit_RS(sc):
            s.wait("gpsimd", "GD")
            s.wait("gpsimd", "LD")
            s.op("gpsimd", lambda g: g.collective_compute(
                "ReduceScatter", ALU.add,
                replica_groups=[[0, 1], [2, 3], [4, 5], [6, 7]],
                ins=[pay_in[sc].ap().opt()],
                outs=[pay_out[sc].ap().opt()]), "CC")

        def emit_C_s1(sc, ch):
            p3 = pay_out[sc].ap().rearrange("(t p) d -> p t d", p=128)
            s.wait("sync", "CC", 2 * sc + 2)
            emit_ffn_s1(p3, ch * NTILE, (sc * 2 + ch) % 2)

        def emit_C_s3(sc, ch):
            def epilogue(ti):
                s.wait("sync", "VE")
                s.dma("sync", lambda e, ti=ti: e.dma_start(
                    out=xa[:],
                    in_=pay_out[sc][ch * C + ti * 128:ch * C + (ti + 1) * 128,
                                    :]), "LD")
                s.wait("vector", "ODMA")
                s.wait("vector", "LD")
                s.op("vector", lambda v: v.tensor_add(
                    out=x1t[:], in0=ps_t2[:], in1=xa[:]), sw=False)
                j0 = sc * 1024 + ch * C + ti * 128
                s.wait("sync", "VE")
                s.dma("sync", lambda e, j0=j0: e.dma_start(
                    out=out_d[j0:j0 + 128, :], in_=x1t[:]), "ODMA")

            emit_ffn_s3(epilogue)

        def emit_prep():
            def ld(out_ap, in_ap):
                s.dma("gpsimd", lambda g, o=out_ap, i=in_ap:
                      g.dma_start(out=o, in_=i), "GD")
            for kt in range(KD):
                ld(win_s[:, kt * 2 * DH:(kt + 1) * 2 * DH],
                   win_in[kt * 128:(kt + 1) * 128, :])
            for kt in range(MD):
                ld(wout_s[:, kt * DMODEL:(kt + 1) * DMODEL],
                   wout_in[kt * 128:(kt + 1) * 128, :])
                ld(wx_s[:, kt * 2 * NST:(kt + 1) * 2 * NST],
                   wx_in[kt * 128:(kt + 1) * 128, :])
            ld(wdt_s[:, :], wdt_in)
            ld(ident[:, :], ident_in)
            for kt in range(MD):
                for tap in range(4):
                    ld(convw_c[:, tap * MD + kt:tap * MD + kt + 1],
                       convw_in[tap, kt * 128:(kt + 1) * 128][:, None])
                ld(convb_c[:, kt:kt + 1], convb_in[kt * 128:(kt + 1) * 128, :])
                ld(bdt_c[:, kt:kt + 1], bdt_in[kt * 128:(kt + 1) * 128, :])
                ld(dvec_c[:, kt:kt + 1], dvec_in[kt * 128:(kt + 1) * 128, :])
                ld(acol_c[:, kt * NST:(kt + 1) * NST],
                   alog_in[kt * 128:(kt + 1) * 128, :])
            for mt in range(KD):
                ld(fbias_c[:, mt:mt + 1], fb_in[mt * 128:(mt + 1) * 128, :])
            s.op("vector", lambda v: v.memset(state[:], 0.0))
            s.op("vector", lambda v: v.memset(xproj[:], 0.0))
            s.wait("scalar", "GD")
            s.op("scalar", lambda e: e.activation(out=acol_c[:], in_=acol_c[:],
                                                  func=AF.Exp, bias=0.0))
            s.wait("vector", "AC")
            s.op("vector", lambda v: v.tensor_single_scalar(
                out=acol_c[:], in_=acol_c[:], scalar=-1.0, op=ALU.mult))
            s.op("vector", lambda v: v.tensor_single_scalar(
                out=fbias_c[:], in_=fbias_c[:], scalar=0.5, op=ALU.mult))
            for mt in range(KD):
                s.wait("sync", "VE")
                s.dma("sync", lambda e, mt=mt: e.dma_start(
                    out=xa[:], in_=fw_in[mt * 128:(mt + 1) * 128, :]), "LD")
                s.wait("vector", "LD")
                s.wait("scalar", "LD")
                s.wait("vector", "TE")
                emit_quant(xa, xdt, wscale=True)
                s.wait("tensor", "VE")
                s.wait("tensor", "AC")
                for kt in range(KD):
                    s.op("tensor", lambda e, kt=kt: e.transpose(
                        out=ps_t[:, kt * 128:(kt + 1) * 128],
                        in_=xdt[:, kt * 128:(kt + 1) * 128], identity=ident[:]))
                s.wait("scalar", "TE")
                for kt in range(KD):
                    s.op("scalar", lambda e, kt=kt, mt=mt: e.activation(
                        out=wd_s[:, kt * DMODEL + mt * 128:
                                 kt * DMODEL + mt * 128 + 128],
                        in_=ps_t[:, kt * 128:(kt + 1) * 128],
                        func=AF.Copy, bias=0.0))

        emit_prep()
        for sc in range(NSC):
            for ch in range(NCH):
                emit_A_s1(sc, ch)
                if ch > 0:
                    emit_A_s3(sc, ch - 1)
                    emit_A_ln2(sc, ch - 1)
                    emit_phaseB1_chunk(sc, ch - 1)
                emit_ffn_s2(ch % 2)
            emit_A_s3(sc, NCH - 1)
            emit_A_ln2(sc, NCH - 1)
            emit_phaseB1_chunk(sc, NCH - 1)
            emit_AR(sc)
            emit_phaseB2_scroll(sc)
            emit_RS(sc)
        prevc = None
        for sc in range(NSC):
            for ch in range(2):
                emit_C_s1(sc, ch)
                if prevc is not None:
                    emit_C_s3(*prevc)
                emit_ffn_s2((sc * 2 + ch) % 2)
                prevc = (sc, ch)
        emit_C_s3(*prevc)
        s.wait("gpsimd", "ODMA")

        @block.sync
        def _(e):
            s.replay("sync", e, sems)

        @block.vector
        def _(e):
            s.replay("vector", e, sems)

        @block.scalar
        def _(e):
            s.replay("scalar", e, sems)

        @block.tensor
        def _(e):
            s.replay("tensor", e, sems)

        @block.gpsimd
        def _(e):
            s.replay("gpsimd", e, sems)

    return nc


def kernel(L0_attn_x, L0_attn_W_in, L0_attn_conv_w, L0_attn_conv_b, L0_attn_W_x,
           L0_attn_W_dt, L0_attn_b_dt, L0_attn_A_log, L0_attn_D, L0_attn_W_out,
           L0_ffn_weight, L0_ffn_bias):
    x = np.ascontiguousarray(np.asarray(L0_attn_x, dtype=np.float32))
    W_in = np.ascontiguousarray(np.asarray(L0_attn_W_in, dtype=np.float32))
    conv_w = np.ascontiguousarray(np.asarray(L0_attn_conv_w, dtype=np.float32))
    conv_b = np.ascontiguousarray(np.asarray(L0_attn_conv_b, dtype=np.float32))
    W_x = np.ascontiguousarray(np.asarray(L0_attn_W_x, dtype=np.float32))
    W_dt = np.ascontiguousarray(np.asarray(L0_attn_W_dt, dtype=np.float32))
    b_dt = np.ascontiguousarray(np.asarray(L0_attn_b_dt, dtype=np.float32))
    A_log = np.ascontiguousarray(np.asarray(L0_attn_A_log, dtype=np.float32))
    Dv = np.ascontiguousarray(np.asarray(L0_attn_D, dtype=np.float32))
    W_out = np.ascontiguousarray(np.asarray(L0_attn_W_out, dtype=np.float32))
    fw = np.ascontiguousarray(np.asarray(L0_ffn_weight, dtype=np.float32))
    fb = np.ascontiguousarray(np.asarray(L0_ffn_bias, dtype=np.float32))
    ident = np.eye(128, dtype=ml_dtypes.bfloat16)

    in_maps = []
    for r in range(8):
        b, h = r >> 1, r & 1
        sl = slice(h * DH, (h + 1) * DH)
        in_maps.append({
            "x_b": x[b],
            "win_sl": np.ascontiguousarray(
                np.concatenate([W_in[:, h * DH:(h + 1) * DH],
                                W_in[:, DIN + h * DH:DIN + (h + 1) * DH]],
                               axis=1)),
            "convw_sl": np.ascontiguousarray(conv_w[sl].T),
            "convb_sl": np.ascontiguousarray(conv_b[sl, None]),
            "wx_sl": np.ascontiguousarray(W_x[sl]),
            "wdt_sl": np.ascontiguousarray(W_dt[:, sl]),
            "bdt_sl": np.ascontiguousarray(b_dt[sl, None]),
            "alog_sl": np.ascontiguousarray(A_log[sl]),
            "dvec_sl": np.ascontiguousarray(Dv[sl, None]),
            "wout_sl": np.ascontiguousarray(W_out[sl]),
            "ffnw": fw,
            "ffnb": fb[:, None],
            "ident": ident,
        })

    nc = build_program()
    res = run_bass_kernel_spmd(nc, in_maps, core_ids=list(range(8)))

    out = np.empty((4, T, DMODEL), dtype=np.float32)
    for r in range(8):
        b, h = r >> 1, r & 1
        o = res.results[r]["out_r"]
        for k in range(NSC):
            out[b, k * TSC + h * 1024:k * TSC + (h + 1) * 1024, :] = \
                o[k * 1024:(k + 1) * 1024, :]
    return out
